# version 24
# baseline (speedup 1.0000x reference)
import numpy as np
import ml_dtypes

import concourse.bass as bass
import concourse.mybir as mybir
from concourse.bass_utils import run_bass_kernel_spmd

F32 = mybir.dt.float32
BF16 = mybir.dt.bfloat16
I32 = mybir.dt.int32
AF = mybir.ActivationFunctionType
ALU = mybir.AluOpType
AX = mybir.AxisListType

NCORES = 8
T = 2048
E = 1024
H = 16
L = 64
HF = 4096
TB = 256
NB = T // TB
NJ = HF // 128
EPS = 1e-5
RSQRT_MAGIC = 0x5F3759DF

ENGS = ("sy", "gp", "pe", "dv", "sc")


class Plan:

    def __init__(self):
        self.items = {e: [] for e in ENGS}
        self.c = {}

    def val(self, sem):
        return self.c.get(sem, 0)

    def add(self, eng, fn, waits=(), inc=None):
        tgt = None
        if inc is not None:
            s, a = inc
            self.c[s] = self.c.get(s, 0) + a
            tgt = self.c[s]
        self.items[eng].append((tuple(waits), fn, inc))
        return tgt


def _stripes(jt):
    out = []
    p = jt * 128
    while p < T:
        e = min((p // 512 + 1) * 512, T)
        out.append((p, e))
        p = e
    return out


def build(debug=False, phase=8, bstage=7):
    nc = bass.Bass("TRN2", target_bir_lowering=False)

    d_x = nc.dram_tensor("x_fm", [E, T], F32, kind="ExternalInput").ap()
    d_wq = nc.dram_tensor("wq", [E, E], F32, kind="ExternalInput").ap()
    d_q2l = nc.dram_tensor("q2l", [128, 512], F32, kind="ExternalInput").ap()
    d_wkv = nc.dram_tensor("wkvT", [E, 128], F32, kind="ExternalInput").ap()
    d_wout = nc.dram_tensor("woutT", [E, E], F32, kind="ExternalInput").ap()
    d_vup = nc.dram_tensor("vupT", [128, 512], F32, kind="ExternalInput").ap()
    d_w1t = nc.dram_tensor("w1T", [E, HF], F32, kind="ExternalInput").ap()
    d_w2t = nc.dram_tensor("w2T", [HF, E], F32, kind="ExternalInput").ap()
    d_w1n = nc.dram_tensor("w1n", [HF, E], F32, kind="ExternalInput").ap()
    d_w2n = nc.dram_tensor("w2n", [E, HF], F32, kind="ExternalInput").ap()
    d_b1 = nc.dram_tensor("b1c", [128, 32], F32, kind="ExternalInput").ap()
    d_b2 = nc.dram_tensor("b2c", [128, 8], F32, kind="ExternalInput").ap()
    d_mask = nc.dram_tensor("mask01", [128, 128], BF16, kind="ExternalInput").ap()
    d_ident = nc.dram_tensor("ident", [128, 128], BF16, kind="ExternalInput").ap()
    d_onesb = nc.dram_tensor("onesb", [128, 1], BF16, kind="ExternalInput").ap()
    d_onesf = nc.dram_tensor("onesf", [1, 128], F32, kind="ExternalInput").ap()

    import os as _os
    spy = bool(_os.environ.get("KSPY"))
    d_y = nc.dram_tensor("y_fm", [E, T], F32, kind="ExternalOutput").ap()
    scratch_kind = "ExternalOutput" if (debug or spy) else "Internal"
    d_ctxs = nc.dram_tensor("ctxs", [E, T], BF16, kind=scratch_kind).ap()
    d_s2d = nc.dram_tensor("s2d", [HF, E], BF16, kind=scratch_kind).ap()
    d_rs_spy = nc.dram_tensor("rs_spy", [H, 2, 2, 64, 1024], F32, kind=scratch_kind).ap() if spy else None
    d_krep_spy = nc.dram_tensor("krep_spy", [128, T], BF16, kind=scratch_kind).ap() if spy else None
    d_qlat_spy = nc.dram_tensor("qlat_spy", [128, 8, T], BF16, kind=scratch_kind).ap() if spy else None
    d_v1_spy = nc.dram_tensor("v1_spy", [128, 16 * 128], BF16, kind=scratch_kind).ap() if spy else None
    if debug:
        d_tqlat = nc.dram_tensor("t_qlat", [128, 8, T], BF16, kind="ExternalOutput").ap()
        d_tkrep = nc.dram_tensor("t_krep", [128, T], BF16, kind="ExternalOutput").ap()
        d_tv1 = nc.dram_tensor("t_v1", [128, 16 * 128], BF16, kind="ExternalOutput").ap()
        d_ty1 = nc.dram_tensor("t_y1", [128, NB, 8, TB], F32, kind="ExternalOutput").ap()
        d_tffn = nc.dram_tensor("t_ffn", [128, NB, 8, TB], F32, kind="ExternalOutput").ap()
        d_ta1 = nc.dram_tensor("t_a1", [128, 32], F32, kind="ExternalOutput").ap()

    from contextlib import ExitStack
    with ExitStack() as ctx:
        s1T_h = ctx.enter_context(nc.sbuf_tensor("s1T", [128, 8, HF], BF16))
        s2b_h = ctx.enter_context(nc.sbuf_tensor("s2b", [128, 2, 2, E], BF16))
        vcomb_h = ctx.enter_context(nc.sbuf_tensor("vcomb", [128, 8, E], BF16))
        ar1_h = ctx.enter_context(nc.sbuf_tensor("arena1", [128, 32768], mybir.dt.uint8))
        ar2_h = ctx.enter_context(nc.sbuf_tensor("arena2", [128, 32768], mybir.dt.uint8))
        ar4_h = ctx.enter_context(nc.sbuf_tensor("arena4", [128, 16384], mybir.dt.uint8))
        v1_h = ctx.enter_context(nc.sbuf_tensor("v1", [128, 16, 128], BF16))
        krep_h = ctx.enter_context(nc.sbuf_tensor("krep", [128, T], BF16))
        q2lb_h = ctx.enter_context(nc.sbuf_tensor("q2lb", [128, 512], BF16))
        vupb_h = ctx.enter_context(nc.sbuf_tensor("vupb", [128, 512], BF16))
        wkvb_h = ctx.enter_context(nc.sbuf_tensor("wkvb", [128, 8, 128], BF16))

        rs_h = ctx.enter_context(nc.sbuf_tensor("rs", [128, 1024], F32))
        rs2_h = ctx.enter_context(nc.sbuf_tensor("rs2", [128, 1024], F32))
        ctxn_h = ctx.enter_context(nc.sbuf_tensor("ctxn", [64, T], BF16))
        a1c_h = ctx.enter_context(nc.sbuf_tensor("a1c", [128, 32], F32))
        a2c_h = ctx.enter_context(nc.sbuf_tensor("a2c", [128, 8], F32))
        b1c_h = ctx.enter_context(nc.sbuf_tensor("b1c_s", [128, 32], F32))
        b2c_h = ctx.enter_context(nc.sbuf_tensor("b2c_s", [128, 8], F32))
        mask_h = ctx.enter_context(nc.sbuf_tensor("mask01_s", [128, 128], BF16))
        ident_h = ctx.enter_context(nc.sbuf_tensor("ident_s", [128, 128], BF16))
        onesb_h = ctx.enter_context(nc.sbuf_tensor("onesb_s", [128, 1], BF16))
        onesf_h = ctx.enter_context(nc.sbuf_tensor("onesf_s", [1, 128], F32))
        hb_h = ctx.enter_context(nc.sbuf_tensor("hb", [128, 4, 256], BF16))
        psA_h = ctx.enter_context(nc.psum_tensor("psA", [128, 2048], F32))
        psS4_h = ctx.enter_context(nc.psum_tensor("psS4", [128, 2048], F32))
        S_dW = ctx.enter_context(nc.semaphore("dW"))
        S_qA = ctx.enter_context(nc.semaphore("qA"))
        S_qX = ctx.enter_context(nc.semaphore("qX"))
        S_qB = ctx.enter_context(nc.semaphore("qB"))
        S_qC = ctx.enter_context(nc.semaphore("qC"))
        S_qD = ctx.enter_context(nc.semaphore("qD"))
        S_qE0 = ctx.enter_context(nc.semaphore("qE0"))
        S_qE1 = ctx.enter_context(nc.semaphore("qE1"))
        S_qF0 = ctx.enter_context(nc.semaphore("qF0"))
        S_qF1 = ctx.enter_context(nc.semaphore("qF1"))
        S_qG0 = ctx.enter_context(nc.semaphore("qG0"))
        S_qG1 = ctx.enter_context(nc.semaphore("qG1"))
        S_qH0 = ctx.enter_context(nc.semaphore("qH0"))
        S_qH1 = ctx.enter_context(nc.semaphore("qH1"))
        S_dS2b = ctx.enter_context(nc.semaphore("dS2b"))
        S_dI = ctx.enter_context(nc.semaphore("dI"))
        S_dCW = ctx.enter_context(nc.semaphore("dCW"))
        S_dCW1 = ctx.enter_context(nc.semaphore("dCW1"))
        S_dCR = ctx.enter_context(nc.semaphore("dCR"))
        S_dXR = ctx.enter_context(nc.semaphore("dXR"))
        S_dS2 = ctx.enter_context(nc.semaphore("dS2"))
        S_dSP = ctx.enter_context(nc.semaphore("dSP"))
        S_dO = ctx.enter_context(nc.semaphore("dO"))
        S_pe = ctx.enter_context(nc.semaphore("pe"))
        S_sc = ctx.enter_context(nc.semaphore("sc"))
        S_dv = ctx.enter_context(nc.semaphore("dv"))
        S_gp = ctx.enter_context(nc.semaphore("gp"))
        block = ctx.enter_context(nc.Block())
        SEM = dict(dW=S_dW, dI=S_dI, dCW=S_dCW, dCW1=S_dCW1, dCR=S_dCR, dXR=S_dXR, dS2=S_dS2,
                   dS2b=S_dS2b, dSP=S_dSP, dO=S_dO, pe=S_pe, sc=S_sc, dv=S_dv, gp=S_gp,
                   qA=S_qA, qB=S_qB, qC=S_qC, qD=S_qD, qE0=S_qE0, qE1=S_qE1,
                   qF0=S_qF0, qF1=S_qF1, qG0=S_qG0, qG1=S_qG1, qH0=S_qH0, qH1=S_qH1,
                   qX=S_qX)

        s1T = s1T_h.ap()
        s1flat = s1T.rearrange("p a b -> p (a b)")
        wqv = s1flat[:, 0:8192].rearrange("p (c e) -> p c e", c=8)
        woutv = s1flat[:, 8192:16384].rearrange("p (c e) -> p c e", c=8)
        s2b = s2b_h.ap()
        s2stage = s2b.rearrange("p a b e -> p (a b e)")
        vcomb = vcomb_h.ap()

        ar1 = ar1_h.ap()
        qlat = ar1[:, :].bitcast(BF16).rearrange("p (c t) -> p c t", c=8)
        xin = [ar1[:, 0:8192].bitcast(F32).rearrange("p (c t) -> p c t", c=8),
               ar1[:, 8192:16384].bitcast(F32).rearrange("p (c t) -> p c t", c=8)]
        y1f = ar1[:, 16384:24576].bitcast(F32).rearrange("p (c t) -> p c t", c=8)
        res2f = ar1[:, 24576:32768].bitcast(F32).rearrange("p (c t) -> p c t", c=8)

        ar2 = ar2_h.ap()
        xb = ar2[:, :].bitcast(BF16).rearrange("p (c t) -> p c t", c=8)
        wstage = [ar2[:, 0:8192].bitcast(BF16),
                  ar2[:, 8192:16384].bitcast(BF16)]
        wnstage = [ar2[:, 16384:24576].bitcast(BF16).rearrange("p (r e) -> p r e", r=4),
                   ar2[:, 24576:32768].bitcast(BF16).rearrange("p (r e) -> p r e", r=4)]
        vfm = ar2[0:64, 0:4096].bitcast(BF16)
        y2 = ar2[:, 0:8192].bitcast(F32).rearrange("p (c t) -> p c t", c=8)
        ffnout = ar2[:, 8192:16384].bitcast(F32).rearrange("p (c t) -> p c t", c=8)
        strips = ar2[0:1, 16384:28672].bitcast(F32)
        S_MEAN = (0, 256)
        S_EX2 = (256, 512)
        S_VAR = (512, 768)
        S_Y = (768, 1024)
        S_T = (1024, 1280)
        S_NMR = (1280, 1536)

        def strip(rg):
            return strips[:, rg[0]:rg[1]]

        def strip_i(rg):
            return strips[:, rg[0]:rg[1]].bitcast(I32)

        ar4 = ar4_h.ap()
        wq2l = ar4[:, :].bitcast(BF16).rearrange("p (c e) -> p c e", c=8)
        cin = ar4[:, 0:4096].bitcast(BF16).rearrange("p (c t) -> p c t", c=8)
        xresb = ar4[:, 4096:8192].bitcast(BF16).rearrange("p (c t) -> p c t", c=8)
        xsq = ar4[:, 8192:12288].bitcast(BF16).rearrange("p (c t) -> p c t", c=8)
        y1b = [ar4[:, 12288:16384].bitcast(BF16).rearrange("p (c t) -> p c t", c=8),
               ar2[:, 28672:32768].bitcast(BF16).rearrange("p (c t) -> p c t", c=8)]

        v1 = v1_h.ap()
        krep = krep_h.ap()
        q2lb = q2lb_h.ap()
        vupb = vupb_h.ap()
        wkvb = wkvb_h.ap()
        ptS_h = ctx.enter_context(nc.sbuf_tensor("ptS", [128, 4096], BF16))
        ptS = ptS_h.ap()
        y1fb = [y1f, ptS_h.ap().bitcast(F32).rearrange("p (c t) -> p c t", c=8)]
        rs = rs_h.ap()
        rs2 = rs2_h.ap()
        ctxn = ctxn_h.ap()
        a1c = a1c_h.ap()
        a2c = a2c_h.ap()
        b1c = b1c_h.ap()
        b2c = b2c_h.ap()
        mask01 = mask_h.ap()
        ident = ident_h.ap()
        onesb = onesb_h.ap()
        onesf = onesf_h.ap()
        hb = hb_h.ap()

        psA = psA_h.ap()
        psS4 = psS4_h.ap()
        psB = psS4[:, 0:512]
        psC = psS4[:, 512:1024]
        psD = psS4[:, 1024:1536]
        psE = psS4[:, 1536:2048]
        psB_bf = psB.bitcast(BF16)
        psC_bf = psC.bitcast(BF16)

        P = Plan()

        for dst, src in ((mask01, d_mask), (ident, d_ident), (onesb, d_onesb),
                         (onesf, d_onesf), (b1c, d_b1), (b2c, d_b2)):
            P.add("sy", lambda e, a=dst, b=src: e.dma_start(a, b), inc=("dI", 16))
        V_CONST = P.val("dI")

        def gload(dst, src, sem, waits=()):
            return P.add("gp", lambda e, a=dst, b=src: e.dma_start(a, b),
                         waits=waits, inc=(sem, 16))

        s2stg = s2b.rearrange("p a b e -> p (a b e)")
        xslots = [s2stg[:, 0:2048].bitcast(F32), s2stg[:, 2048:4096].bitcast(F32)]
        v_xcast = [None] * 16
        for c in range(8):
            for th in range(2):
                i = 2 * c + th
                xsem = "qA" if i % 2 == 0 else "qX"
                dw = [("sc", v_xcast[i - 2])] if i >= 2 else []
                P.add("sy",
                      lambda e, c=c, th=th, i=i: e.dma_start(
                          xslots[i % 2],
                          d_x.rearrange("(c p) t -> p c t", p=128)[:, c, th * 1024:th * 1024 + 1024]),
                      waits=dw, inc=(xsem, 16))
                v_xcast[i] = P.add("sc",
                                   lambda e, c=c, th=th, i=i: e.copy(
                                       xb[:, c, th * 1024:th * 1024 + 1024], xslots[i % 2]),
                                   waits=[(xsem, 16 * (i // 2 + 1))], inc=("sc", 1))
        V_XBC = v_xcast[15]
        V_WQ = gload(wqv, d_wq.rearrange("(c p) t -> p c t", p=128), "qB")
        V_Q2L = gload(q2lb, d_q2l, "qB")
        V_WKV = gload(wkvb, d_wkv.rearrange("(c p) t -> p c t", p=128), "qC")
        V_WOUT = gload(woutv, d_wout.rearrange("(c p) t -> p c t", p=128), "qD")
        V_VUP = gload(vupb, d_vup, "qD")

        if debug and phase >= 1:
            dbg_xb = nc.dram_tensor("t_xb", [128, 8, T], BF16, kind="ExternalOutput").ap()
            P.add("sy", lambda e: e.dma_start(dbg_xb, xb), waits=[("sc", V_XBC)], inc=("dO", 16))
        v_evac_wq2l = [0] * 8
        for ec in (range(8) if phase >= 2 else ()):
            waits = [("qB", 32)]
            if ec >= 2:
                waits.append(("sc", v_evac_wq2l[ec - 2]))
            for h in range(H):
                po = (h % 2) * 64
                col = (ec % 2) * 1024 + (h % 2) * 512 + (h // 2) * 64
                def mm(e, ec=ec, h=h, po=po, col=col):
                    return e.matmul(
                        psA[:, col:col + 64],
                        wqv[po:po + 64, h // 2, ec * 128:(ec + 1) * 128],
                        q2lb[po:po + 64, (h // 2) * 64:(h // 2) * 64 + 64],
                        start=True, stop=True)
                P.add("pe", mm, waits=waits if h == 0 else (), inc=("pe", 1) if h == H - 1 else None)
            vp = P.val("pe")
            v_evac_wq2l[ec] = P.add(
                "sc",
                lambda e, ec=ec: e.activation(
                    wq2l[:, ec, :].rearrange("p (c a l) -> p c a l", a=2, c=8, l=64),
                    psA[:, (ec % 2) * 1024:(ec % 2) * 1024 + 1024].rearrange(
                        "p (a c l) -> p c a l", a=2, c=8, l=64),
                    AF.Copy, scale=0.125),
                waits=[("pe", vp)], inc=("sc", 1))

        if debug and phase >= 2:
            dbg_wq2l = nc.dram_tensor("t_wq2l", [128, 8, E], BF16, kind="ExternalOutput").ap()
            P.add("sy", lambda e: e.dma_start(dbg_wq2l, wq2l), waits=[("sc", v_evac_wq2l[7])], inc=("dO", 16))
        v_evac_vcomb = [0] * 8
        for c in (range(8) if phase >= 3 else ()):
            waits = [("qD", 32), ("sc", v_evac_wq2l[7])]
            if c >= 2:
                waits = [("sc", v_evac_vcomb[c - 2])]
            first = True
            for h, po in ((2 * c, 0), (2 * c + 1, 64)):
                for nb2 in range(2):
                    def mm(e, c=c, h=h, po=po, nb2=nb2):
                        return e.matmul(
                            psA[po:po + 64, (c % 2) * 1024 + nb2 * 512:(c % 2) * 1024 + nb2 * 512 + 512],
                            vupb[po:po + 64, (h // 2) * 64:(h // 2) * 64 + 64],
                            woutv[po:po + 64, h // 2, nb2 * 512:nb2 * 512 + 512],
                            start=True, stop=True,
                            tile_position=(po, po))
                    last = (h % 2 == 1 and nb2 == 1)
                    P.add("pe", mm, waits=waits if first else (), inc=("pe", 1) if last else None)
                    first = False
            vp = P.val("pe")
            v_evac_vcomb[c] = P.add(
                "sc",
                lambda e, c=c: e.copy(vcomb[:, c, :], psA[:, (c % 2) * 1024:(c % 2) * 1024 + 1024]),
                waits=[("pe", vp)], inc=("sc", 1))

        if debug and phase >= 3:
            dbg_vc = nc.dram_tensor("t_vcomb", [128, 8, E], BF16, kind="ExternalOutput").ap()
            P.add("sy", lambda e: e.dma_start(dbg_vc, vcomb), waits=[("sc", v_evac_vcomb[7])], inc=("dO", 16))
        v_qlat_evac = []
        unit = 0
        for mc in (range(8) if phase >= 4 else ()):
            for th in range(2):
                half = unit % 2
                waits = [("sc", V_XBC), ("sc", v_evac_vcomb[7])]
                if len(v_qlat_evac) >= 2:
                    eng, val = v_qlat_evac[unit - 2]
                    waits.append((eng, val))
                for tc2 in range(2):
                    for kc in range(8):
                        def mm(e, mc=mc, th=th, half=half, tc2=tc2, kc=kc):
                            return e.matmul(
                                psA[:, half * 1024 + tc2 * 512: half * 1024 + tc2 * 512 + 512],
                                wq2l[:, kc, mc * 128:(mc + 1) * 128],
                                xb[:, kc, th * 1024 + tc2 * 512: th * 1024 + tc2 * 512 + 512],
                                start=(kc == 0), stop=(kc == 7))
                        P.add("pe", mm,
                              waits=waits if (tc2 == 0 and kc == 0) else (),
                              inc=("pe", 1) if (tc2 == 1 and kc == 7) else None)
                vp = P.val("pe")
                ev_eng = "dv"
                def ev(e, mc=mc, th=th, half=half, ev_eng=ev_eng):
                    if ev_eng == "dv":
                        return e.tensor_copy(qlat[:, mc, th * 1024: th * 1024 + 1024],
                                             psA[:, half * 1024: half * 1024 + 1024])
                    return e.copy(qlat[:, mc, th * 1024: th * 1024 + 1024],
                                  psA[:, half * 1024: half * 1024 + 1024])
                val = P.add(ev_eng, ev, waits=[("pe", vp)], inc=(ev_eng, 1))
                v_qlat_evac.append((ev_eng, val))
                unit += 1

        kv_waits = [("qC", 16)] + [(e, v) for e, v in v_qlat_evac[-2:]]
        for tc in (range(4) if phase >= 5 else ()):
            for kc in range(8):
                def mm(e, tc=tc, kc=kc):
                    return e.matmul(psA[:, tc * 512:tc * 512 + 512],
                                    wkvb[:, kc, :],
                                    xb[:, kc, tc * 512:tc * 512 + 512],
                                    start=(kc == 0), stop=(kc == 7))
                P.add("pe", mm, waits=kv_waits if (tc == 0 and kc == 0) else (),
                      inc=("pe", 1) if (tc == 3 and kc == 7) else None)
        V_KV = P.val("pe")

        if phase < 5:
            V_KV = None
        V_K0 = (P.add("sc", lambda e: e.copy(krep[0:64, :], psA[0:64, :]),
                     waits=[("pe", V_KV)], inc=("sc", 1)) if phase >= 5 else None)
        V_K1 = (P.add("sc", lambda e: e.copy(krep[64:128, :], psA[0:64, :]), inc=("sc", 1)) if phase >= 5 else None)
        V_VF = (P.add("sc", lambda e: e.copy(vfm, psA[64:128, :]), inc=("sc", 1)) if phase >= 5 else None)

        V_ONES = (P.add("dv", lambda e: e.memset(v1[:, :, 64:128], 1.0), inc=("dv", 1)) if phase >= 5 else None)

        v_tr = [0] * 16
        for jt in (range(16) if phase >= 5 else ()):
            waits = [("sc", V_VF)]
            if jt == 0:
                waits.append(("dI", V_CONST))
            if jt >= 2:
                waits.append(("dv", v_tr[jt - 2]))
            def tr(e, jt=jt):
                dst = psB_bf if jt % 2 == 0 else psC_bf
                return e.transpose(dst[:, 0:64],
                                   vfm[:, jt * 128:(jt + 1) * 128],
                                   ident[0:64, 0:64])
            P.add("pe", tr, waits=waits, inc=("pe", 1))
            vp = P.val("pe")
            def cp(e, jt=jt):
                srcb = psB_bf if jt % 2 == 0 else psC_bf
                return e.tensor_copy(v1[:, jt, 0:64], srcb[:, 0:64])
            v_tr[jt] = P.add("dv", cp, waits=[("pe", vp)], inc=("dv", 1))
        V_V1 = v_tr[15] if phase >= 5 else None
        v_tr_last = V_V1

        if debug and phase >= 4:
            P.add("sy", lambda e: e.dma_start(d_tqlat, qlat),
                  waits=[(e, v) for e, v in v_qlat_evac[-2:]], inc=("dO", 16))
        if debug and phase >= 5:
            P.add("sy", lambda e: e.dma_start(d_tkrep, krep), waits=[("sc", V_K1)], inc=("dO", 16))
            P.add("sy", lambda e: e.dma_start(d_tv1, v1.rearrange("p a b -> p (a b)")),
                  waits=[("dv", V_V1), ("dv", V_ONES)], inc=("dO", 16))

        PR = {}
        v_a2red = [0] * 8
        v_s2sign = [0] * 8
        v_a1red = [0] * 32
        v_s1sign = [0] * 8

        def prep_gen():
            for c in range(8):
                waits = [("pe", V_KV)]
                if c >= 2:
                    waits.append(("dv", v_a2red[c - 2]))
                vdma = P.add("gp",
                             lambda e, c=c: e.dma_start(
                                 wnstage[c % 2].rearrange("p r e -> p (r e)"),
                                 d_w2n[c * 128:(c + 1) * 128, :]),
                             waits=waits, inc=(f"qE{c % 2}", 16))
                yield
                v_a2red[c] = P.add("dv",
                                   lambda e, c=c: e.reduce_sum(
                                       a2c[:, c:c + 1],
                                       wnstage[c % 2].rearrange("p r e -> p (r e)"),
                                       axis=AX.X, apply_absolute_value=True),
                                   waits=[(f"qE{c % 2}", vdma)], inc=("dv", 1))
            PR["V_A2"] = P.add("dv",
                               lambda e: e.tensor_scalar(a2c, a2c, 1.0 / HF, None, op0=ALU.mult),
                               waits=[("dv", v_a2red[7])], inc=("dv", 1))
            for g in range(8):
                waits = []
                if g == 0:
                    waits.append(("dv", v_tr_last))
                if g >= 2:
                    waits.append(("dv", v_s2sign[g - 2]))
                vdma = P.add("gp",
                             lambda e, g=g: e.dma_start(
                                 wstage[g % 2].rearrange("p (r e) -> p r e", r=4),
                                 d_w2t[g * 512:(g + 1) * 512, :].rearrange("(r p) e -> p r e", p=128)),
                             waits=waits, inc=(f"qF{g % 2}", 16))
                yield
                w = [(f"qF{g % 2}", vdma)]
                if g >= 1:
                    w.append(("dSP", 16 * g))
                v_s2sign[g] = P.add("dv",
                      lambda e, g=g: e.tensor_scalar(
                          s2stage.bitcast(mybir.dt.int16), wstage[g % 2].bitcast(mybir.dt.int16),
                          0x8000 - 0x10000, 0x3F80, op0=ALU.bitwise_and, op1=ALU.bitwise_or),
                      waits=w, inc=("dv", 1))
                P.add("gp",
                      lambda e, g=g: e.dma_start(
                          d_s2d[g * 512:(g + 1) * 512, :].rearrange("(r p) e -> p r e", p=128),
                          s2stage.rearrange("p (r e) -> p r e", r=4)),
                      waits=[("dv", v_s2sign[g])], inc=("dSP", 16))
            for g in range(8):
                waits = []
                if g < 2:
                    waits.append(("dv", v_a2red[6 + g]))
                if g >= 2:
                    waits.append(("dv", v_a1red[4 * (g - 2) + 3]))
                vdma = P.add("gp",
                             lambda e, g=g: e.dma_start(
                                 wnstage[g % 2],
                                 d_w1n[g * 512:(g + 1) * 512, :].rearrange("(r p) e -> p r e", p=128)),
                             waits=waits, inc=(f"qG{g % 2}", 16))
                yield
                for r in range(4):
                    v_a1red[4 * g + r] = P.add(
                        "dv",
                        lambda e, g=g, r=r: e.reduce_sum(a1c[:, 4 * g + r:4 * g + r + 1],
                                                         wnstage[g % 2][:, r, :],
                                                         axis=AX.X, apply_absolute_value=True),
                        waits=[(f"qG{g % 2}", vdma)] if r == 0 else (), inc=("dv", 1))
            PR["V_A1"] = P.add("dv",
                               lambda e: e.tensor_scalar(a1c, a1c, 1.0 / E, None, op0=ALU.mult),
                               waits=[("dv", v_a1red[31])], inc=("dv", 1))
            for c in range(8):
                waits = []
                if c < 2:
                    waits.append(("dv", v_s2sign[6 + c]))
                if c >= 2:
                    waits.append(("dv", v_s1sign[c - 2]))
                vdma = P.add("gp",
                             lambda e, c=c: e.dma_start(wstage[c % 2], d_w1t[c * 128:(c + 1) * 128, :]),
                             waits=waits, inc=(f"qH{c % 2}", 16))
                yield
                v_s1sign[c] = P.add("dv",
                                    lambda e, c=c: e.tensor_scalar(
                                        s1T[:, c, :].bitcast(mybir.dt.int16), wstage[c % 2].bitcast(mybir.dt.int16),
                                        0x8000 - 0x10000, 0x3F80,
                                        op0=ALU.bitwise_and, op1=ALU.bitwise_or),
                                    waits=[(f"qH{c % 2}", vdma)], inc=("dv", 1))
            if debug:
                P.add("sy", lambda e: e.dma_start(d_ta1, a1c), waits=[("dv", PR["V_A1"])], inc=("dO", 16))
            while True:
                yield

        if phase >= 6:
            _pg = prep_gen()

            def emit_prep_slice(n):
                for _ in range(n):
                    next(_pg)
        else:
            def emit_prep_slice(n):
                pass

        if phase == 6:
            emit_prep_slice(64)
            V_SPILLS = 16 * 8
        if phase < 7:
            v_head_mult = None
        if phase >= 7:
            sidx = 0
            stripe_hist = []
            v_head_mult = [0] * H
            prev_srecip_mult = {0: None, 1024: None}
            prev_rs_read = {0: None, 1024: None}
            v_head_ln = [0] * H
            v_spill = [0] * H

            def half_stripes(jt, half):
                lo, hi = half * 1024, half * 1024 + 1024
                out = []
                p = max(jt * 128, lo)
                while p < hi:
                    e = min((p // 512 + 1) * 512, hi)
                    out.append((p, e))
                    p = e
                return out

            mask_gated = [False]
            pending_epi = [None]
            spill_idx = {}
            n_spilled = 0
            prev_half_guard = None

            for hp in range(8):
                hA, hB = 2 * hp, 2 * hp + 1
                for half in range(2):
                    pending_pv = []
                    LAG = 3
                    first_pv = True
                    jts = range(8) if half == 0 else range(16)
                    for jt in jts:
                        for (qs, qe) in half_stripes(jt, half):
                            w = qe - qs
                            pu = sidx // 2
                            slotA = (pu % 2) * 2
                            off = qs % 512
                            base = (pu % 4) * 1024
                            pbase = slotA * 512
                            psS_A = (psB, psC, psD, psE)[slotA]
                            psS_B = (psB, psC, psD, psE)[slotA + 1]
                            qkv = []
                            for head, po, psX in ((hA, 0, psS_A), (hB, 64, psS_B)):
                                waits = []
                                if head == hA and pu >= 2:
                                    waits.append(("sc", stripe_hist[pu - 2]["exp"]))
                                if sidx == 0:
                                    waits.append(("sc", V_K1))
                                    waits.append(("dv", V_V1))
                                    waits.append(("dv", V_ONES))
                                def qk(e, po=po, hp=hp, jt=jt, qs=qs, qe=qe, psX=psX, off=off, w=w):
                                    return e.matmul(psX[:, off:off + w],
                                                    krep[po:po + 64, jt * 128:(jt + 1) * 128],
                                                    qlat[po:po + 64, hp, qs:qe],
                                                    start=True, stop=True, skip_group_check=True)
                                qkv.append(P.add("pe", qk, waits=waits, inc=("pe", 1)))
                                sidx += 1
                            ew = [("pe", qkv[1])]
                            if pu >= 4:
                                assert stripe_hist[pu - 4]["pv"] is not None
                                ew.append(("pe", stripe_hist[pu - 4]["pv"]))
                            ew_w = 1024 - off
                            vexp = P.add("sc",
                                         lambda e, base=base, off=off, ew_w=ew_w, pbase=pbase: e.activation(
                                             ptS[:, base + off:base + off + ew_w],
                                             psS4[:, pbase + off:pbase + off + ew_w], AF.Exp),
                                         waits=ew, inc=("sc", 1))
                            if pending_epi[0] is not None:
                                pending_epi[0]()
                                pending_epi[0] = None
                            vmask = None
                            if qs == jt * 128:
                                mw0 = [("sc", vexp)]
                                if not mask_gated[0]:
                                    mw0.append(("dI", V_CONST))
                                    mask_gated[0] = True
                                P.add("gp",
                                      lambda e, base=base, off=off: e.tensor_tensor(
                                          ptS[:, base + off:base + off + 128],
                                          ptS[:, base + off:base + off + 128],
                                          mask01, op=ALU.mult),
                                      waits=mw0)
                                vmask = P.add("gp",
                                              lambda e, base=base, off=off: e.tensor_tensor(
                                                  ptS[:, base + 512 + off:base + 512 + off + 128],
                                                  ptS[:, base + 512 + off:base + 512 + off + 128],
                                                  mask01, op=ALU.mult),
                                              waits=[("sc", vexp)], inc=("gp", 1))
                            pw = [("sc", vexp)]
                            if vmask is not None:
                                pw.append(("gp", vmask))
                            if first_pv and prev_half_guard is not None:
                                pw.extend(prev_half_guard)
                            first_pv = False
                            def pvA(e, jt=jt, qs=qs, w=w, base=base, off=off, half=half):
                                return e.matmul(psA[:, qs - half * 1024:qs - half * 1024 + w],
                                                v1[:, jt, :], ptS[:, base + off:base + off + w],
                                                start=(jt == 0), stop=False, skip_group_check=True)
                            def pvB(e, jt=jt, qs=qs, w=w, base=base, off=off, half=half):
                                return e.matmul(psA[:, 1024 + qs - half * 1024:1024 + qs - half * 1024 + w],
                                                v1[:, jt, :], ptS[:, base + 512 + off:base + 512 + off + w],
                                                start=(jt == 0), stop=False, skip_group_check=True)
                            hist = dict(exp=vexp, pv=None)
                            stripe_hist.append(hist)
                            pending_pv.append((pvA, pw, None))
                            pending_pv.append((pvB, [], hist))
                            while len(pending_pv) > 2 * LAG:
                                assert pending_epi[0] is None
                                fn, fw, fh = pending_pv.pop(0)
                                vv = P.add("pe", fn, waits=fw, inc=("pe", 1))
                                if fh is not None:
                                    fh["pv"] = vv
                        if half == 0 and jt == 3:
                            emit_prep_slice(1)
                        if half == 1 and jt in (3, 9):
                            emit_prep_slice(1)
                    while pending_pv:
                        fn, fw, fh = pending_pv.pop(0)
                        vv = P.add("pe", fn, waits=fw, inc=("pe", 1))
                        if fh is not None:
                            fh["pv"] = vv
                    vpv_last = stripe_hist[-1]["pv"]
                    def run_epilogue(hA=hA, hB=hB, half=half, vpv_last=vpv_last):
                        nonlocal prev_half_guard, n_spilled
                        guard = []
                        vexp_h = {}
                        for head, cb in ((hA, 0), (hB, 1024)):
                            lnw = [("pe", vpv_last)] if cb == 0 else []
                            if prev_rs_read[cb] is not None:
                                lnw.append(("sc", prev_rs_read[cb]))
                            vexp_h[cb] = P.add("sc",
                                  lambda e, cb=cb: e.activation(rs[cb // 16:cb // 16 + 64, :],
                                                                psA[64:128, cb:cb + 1024], AF.Ln),
                                  waits=lnw, inc=("sc", 1))
                        for head, cb in ((hA, 0), (hB, 1024)):
                            ew2 = [("sc", vexp_h[cb])]
                            if prev_srecip_mult[cb] is not None:
                                ew2.append(("dv", prev_srecip_mult[cb]))
                            vexp_h[cb] = prev_rs_read[cb] = P.add("sc",
                                        lambda e, cb=cb: e.activation(rs2[cb // 16:cb // 16 + 64, :],
                                                                      rs[cb // 16:cb // 16 + 64, :],
                                                                      AF.Exp, scale=-1.0),
                                        waits=ew2, inc=("sc", 1))
                        for head, cb in ((hA, 0), (hB, 1024)):
                            vln = vexp_h[cb]
                            mw = [("sc", vln)]
                            if (head, half, "prevspill") in spill_idx:
                                csem = "dCW" if head % 2 == 0 else "dCW1"
                                mw.append((csem, spill_idx[(head, half, "prevspill")]))
                            def mult(e, cb=cb, head=head):
                                return e.tensor_tensor(
                                    ctxn[0:64, (head % 2) * 1024:(head % 2) * 1024 + 1024],
                                    psA[0:64, cb:cb + 1024],
                                    rs2[cb // 16:cb // 16 + 64, :], op=ALU.mult)
                            vm = P.add("dv", mult, waits=mw, inc=("dv", 1))
                            prev_srecip_mult[cb] = vm
                            guard.append(("dv", vm))
                            guard.append(("sc", vln))
                            n_spilled += 1
                            vsp = P.add("sy",
                                        lambda e, head=head, half=half: e.dma_start(
                                            d_ctxs[head * 64:(head + 1) * 64, half * 1024:half * 1024 + 1024],
                                            ctxn[0:64, (head % 2) * 1024:(head % 2) * 1024 + 1024]),
                                        waits=[("dv", vm)],
                                        inc=("dCW" if head % 2 == 0 else "dCW1", 16))
                            nxt = (head, half + 1) if half == 0 else (head + 2, 0)
                            spill_idx[(nxt[0], nxt[1], "prevspill")] = vsp
                        prev_half_guard = guard
                    pending_epi[0] = run_epilogue
                emit_prep_slice(1)

        if phase >= 7:
            if pending_epi[0] is not None:
                pending_epi[0]()
                pending_epi[0] = None
            emit_prep_slice(40)
            V_A1 = PR["V_A1"]
            V_A2 = PR["V_A2"]
            V_S1 = v_s1sign[7]
            V_SPILLS = 16 * 8

        if phase >= 8:
            v_outproj = {}
            v_res1 = {}
            v_epi = [0] * NB
            v_ffn2_grp = {}
            v_sums1 = [0] * NB
            v_sums2 = [0] * NB
            v_strips1 = [0] * NB
            v_strips2 = [0] * NB
            v_bcast1 = [0] * NB
            v_bcast2 = [0] * NB
            v_apply2 = [0] * NB
            v_apply1d3 = [0] * NB
            v_y2out = [0] * NB
            v_vr2b = [0] * NB
            v_vr2 = [0] * NB
            v_vr2_mc = {}
            v_ffn2_last = {}
            v_epi_mc = {}
            v_y1bc = {}
            v_r2sq = [0] * NB
            v_xsq = [0] * NB
            v_cin = [0] * (NB + 1)
            v_xin = [0] * (NB + 1)
            v_s2rd = {}
            v_gelu_hist = {}
            NG = 16

            def strip(bank, rg):
                o = bank * 1536
                return strips[:, o + rg[0]:o + rg[1]]

            def strip_i(bank, rg):
                return strip(bank, rg).bitcast(I32)

            def emit_strips(bank, vcopy_val):
                va = P.add("dv",
                           lambda e: e.tensor_tensor(strip(bank, S_T), strip(bank, S_MEAN),
                                                     strip(bank, S_MEAN), op=ALU.mult),
                           waits=[("sc", vcopy_val)], inc=("dv", 1))
                vb = P.add("dv",
                           lambda e: e.scalar_tensor_tensor(strip(bank, S_VAR), strip(bank, S_EX2),
                                                            EPS, strip(bank, S_T),
                                                            op0=ALU.add, op1=ALU.subtract),
                           waits=[("dv", va)], inc=("dv", 1))
                vc1 = P.add("sc",
                            lambda e: e.activation(strip(bank, S_T), strip(bank, S_VAR),
                                                   AF.Ln),
                            waits=[("dv", vb)], inc=("sc", 1))
                vc2 = P.add("sc",
                            lambda e: e.activation(strip(bank, S_Y), strip(bank, S_T),
                                                   AF.Exp, scale=-0.5),
                            waits=[("sc", vc1)], inc=("sc", 1))
                vd = P.add("dv",
                           lambda e: e.scalar_tensor_tensor(strip(bank, S_NMR), strip(bank, S_MEAN),
                                                            -1.0, strip(bank, S_Y),
                                                            op0=ALU.mult, op1=ALU.mult),
                           waits=[("sc", vc2)], inc=("dv", 1))
                return vd

            def emit_outproj_mc(bb, mc):
                waits = []
                if mc == 0:
                    waits.append(("dCR", v_cin[bb]))
                    waits.append(("sc", v_evac_vcomb[7]))
                    if bb >= 1:
                        waits.append(("dv", v_res1[(bb - 1, 7)]))
                        waits.append(("dv", v_apply2[bb - 2]) if bb >= 2 else ("dv", v_res1[(bb - 1, 7)]))
                else:
                    waits.append(("dv", v_res1[(bb, mc - 1)]))
                for kc in range(8):
                    def mm(e, mc=mc, kc=kc):
                        return e.matmul(psE[:, 0:256],
                                        vcomb[:, kc, mc * 128:(mc + 1) * 128],
                                        cin[:, kc, :],
                                        start=(kc == 0), stop=(kc == 7))
                    P.add("pe", mm, waits=waits if kc == 0 else (),
                          inc=("pe", 1) if kc == 7 else None)
                v_outproj[(bb, mc)] = P.val("pe")
                rw = [("pe", v_outproj[(bb, mc)]), ("dXR", v_xin[bb])]
                if mc == 0 and bb >= 2:
                    rw.append(("gp", v_r2sq[bb - 2]))
                    rw.append(("pe", v_sums2[bb - 2]))
                def r1(e, bb=bb, mc=mc):
                    return e.tensor_tensor(xresb[:, mc, :], psE[:, 0:256],
                                           xin[bb % 2][:, mc, :], op=ALU.add)
                v_res1[(bb, mc)] = P.add("dv", r1, waits=rw, inc=("dv", 1))

            def emit_outproj_tail(bb):
                xw = [("dv", v_res1[(bb, 7)])]
                if bb >= 2:
                    xw.append(("pe", v_sums2[bb - 2]))
                v_xsq[bb] = P.add("gp",
                                  lambda e: e.tensor_tensor(
                                      xsq.rearrange("p c t -> p (c t)"),
                                      xresb.rearrange("p c t -> p (c t)"),
                                      xresb.rearrange("p c t -> p (c t)"), op=ALU.mult),
                                  waits=xw, inc=("gp", 1))

            def emit_outproj(bb):
                for mc in range(8):
                    emit_outproj_mc(bb, mc)
                emit_outproj_tail(bb)

            def emit_sums1(bb):
                sw = [("dv", v_res1[(bb, 7)]), ("dI", V_CONST)]
                if bb >= 2:
                    sw.append(("dv", v_apply2[bb - 2]))
                for mc in range(8):
                    P.add("pe",
                          lambda e, mc=mc: e.matmul(psE[0:1, 0:256], onesb, xresb[:, mc, :],
                                                    start=(mc == 0), stop=(mc == 7)),
                          waits=sw if mc == 0 else ())
                for mc in range(8):
                    P.add("pe",
                          lambda e, mc=mc: e.matmul(psE[0:1, 256:512], onesb, xsq[:, mc, :],
                                                    start=(mc == 0), stop=(mc == 7)),
                          waits=[("gp", v_xsq[bb])] if mc == 0 else (),
                          inc=("pe", 1) if mc == 7 else None)
                v_sums1[bb] = P.val("pe")
                vcopy = P.add("sc",
                              lambda e: e.activation(strip(0, (0, 512)), psE[0:1, 0:512],
                                                     AF.Copy, scale=1.0 / E),
                              waits=[("pe", v_sums1[bb])] + ([("dv", PR["V_A1"])] if bb == 0 else []),
                              inc=("sc", 1))
                v_strips1[bb] = emit_strips(0, vcopy)

            def emit_bcast1(bb):
                P.add("pe", lambda e: e.matmul(psE[:, 0:256], onesf, strip(0, S_Y), start=True, stop=True),
                      waits=[("dv", v_strips1[bb])])
                P.add("pe", lambda e: e.matmul(psE[:, 256:512], onesf, strip(0, S_NMR), start=True, stop=True),
                      inc=("pe", 1))
                v_bcast1[bb] = P.val("pe")

            def emit_apply1(bb):
                yf = y1fb[bb % 2]
                vd1 = {}
                vd2 = {}
                def emit_d1(mc):
                    d1w = [("dv", v_res1[(bb, mc)])]
                    if mc == 0:
                        d1w.append(("pe", v_bcast1[bb]))
                    if bb >= 2:
                        d1w.append(("dv", v_vr2_mc[(bb - 2, mc)]))
                    vd1[mc] = P.add("dv",
                                    lambda e, mc=mc: e.tensor_tensor(yf[:, mc, :], xresb[:, mc, :],
                                                                     psE[:, 0:256], op=ALU.mult),
                                    waits=d1w, inc=("dv", 1))
                def emit_d2(mc):
                    vd2[mc] = P.add("dv",
                                    lambda e, mc=mc: e.tensor_tensor(yf[:, mc, :], yf[:, mc, :],
                                                                     psE[:, 256:512], op=ALU.add),
                                    waits=[("dv", vd1[mc])], inc=("dv", 1))
                    v_y1bc[(bb, mc)] = P.add("sc",
                                             lambda e, mc=mc, bb=bb: e.copy(y1b[bb % 2][:, mc, :],
                                                                            yf[:, mc, :]),
                                             waits=[("dv", vd2[mc])], inc=("sc", 1))
                emit_d1(0)
                for mc in range(1, 8):
                    emit_d1(mc)
                    emit_d2(mc - 1)
                emit_d2(7)
                v_apply1d3[bb] = v_y1bc[(bb, 7)]
                if debug:
                    P.add("sy", lambda e, bb=bb: e.dma_start(d_ty1[:, bb, :, :], y1fb[bb % 2]),
                          waits=[("sc", v_apply1d3[bb])], inc=("dO", 16))

            def emit_res2part1(bb):
                for mc in range(8):
                    def ep(e, mc=mc):
                        return e.activation(ffnout[:, mc, :], psA[:, mc * 256:mc * 256 + 256],
                                            AF.Identity, bias=b2c[:, mc:mc + 1], scale=a2c[:, mc:mc + 1])
                    v_epi_mc[(bb, mc)] = P.add(
                        "sc", ep,
                        waits=[("pe", v_ffn2_last[(bb, mc)])] + ([("dv", PR["V_A2"])] if mc == 0 else []),
                        inc=("sc", 1))
                v_epi[bb] = v_epi_mc[(bb, 7)]
                if debug:
                    P.add("sy", lambda e, bb=bb: e.dma_start(d_tffn[:, bb, :, :], ffnout),
                          waits=[("sc", v_epi[bb])], inc=("dO", 16))
                for mc in range(8):
                    v_vr2_mc[(bb, mc)] = P.add(
                        "dv",
                        lambda e, mc=mc, bb=bb: e.tensor_tensor(res2f[:, mc, :], ffnout[:, mc, :],
                                                                y1fb[bb % 2][:, mc, :], op=ALU.add),
                        waits=[("sc", v_epi_mc[(bb, mc)])], inc=("dv", 1))
                v_vr2[bb] = v_vr2_mc[(bb, 7)]

            def emit_res2part2(bb):
                nxt = bb + 1 if bb + 1 < NB else bb
                v_vr2b[bb] = P.add("dv",
                                   lambda e: e.tensor_copy(xresb.rearrange("p c t -> p (c t)"),
                                                           res2f.rearrange("p c t -> p (c t)")),
                                   waits=[("pe", v_sums1[nxt]), ("gp", v_xsq[nxt]), ("dv", v_vr2[bb])],
                                   inc=("dv", 1))
                v_r2sq[bb] = P.add("gp",
                                   lambda e: e.tensor_tensor(
                                       xsq.rearrange("p c t -> p (c t)"),
                                       xresb.rearrange("p c t -> p (c t)"),
                                       xresb.rearrange("p c t -> p (c t)"), op=ALU.mult),
                                   waits=[("dv", v_vr2b[bb]), ("pe", v_sums1[nxt])], inc=("gp", 1))

            def emit_sums2(bb):
                sw = [("dv", v_vr2b[bb]), ("sc", v_apply1d3[bb + 1] if bb + 1 < NB else v_apply1d3[bb])]
                for mc in range(8):
                    P.add("pe",
                          lambda e, mc=mc: e.matmul(psE[0:1, 0:256], onesb, xresb[:, mc, :],
                                                    start=(mc == 0), stop=(mc == 7)),
                          waits=sw if mc == 0 else ())
                for mc in range(8):
                    P.add("pe",
                          lambda e, mc=mc: e.matmul(psE[0:1, 256:512], onesb, xsq[:, mc, :],
                                                    start=(mc == 0), stop=(mc == 7)),
                          waits=[("gp", v_r2sq[bb])] if mc == 0 else (),
                          inc=("pe", 1) if mc == 7 else None)
                v_sums2[bb] = P.val("pe")
                vcopy = P.add("sc",
                              lambda e: e.activation(strip(1, (0, 512)), psE[0:1, 0:512],
                                                     AF.Copy, scale=1.0 / E),
                              waits=[("pe", v_sums2[bb])], inc=("sc", 1))
                v_strips2[bb] = emit_strips(1, vcopy)

            def emit_bcast2(bb):
                P.add("pe", lambda e: e.matmul(psE[:, 0:256], onesf, strip(1, S_Y), start=True, stop=True),
                      waits=[("dv", v_strips2[bb])])
                P.add("pe", lambda e: e.matmul(psE[:, 256:512], onesf, strip(1, S_NMR), start=True, stop=True),
                      inc=("pe", 1))
                v_bcast2[bb] = P.val("pe")

            def emit_apply2(bb):
                bca = psE[:, 0:256].unsqueeze(1).broadcast_to([128, 8, 256])
                bcb = psE[:, 256:512].unsqueeze(1).broadcast_to([128, 8, 256])
                a2w = [("pe", v_bcast2[bb]), ("dv", v_vr2[bb])]
                if bb >= 1:
                    a2w.append(("dO", v_y2out[bb - 1]))
                va1 = P.add("dv", lambda e, bca=bca: e.tensor_tensor(y2, res2f, bca, op=ALU.mult),
                            waits=a2w, inc=("dv", 1))
                v_apply2[bb] = P.add("dv", lambda e, bcb=bcb: e.tensor_tensor(y2, y2, bcb, op=ALU.add),
                                     waits=[("dv", va1)], inc=("dv", 1))
                v_y2out[bb] = P.add("sy",
                                    lambda e, bb=bb: e.dma_start(
                                        d_y.rearrange("(c p) t -> p c t", p=128)[:, :, bb * TB:bb * TB + TB], y2),
                                    waits=[("dv", v_apply2[bb])], inc=("dO", 16))

            def emit_ffn_steps(bb, j_lo, j_hi):
                for j in range(j_lo, j_hi):
                    if j < NJ:
                        g = j // 2
                        ps = (psB, psC, psD)[j % 3]
                        w1w = []
                        if j == 0:
                            w1w.append(("dv", V_S1))
                        if j <= 2 and bb >= 1:
                            w1w.append(("sc", v_gelu_hist[{0: 30, 1: 31, 2: 29}[j]]))
                        if j >= 3:
                            w1w.append(("sc", v_gelu_hist[j - 3]))
                        for kc in range(8):
                            kw = list(w1w) if kc == 0 else []
                            if j == 0:
                                kw.append(("sc", v_y1bc[(bb, kc)]))
                            def mm(e, j=j, kc=kc, ps=ps, bb=bb):
                                return e.matmul(ps[:, 0:256],
                                                s1T[:, kc, j * 128:(j + 1) * 128],
                                                y1b[bb % 2][:, kc, :],
                                                start=(kc == 0), stop=(kc == 7))
                            P.add("pe", mm, waits=kw,
                                  inc=("pe", 1) if kc == 7 else None)
                        vf1 = P.val("pe")
                        gw0 = [("pe", vf1), ("dv", PR["V_A1"])]
                        if bb == 0 and j == 0:
                            gw0.append(("dI", V_CONST))
                        vg = P.add("sc",
                                   lambda e, j=j, ps=ps: e.activation(
                                       hb[:, j % 4, :], ps[:, 0:256], AF.Gelu,
                                       bias=b1c[:, j:j + 1], scale=a1c[:, j:j + 1]),
                                   waits=gw0, inc=("sc", 1))
                        v_gelu_hist[j] = vg
                    jj = j - 2
                    if jj >= 0 and jj < NJ:
                        g2 = jj // 2
                        s2sem = "dS2" if g2 % 2 == 0 else "dS2b"
                        f2w = [("sc", v_gelu_hist[jj])]
                        if jj == 0:
                            f2w.append((s2sem, v_s2rd[(bb, g2)]))
                            if bb > 0:
                                f2w.append(("sc", v_epi[bb - 1]))
                            f2w.append(("dv", PR["V_A2"]))
                        elif jj % 2 == 0:
                            f2w.append((s2sem, v_s2rd[(bb, g2)]))
                        for mc in range(8):
                            def mm2(e, jj=jj, mc=mc, g2=g2, bb=bb):
                                return e.matmul(psA[:, mc * 256:mc * 256 + 256],
                                                s2b[:, g2 % 2, jj % 2, mc * 128:(mc + 1) * 128],
                                                hb[:, jj % 4, :],
                                                start=(jj == 0 and mc % 2 == 0),
                                                stop=(jj == NJ - 1), skip_group_check=True)
                            last = (jj == NJ - 1)
                            P.add("pe", mm2, waits=f2w if mc == 0 else (),
                                  inc=("pe", 1) if (mc == 7 or last) else None)
                            if last:
                                v_ffn2_last[(bb, mc)] = P.val("pe")
                        if jj % 2 == 1:
                            gd = jj // 2
                            v_ffn2_grp[(bb, gd)] = P.val("pe")
                            gg, b2 = gd + 2, bb
                            if gg >= NG:
                                gg -= NG
                                b2 += 1
                            if b2 < NB and (b2, gg) not in v_s2rd:
                                v_s2rd[(b2, gg)] = P.add(
                                    "gp",
                                    lambda e, b2=b2, gg=gg: e.dma_start(
                                        s2b[:, gg % 2, :, :],
                                        d_s2d[gg * 256:(gg + 1) * 256, :].rearrange("(j p) e -> p j e", p=128)),
                                    waits=[("pe", v_ffn2_grp[(bb, gd)])],
                                    inc=("dS2" if gg % 2 == 0 else "dS2b", 16))

            v_cin[0] = P.add("sy",
                             lambda e: e.dma_start(cin, d_ctxs.rearrange("(c p) t -> p c t", p=128)[:, :, 0:TB]),
                             waits=[("dCW", 16 * 15), ("dCW1", 16 * 15)], inc=("dCR", 16))
            v_xin[0] = P.add("sy",
                             lambda e: e.dma_start(xin[0], d_x.rearrange("(c p) t -> p c t", p=128)[:, :, 0:TB]),
                             waits=[("dCW", 16 * H), ("dCW1", 16 * H)], inc=("dXR", 16))
            for g in range(2):
                v_s2rd[(0, g)] = P.add("gp",
                                       lambda e, g=g: e.dma_start(
                                           s2b[:, g % 2, :, :],
                                           d_s2d[g * 256:(g + 1) * 256, :].rearrange("(j p) e -> p j e", p=128)),
                                       waits=[("dSP", V_SPILLS)],
                                       inc=("dS2" if g % 2 == 0 else "dS2b", 16))

            emit_outproj(0)
            v_cin[1] = P.add("sy",
                             lambda e: e.dma_start(cin, d_ctxs.rearrange("(c p) t -> p c t", p=128)[:, :, TB:2 * TB]),
                             waits=[("pe", v_outproj[(0, 7)])], inc=("dCR", 16))
            v_xin[1] = P.add("sy",
                             lambda e: e.dma_start(xin[1], d_x.rearrange("(c p) t -> p c t", p=128)[:, :, TB:2 * TB]),
                             waits=[("dv", v_res1[(0, 7)])], inc=("dXR", 16))
            emit_sums1(0)
            emit_bcast1(0)
            emit_apply1(0)

            for b in range(NB):
                emit_ffn_steps(b, 0, 8)
                if b >= 1:
                    emit_sums2(b - 1)
                emit_ffn_steps(b, 8, 12)
                if b >= 1:
                    emit_bcast2(b - 1)
                emit_ffn_steps(b, 12, 13)
                if b >= 1:
                    emit_apply2(b - 1)
                if b + 1 < NB:
                    for mc in range(8):
                        emit_outproj_mc(b + 1, mc)
                        emit_ffn_steps(b, 13 + mc, 14 + mc)
                    emit_outproj_tail(b + 1)
                    if b + 2 < NB:
                        v_cin[b + 2] = P.add("sy",
                                             lambda e, b=b: e.dma_start(
                                                 cin, d_ctxs.rearrange("(c p) t -> p c t", p=128)[:, :, (b + 2) * TB:(b + 3) * TB]),
                                             waits=[("pe", v_outproj[(b + 1, 7)])], inc=("dCR", 16))
                        v_xin[b + 2] = P.add("sy",
                                             lambda e, b=b: e.dma_start(
                                                 xin[(b + 2) % 2],
                                                 d_x.rearrange("(c p) t -> p c t", p=128)[:, :, (b + 2) * TB:(b + 3) * TB]),
                                             waits=[("dv", v_res1[(b + 1, 7)])], inc=("dXR", 16))
                else:
                    emit_ffn_steps(b, 13, 21)
                emit_ffn_steps(b, 21, 24)
                if b + 1 < NB:
                    emit_sums1(b + 1)
                emit_ffn_steps(b, 24, 27)
                if b + 1 < NB:
                    emit_bcast1(b + 1)
                emit_ffn_steps(b, 27, NJ + 2)
                if b + 1 < NB:
                    emit_apply1(b + 1)
                emit_res2part1(b)
                emit_res2part2(b)

            emit_sums2(NB - 1)
            emit_bcast2(NB - 1)
            emit_apply2(NB - 1)

        P.add("sy", lambda e: None, waits=[("dO", P.val("dO"))])

        def emit(eng_key):
            def run(eng):
                for waits, fn, inc in P.items[eng_key]:
                    for s, v in waits:
                        if v is None:
                            continue
                        eng.wait_ge(SEM[s], v)
                    ins = fn(eng)
                    if inc is not None:
                        assert ins is not None
                        ins.then_inc(SEM[inc[0]], inc[1])
            return run

        @block.sync
        def _(eng):
            emit("sy")(eng)

        @block.gpsimd
        def _(eng):
            emit("gp")(eng)

        @block.tensor
        def _(eng):
            emit("pe")(eng)

        @block.vector
        def _(eng):
            emit("dv")(eng)

        @block.scalar
        def _(eng):
            emit("sc")(eng)

    return nc


def _prep_shared(inp):
    bf = ml_dtypes.bfloat16
    wq = np.ascontiguousarray(inp["L0_attn_Wq"], dtype=np.float32)
    q2l = np.ascontiguousarray(
        inp["L0_attn_q_to_latent"].reshape(8, 2, 64, 64).transpose(1, 2, 0, 3).reshape(128, 512),
        dtype=np.float32)
    wkvT = np.ascontiguousarray(
        np.concatenate([inp["L0_attn_Wk_down"], inp["L0_attn_Wv_down"]], axis=0).T, dtype=np.float32)
    woutT = np.ascontiguousarray(inp["L0_attn_Wout"].T, dtype=np.float32)
    vupT = np.ascontiguousarray(
        inp["L0_attn_v_up"].reshape(8, 2, 64, 64).transpose(1, 3, 0, 2).reshape(128, 512),
        dtype=np.float32)
    w1 = np.ascontiguousarray(inp["L0_ffn_w1"], dtype=np.float32)
    w2 = np.ascontiguousarray(inp["L0_ffn_w2"], dtype=np.float32)
    s = np.arange(128)
    shared = dict(
        wq=wq,
        q2l=q2l,
        wkvT=wkvT,
        woutT=woutT,
        vupT=vupT,
        w1T=np.ascontiguousarray(w1.T),
        w2T=np.ascontiguousarray(w2.T),
        w1n=w1,
        w2n=w2,
        b1c=np.ascontiguousarray(inp["L0_ffn_b1"].reshape(32, 128).T, dtype=np.float32),
        b2c=np.ascontiguousarray(inp["L0_ffn_b2"].reshape(8, 128).T, dtype=np.float32),
        mask01=np.ascontiguousarray((s[None, :] >= s[:, None]).astype(bf)),
        ident=np.eye(128, dtype=bf),
        onesb=np.ones((128, 1), dtype=bf),
        onesf=np.ones((1, 128), dtype=np.float32),
    )
    return shared


def kernel(**inputs):
    x = np.asarray(inputs["L0_attn_hidden_states"], dtype=np.float32)
    shared = _prep_shared(inputs)
    nc = build(debug=False)
    in_maps = []
    for i in range(NCORES):
        m = dict(shared)
        m["x_fm"] = np.ascontiguousarray(x[i].T)
        in_maps.append(m)
    res = run_bass_kernel_spmd(nc, in_maps, core_ids=list(range(NCORES)))
    out = np.empty((NCORES, T, E), dtype=np.float32)
    for i in range(NCORES):
        out[i] = res.results[i]["y_fm"].T
    return out


if __name__ == "__main__":
    nc = build(debug=False)
    print("instructions:", nc.n_instructions())


# version 25
# speedup vs baseline: 1.0260x; 1.0260x over previous
import numpy as np
import ml_dtypes

import concourse.bass as bass
import concourse.mybir as mybir
from concourse.bass_utils import run_bass_kernel_spmd

F32 = mybir.dt.float32
BF16 = mybir.dt.bfloat16
I32 = mybir.dt.int32
AF = mybir.ActivationFunctionType
ALU = mybir.AluOpType
AX = mybir.AxisListType

NCORES = 8
T = 2048
E = 1024
H = 16
L = 64
HF = 4096
TB = 256
NB = T // TB
NJ = HF // 128
EPS = 1e-5
RSQRT_MAGIC = 0x5F3759DF

ENGS = ("sy", "gp", "pe", "dv", "sc")


class Plan:

    def __init__(self):
        self.items = {e: [] for e in ENGS}
        self.c = {}

    def val(self, sem):
        return self.c.get(sem, 0)

    def add(self, eng, fn, waits=(), inc=None):
        tgt = None
        if inc is not None:
            s, a = inc
            self.c[s] = self.c.get(s, 0) + a
            tgt = self.c[s]
        self.items[eng].append((tuple(waits), fn, inc))
        return tgt


def _stripes(jt):
    out = []
    p = jt * 128
    while p < T:
        e = min((p // 512 + 1) * 512, T)
        out.append((p, e))
        p = e
    return out


def build(debug=False, phase=8, bstage=7):
    nc = bass.Bass("TRN2", target_bir_lowering=False)

    d_x = nc.dram_tensor("x_fm", [E, T], F32, kind="ExternalInput").ap()
    d_wq = nc.dram_tensor("wq", [E, E], F32, kind="ExternalInput").ap()
    d_q2l = nc.dram_tensor("q2l", [128, 512], F32, kind="ExternalInput").ap()
    d_wkv = nc.dram_tensor("wkvT", [E, 128], F32, kind="ExternalInput").ap()
    d_wout = nc.dram_tensor("woutT", [E, E], F32, kind="ExternalInput").ap()
    d_vup = nc.dram_tensor("vupT", [128, 512], F32, kind="ExternalInput").ap()
    d_w1t = nc.dram_tensor("w1T", [E, HF], F32, kind="ExternalInput").ap()
    d_w2t = nc.dram_tensor("w2T", [HF, E], F32, kind="ExternalInput").ap()
    d_w1n = nc.dram_tensor("w1n", [HF, E], F32, kind="ExternalInput").ap()
    d_w2n = nc.dram_tensor("w2n", [E, HF], F32, kind="ExternalInput").ap()
    d_b1 = nc.dram_tensor("b1c", [128, 32], F32, kind="ExternalInput").ap()
    d_b2 = nc.dram_tensor("b2c", [128, 8], F32, kind="ExternalInput").ap()
    d_mask = nc.dram_tensor("mask01", [128, 128], BF16, kind="ExternalInput").ap()
    d_ident = nc.dram_tensor("ident", [128, 128], BF16, kind="ExternalInput").ap()
    d_onesb = nc.dram_tensor("onesb", [128, 1], BF16, kind="ExternalInput").ap()
    d_onesf = nc.dram_tensor("onesf", [1, 128], F32, kind="ExternalInput").ap()

    import os as _os
    spy = bool(_os.environ.get("KSPY"))
    d_y = nc.dram_tensor("y_fm", [E, T], F32, kind="ExternalOutput").ap()
    scratch_kind = "ExternalOutput" if (debug or spy) else "Internal"
    d_ctxs = nc.dram_tensor("ctxs", [E, T], BF16, kind=scratch_kind).ap()
    d_s2d = nc.dram_tensor("s2d", [HF, E], BF16, kind=scratch_kind).ap()
    d_rs_spy = nc.dram_tensor("rs_spy", [H, 2, 2, 64, 1024], F32, kind=scratch_kind).ap() if spy else None
    d_krep_spy = nc.dram_tensor("krep_spy", [128, T], BF16, kind=scratch_kind).ap() if spy else None
    d_qlat_spy = nc.dram_tensor("qlat_spy", [128, 8, T], BF16, kind=scratch_kind).ap() if spy else None
    d_v1_spy = nc.dram_tensor("v1_spy", [128, 16 * 128], BF16, kind=scratch_kind).ap() if spy else None
    if debug:
        d_tqlat = nc.dram_tensor("t_qlat", [128, 8, T], BF16, kind="ExternalOutput").ap()
        d_tkrep = nc.dram_tensor("t_krep", [128, T], BF16, kind="ExternalOutput").ap()
        d_tv1 = nc.dram_tensor("t_v1", [128, 16 * 128], BF16, kind="ExternalOutput").ap()
        d_ty1 = nc.dram_tensor("t_y1", [128, NB, 8, TB], F32, kind="ExternalOutput").ap()
        d_tffn = nc.dram_tensor("t_ffn", [128, NB, 8, TB], F32, kind="ExternalOutput").ap()
        d_ta1 = nc.dram_tensor("t_a1", [128, 32], F32, kind="ExternalOutput").ap()

    from contextlib import ExitStack
    with ExitStack() as ctx:
        s1T_h = ctx.enter_context(nc.sbuf_tensor("s1T", [128, 8, HF], BF16))
        s2b_h = ctx.enter_context(nc.sbuf_tensor("s2b", [128, 2, 2, E], BF16))
        vcomb_h = ctx.enter_context(nc.sbuf_tensor("vcomb", [128, 8, E], BF16))
        ar1_h = ctx.enter_context(nc.sbuf_tensor("arena1", [128, 32768], mybir.dt.uint8))
        ar2_h = ctx.enter_context(nc.sbuf_tensor("arena2", [128, 32768], mybir.dt.uint8))
        ar4_h = ctx.enter_context(nc.sbuf_tensor("arena4", [128, 16384], mybir.dt.uint8))
        v1_h = ctx.enter_context(nc.sbuf_tensor("v1", [128, 16, 128], BF16))
        krep_h = ctx.enter_context(nc.sbuf_tensor("krep", [128, T], BF16))
        q2lb_h = ctx.enter_context(nc.sbuf_tensor("q2lb", [128, 512], BF16))
        vupb_h = ctx.enter_context(nc.sbuf_tensor("vupb", [128, 512], BF16))
        wkvb_h = ctx.enter_context(nc.sbuf_tensor("wkvb", [128, 8, 128], BF16))

        rs_h = ctx.enter_context(nc.sbuf_tensor("rs", [128, 1024], F32))
        rs2_h = ctx.enter_context(nc.sbuf_tensor("rs2", [128, 1024], F32))
        ctxn_h = ctx.enter_context(nc.sbuf_tensor("ctxn", [64, T], BF16))
        a1c_h = ctx.enter_context(nc.sbuf_tensor("a1c", [128, 32], F32))
        a2c_h = ctx.enter_context(nc.sbuf_tensor("a2c", [128, 8], F32))
        b1c_h = ctx.enter_context(nc.sbuf_tensor("b1c_s", [128, 32], F32))
        b2c_h = ctx.enter_context(nc.sbuf_tensor("b2c_s", [128, 8], F32))
        mask_h = ctx.enter_context(nc.sbuf_tensor("mask01_s", [128, 128], BF16))
        ident_h = ctx.enter_context(nc.sbuf_tensor("ident_s", [128, 128], BF16))
        onesb_h = ctx.enter_context(nc.sbuf_tensor("onesb_s", [128, 1], BF16))
        onesf_h = ctx.enter_context(nc.sbuf_tensor("onesf_s", [1, 128], F32))
        hb_h = ctx.enter_context(nc.sbuf_tensor("hb", [128, 4, 256], BF16))
        psA_h = ctx.enter_context(nc.psum_tensor("psA", [128, 2048], F32))
        psS4_h = ctx.enter_context(nc.psum_tensor("psS4", [128, 2048], F32))
        S_dW = ctx.enter_context(nc.semaphore("dW"))
        S_qA = ctx.enter_context(nc.semaphore("qA"))
        S_qB = ctx.enter_context(nc.semaphore("qB"))
        S_qC = ctx.enter_context(nc.semaphore("qC"))
        S_qD = ctx.enter_context(nc.semaphore("qD"))
        S_qE0 = ctx.enter_context(nc.semaphore("qE0"))
        S_qE1 = ctx.enter_context(nc.semaphore("qE1"))
        S_qF0 = ctx.enter_context(nc.semaphore("qF0"))
        S_qF1 = ctx.enter_context(nc.semaphore("qF1"))
        S_qG0 = ctx.enter_context(nc.semaphore("qG0"))
        S_qG1 = ctx.enter_context(nc.semaphore("qG1"))
        S_qH0 = ctx.enter_context(nc.semaphore("qH0"))
        S_qH1 = ctx.enter_context(nc.semaphore("qH1"))
        S_dS2b = ctx.enter_context(nc.semaphore("dS2b"))
        S_dI = ctx.enter_context(nc.semaphore("dI"))
        S_dCW = ctx.enter_context(nc.semaphore("dCW"))
        S_dCW1 = ctx.enter_context(nc.semaphore("dCW1"))
        S_dCR = ctx.enter_context(nc.semaphore("dCR"))
        S_dXR = ctx.enter_context(nc.semaphore("dXR"))
        S_dS2 = ctx.enter_context(nc.semaphore("dS2"))
        S_dSP = ctx.enter_context(nc.semaphore("dSP"))
        S_dO = ctx.enter_context(nc.semaphore("dO"))
        S_pe = ctx.enter_context(nc.semaphore("pe"))
        S_sc = ctx.enter_context(nc.semaphore("sc"))
        S_dv = ctx.enter_context(nc.semaphore("dv"))
        S_gp = ctx.enter_context(nc.semaphore("gp"))
        block = ctx.enter_context(nc.Block())
        SEM = dict(dW=S_dW, dI=S_dI, dCW=S_dCW, dCW1=S_dCW1, dCR=S_dCR, dXR=S_dXR, dS2=S_dS2,
                   dS2b=S_dS2b, dSP=S_dSP, dO=S_dO, pe=S_pe, sc=S_sc, dv=S_dv, gp=S_gp,
                   qA=S_qA, qB=S_qB, qC=S_qC, qD=S_qD, qE0=S_qE0, qE1=S_qE1,
                   qF0=S_qF0, qF1=S_qF1, qG0=S_qG0, qG1=S_qG1, qH0=S_qH0, qH1=S_qH1)

        s1T = s1T_h.ap()
        s1flat = s1T.rearrange("p a b -> p (a b)")
        wqv = s1flat[:, 0:8192].rearrange("p (c e) -> p c e", c=8)
        woutv = s1flat[:, 8192:16384].rearrange("p (c e) -> p c e", c=8)
        s2b = s2b_h.ap()
        s2stage = s2b.rearrange("p a b e -> p (a b e)")
        vcomb = vcomb_h.ap()

        ar1 = ar1_h.ap()
        qlat = ar1[:, :].bitcast(BF16).rearrange("p (c t) -> p c t", c=8)
        xin = [ar1[:, 0:8192].bitcast(F32).rearrange("p (c t) -> p c t", c=8),
               ar1[:, 8192:16384].bitcast(F32).rearrange("p (c t) -> p c t", c=8)]
        y1f = ar1[:, 16384:24576].bitcast(F32).rearrange("p (c t) -> p c t", c=8)
        res2f = ar1[:, 24576:32768].bitcast(F32).rearrange("p (c t) -> p c t", c=8)

        ar2 = ar2_h.ap()
        xb = ar2[:, :].bitcast(BF16).rearrange("p (c t) -> p c t", c=8)
        wstage = [ar2[:, 0:8192].bitcast(BF16),
                  ar2[:, 8192:16384].bitcast(BF16)]
        wnstage = [ar2[:, 16384:24576].bitcast(BF16).rearrange("p (r e) -> p r e", r=4),
                   ar2[:, 24576:32768].bitcast(BF16).rearrange("p (r e) -> p r e", r=4)]
        vfm = ar2[0:64, 0:4096].bitcast(BF16)
        y2 = ar2[:, 0:8192].bitcast(F32).rearrange("p (c t) -> p c t", c=8)
        ffnout = ar2[:, 8192:16384].bitcast(F32).rearrange("p (c t) -> p c t", c=8)
        strips = ar2[0:1, 16384:28672].bitcast(F32)
        S_MEAN = (0, 256)
        S_EX2 = (256, 512)
        S_VAR = (512, 768)
        S_Y = (768, 1024)
        S_T = (1024, 1280)
        S_NMR = (1280, 1536)

        def strip(rg):
            return strips[:, rg[0]:rg[1]]

        def strip_i(rg):
            return strips[:, rg[0]:rg[1]].bitcast(I32)

        ar4 = ar4_h.ap()
        wq2l = ar4[:, :].bitcast(BF16).rearrange("p (c e) -> p c e", c=8)
        cin = ar4[:, 0:4096].bitcast(BF16).rearrange("p (c t) -> p c t", c=8)
        xresb = ar4[:, 4096:8192].bitcast(BF16).rearrange("p (c t) -> p c t", c=8)
        xsq = ar4[:, 8192:12288].bitcast(BF16).rearrange("p (c t) -> p c t", c=8)
        y1b = [ar4[:, 12288:16384].bitcast(BF16).rearrange("p (c t) -> p c t", c=8),
               ar2[:, 28672:32768].bitcast(BF16).rearrange("p (c t) -> p c t", c=8)]

        v1 = v1_h.ap()
        krep = krep_h.ap()
        q2lb = q2lb_h.ap()
        vupb = vupb_h.ap()
        wkvb = wkvb_h.ap()
        ptS_h = ctx.enter_context(nc.sbuf_tensor("ptS", [128, 4096], BF16))
        ptS = ptS_h.ap()
        y1fb = [y1f, ptS_h.ap().bitcast(F32).rearrange("p (c t) -> p c t", c=8)]
        rs = rs_h.ap()
        rs2 = rs2_h.ap()
        ctxn = ctxn_h.ap()
        a1c = a1c_h.ap()
        a2c = a2c_h.ap()
        b1c = b1c_h.ap()
        b2c = b2c_h.ap()
        mask01 = mask_h.ap()
        ident = ident_h.ap()
        onesb = onesb_h.ap()
        onesf = onesf_h.ap()
        hb = hb_h.ap()

        psA = psA_h.ap()
        psS4 = psS4_h.ap()
        psB = psS4[:, 0:512]
        psC = psS4[:, 512:1024]
        psD = psS4[:, 1024:1536]
        psE = psS4[:, 1536:2048]
        psB_bf = psB.bitcast(BF16)
        psC_bf = psC.bitcast(BF16)

        P = Plan()

        for dst, src in ((mask01, d_mask), (ident, d_ident), (onesb, d_onesb),
                         (onesf, d_onesf), (b1c, d_b1), (b2c, d_b2)):
            P.add("sy", lambda e, a=dst, b=src: e.dma_start(a, b), inc=("dI", 16))
        V_CONST = P.val("dI")

        def gload(dst, src, sem, waits=()):
            return P.add("gp", lambda e, a=dst, b=src: e.dma_start(a, b),
                         waits=waits, inc=(sem, 16))

        V_XB = gload(xb, d_x.rearrange("(c p) t -> p c t", p=128), "qA")
        V_WQ = gload(wqv, d_wq.rearrange("(c p) t -> p c t", p=128), "qB")
        V_Q2L = gload(q2lb, d_q2l, "qB")
        V_WKV = gload(wkvb, d_wkv.rearrange("(c p) t -> p c t", p=128), "qC")
        V_WOUT = gload(woutv, d_wout.rearrange("(c p) t -> p c t", p=128), "qD")
        V_VUP = gload(vupb, d_vup, "qD")

        if debug and phase >= 1:
            dbg_xb = nc.dram_tensor("t_xb", [128, 8, T], BF16, kind="ExternalOutput").ap()
            P.add("sy", lambda e: e.dma_start(dbg_xb, xb), waits=[("qA", 16)], inc=("dO", 16))
        v_evac_wq2l = [0] * 8
        for ec in (range(8) if phase >= 2 else ()):
            waits = [("qB", 32)]
            if ec >= 2:
                waits.append(("sc", v_evac_wq2l[ec - 2]))
            for h in range(H):
                po = (h % 2) * 64
                col = (ec % 2) * 1024 + (h % 2) * 512 + (h // 2) * 64
                def mm(e, ec=ec, h=h, po=po, col=col):
                    return e.matmul(
                        psA[:, col:col + 64],
                        wqv[po:po + 64, h // 2, ec * 128:(ec + 1) * 128],
                        q2lb[po:po + 64, (h // 2) * 64:(h // 2) * 64 + 64],
                        start=True, stop=True)
                P.add("pe", mm, waits=waits if h == 0 else (), inc=("pe", 1) if h == H - 1 else None)
            vp = P.val("pe")
            v_evac_wq2l[ec] = P.add(
                "sc",
                lambda e, ec=ec: e.activation(
                    wq2l[:, ec, :].rearrange("p (c a l) -> p c a l", a=2, c=8, l=64),
                    psA[:, (ec % 2) * 1024:(ec % 2) * 1024 + 1024].rearrange(
                        "p (a c l) -> p c a l", a=2, c=8, l=64),
                    AF.Copy, scale=0.125),
                waits=[("pe", vp)], inc=("sc", 1))

        if debug and phase >= 2:
            dbg_wq2l = nc.dram_tensor("t_wq2l", [128, 8, E], BF16, kind="ExternalOutput").ap()
            P.add("sy", lambda e: e.dma_start(dbg_wq2l, wq2l), waits=[("sc", v_evac_wq2l[7])], inc=("dO", 16))
        v_evac_vcomb = [0] * 8
        for c in (range(8) if phase >= 3 else ()):
            waits = [("qD", 32), ("sc", v_evac_wq2l[7])]
            if c >= 2:
                waits = [("sc", v_evac_vcomb[c - 2])]
            first = True
            for h, po in ((2 * c, 0), (2 * c + 1, 64)):
                for nb2 in range(2):
                    def mm(e, c=c, h=h, po=po, nb2=nb2):
                        return e.matmul(
                            psA[po:po + 64, (c % 2) * 1024 + nb2 * 512:(c % 2) * 1024 + nb2 * 512 + 512],
                            vupb[po:po + 64, (h // 2) * 64:(h // 2) * 64 + 64],
                            woutv[po:po + 64, h // 2, nb2 * 512:nb2 * 512 + 512],
                            start=True, stop=True,
                            tile_position=(po, po))
                    last = (h % 2 == 1 and nb2 == 1)
                    P.add("pe", mm, waits=waits if first else (), inc=("pe", 1) if last else None)
                    first = False
            vp = P.val("pe")
            v_evac_vcomb[c] = P.add(
                "sc",
                lambda e, c=c: e.copy(vcomb[:, c, :], psA[:, (c % 2) * 1024:(c % 2) * 1024 + 1024]),
                waits=[("pe", vp)], inc=("sc", 1))

        if debug and phase >= 3:
            dbg_vc = nc.dram_tensor("t_vcomb", [128, 8, E], BF16, kind="ExternalOutput").ap()
            P.add("sy", lambda e: e.dma_start(dbg_vc, vcomb), waits=[("sc", v_evac_vcomb[7])], inc=("dO", 16))
        v_qlat_evac = []
        unit = 0
        for mc in (range(8) if phase >= 4 else ()):
            for th in range(2):
                half = unit % 2
                waits = [("qA", 16), ("sc", v_evac_vcomb[7])]
                if len(v_qlat_evac) >= 2:
                    eng, val = v_qlat_evac[unit - 2]
                    waits.append((eng, val))
                for tc2 in range(2):
                    for kc in range(8):
                        def mm(e, mc=mc, th=th, half=half, tc2=tc2, kc=kc):
                            return e.matmul(
                                psA[:, half * 1024 + tc2 * 512: half * 1024 + tc2 * 512 + 512],
                                wq2l[:, kc, mc * 128:(mc + 1) * 128],
                                xb[:, kc, th * 1024 + tc2 * 512: th * 1024 + tc2 * 512 + 512],
                                start=(kc == 0), stop=(kc == 7))
                        P.add("pe", mm,
                              waits=waits if (tc2 == 0 and kc == 0) else (),
                              inc=("pe", 1) if (tc2 == 1 and kc == 7) else None)
                vp = P.val("pe")
                ev_eng = "dv"
                def ev(e, mc=mc, th=th, half=half, ev_eng=ev_eng):
                    if ev_eng == "dv":
                        return e.tensor_copy(qlat[:, mc, th * 1024: th * 1024 + 1024],
                                             psA[:, half * 1024: half * 1024 + 1024])
                    return e.copy(qlat[:, mc, th * 1024: th * 1024 + 1024],
                                  psA[:, half * 1024: half * 1024 + 1024])
                val = P.add(ev_eng, ev, waits=[("pe", vp)], inc=(ev_eng, 1))
                v_qlat_evac.append((ev_eng, val))
                unit += 1

        kv_waits = [("qC", 16)] + [(e, v) for e, v in v_qlat_evac[-2:]]
        for tc in (range(4) if phase >= 5 else ()):
            for kc in range(8):
                def mm(e, tc=tc, kc=kc):
                    return e.matmul(psA[:, tc * 512:tc * 512 + 512],
                                    wkvb[:, kc, :],
                                    xb[:, kc, tc * 512:tc * 512 + 512],
                                    start=(kc == 0), stop=(kc == 7))
                P.add("pe", mm, waits=kv_waits if (tc == 0 and kc == 0) else (),
                      inc=("pe", 1) if (tc == 3 and kc == 7) else None)
        V_KV = P.val("pe")

        if phase < 5:
            V_KV = None
        V_K0 = (P.add("sc", lambda e: e.copy(krep[0:64, :], psA[0:64, :]),
                     waits=[("pe", V_KV)], inc=("sc", 1)) if phase >= 5 else None)
        V_K1 = (P.add("sc", lambda e: e.copy(krep[64:128, :], psA[0:64, :]), inc=("sc", 1)) if phase >= 5 else None)
        V_VF = (P.add("sc", lambda e: e.copy(vfm, psA[64:128, :]), inc=("sc", 1)) if phase >= 5 else None)

        V_ONES = (P.add("dv", lambda e: e.memset(v1[:, :, 64:128], 1.0), inc=("dv", 1)) if phase >= 5 else None)

        v_tr = [0] * 16
        for jt in (range(16) if phase >= 5 else ()):
            waits = [("sc", V_VF)]
            if jt == 0:
                waits.append(("dI", V_CONST))
            if jt >= 2:
                waits.append(("dv", v_tr[jt - 2]))
            def tr(e, jt=jt):
                dst = psB_bf if jt % 2 == 0 else psC_bf
                return e.transpose(dst[:, 0:64],
                                   vfm[:, jt * 128:(jt + 1) * 128],
                                   ident[0:64, 0:64])
            P.add("pe", tr, waits=waits, inc=("pe", 1))
            vp = P.val("pe")
            def cp(e, jt=jt):
                srcb = psB_bf if jt % 2 == 0 else psC_bf
                return e.tensor_copy(v1[:, jt, 0:64], srcb[:, 0:64])
            v_tr[jt] = P.add("dv", cp, waits=[("pe", vp)], inc=("dv", 1))
        V_V1 = v_tr[15] if phase >= 5 else None
        v_tr_last = V_V1

        if debug and phase >= 4:
            P.add("sy", lambda e: e.dma_start(d_tqlat, qlat),
                  waits=[(e, v) for e, v in v_qlat_evac[-2:]], inc=("dO", 16))
        if debug and phase >= 5:
            P.add("sy", lambda e: e.dma_start(d_tkrep, krep), waits=[("sc", V_K1)], inc=("dO", 16))
            P.add("sy", lambda e: e.dma_start(d_tv1, v1.rearrange("p a b -> p (a b)")),
                  waits=[("dv", V_V1), ("dv", V_ONES)], inc=("dO", 16))

        PR = {}
        v_a2red = [0] * 8
        v_s2sign = [0] * 8
        v_a1red = [0] * 32
        v_s1sign = [0] * 8

        def prep_gen():
            for c in range(8):
                waits = [("pe", V_KV)]
                if c >= 2:
                    waits.append(("dv", v_a2red[c - 2]))
                vdma = P.add("gp",
                             lambda e, c=c: e.dma_start(
                                 wnstage[c % 2].rearrange("p r e -> p (r e)"),
                                 d_w2n[c * 128:(c + 1) * 128, :]),
                             waits=waits, inc=(f"qE{c % 2}", 16))
                yield
                v_a2red[c] = P.add("dv",
                                   lambda e, c=c: e.reduce_sum(
                                       a2c[:, c:c + 1],
                                       wnstage[c % 2].rearrange("p r e -> p (r e)"),
                                       axis=AX.X, apply_absolute_value=True),
                                   waits=[(f"qE{c % 2}", vdma)], inc=("dv", 1))
            PR["V_A2"] = P.add("dv",
                               lambda e: e.tensor_scalar(a2c, a2c, 1.0 / HF, None, op0=ALU.mult),
                               waits=[("dv", v_a2red[7])], inc=("dv", 1))
            for g in range(8):
                waits = []
                if g == 0:
                    waits.append(("dv", v_tr_last))
                if g >= 2:
                    waits.append(("dv", v_s2sign[g - 2]))
                vdma = P.add("gp",
                             lambda e, g=g: e.dma_start(
                                 wstage[g % 2].rearrange("p (r e) -> p r e", r=4),
                                 d_w2t[g * 512:(g + 1) * 512, :].rearrange("(r p) e -> p r e", p=128)),
                             waits=waits, inc=(f"qF{g % 2}", 16))
                yield
                w = [(f"qF{g % 2}", vdma)]
                if g >= 1:
                    w.append(("dSP", 16 * g))
                v_s2sign[g] = P.add("dv",
                      lambda e, g=g: e.tensor_scalar(
                          s2stage.bitcast(mybir.dt.int16), wstage[g % 2].bitcast(mybir.dt.int16),
                          0x8000 - 0x10000, 0x3F80, op0=ALU.bitwise_and, op1=ALU.bitwise_or),
                      waits=w, inc=("dv", 1))
                P.add("gp",
                      lambda e, g=g: e.dma_start(
                          d_s2d[g * 512:(g + 1) * 512, :].rearrange("(r p) e -> p r e", p=128),
                          s2stage.rearrange("p (r e) -> p r e", r=4)),
                      waits=[("dv", v_s2sign[g])], inc=("dSP", 16))
            for g in range(8):
                waits = []
                if g < 2:
                    waits.append(("dv", v_a2red[6 + g]))
                if g >= 2:
                    waits.append(("dv", v_a1red[4 * (g - 2) + 3]))
                vdma = P.add("gp",
                             lambda e, g=g: e.dma_start(
                                 wnstage[g % 2],
                                 d_w1n[g * 512:(g + 1) * 512, :].rearrange("(r p) e -> p r e", p=128)),
                             waits=waits, inc=(f"qG{g % 2}", 16))
                yield
                for r in range(4):
                    v_a1red[4 * g + r] = P.add(
                        "dv",
                        lambda e, g=g, r=r: e.reduce_sum(a1c[:, 4 * g + r:4 * g + r + 1],
                                                         wnstage[g % 2][:, r, :],
                                                         axis=AX.X, apply_absolute_value=True),
                        waits=[(f"qG{g % 2}", vdma)] if r == 0 else (), inc=("dv", 1))
            PR["V_A1"] = P.add("dv",
                               lambda e: e.tensor_scalar(a1c, a1c, 1.0 / E, None, op0=ALU.mult),
                               waits=[("dv", v_a1red[31])], inc=("dv", 1))
            for c in range(8):
                waits = []
                if c < 2:
                    waits.append(("dv", v_s2sign[6 + c]))
                if c >= 2:
                    waits.append(("dv", v_s1sign[c - 2]))
                vdma = P.add("gp",
                             lambda e, c=c: e.dma_start(wstage[c % 2], d_w1t[c * 128:(c + 1) * 128, :]),
                             waits=waits, inc=(f"qH{c % 2}", 16))
                yield
                v_s1sign[c] = P.add("dv",
                                    lambda e, c=c: e.tensor_scalar(
                                        s1T[:, c, :].bitcast(mybir.dt.int16), wstage[c % 2].bitcast(mybir.dt.int16),
                                        0x8000 - 0x10000, 0x3F80,
                                        op0=ALU.bitwise_and, op1=ALU.bitwise_or),
                                    waits=[(f"qH{c % 2}", vdma)], inc=("dv", 1))
            if debug:
                P.add("sy", lambda e: e.dma_start(d_ta1, a1c), waits=[("dv", PR["V_A1"])], inc=("dO", 16))
            while True:
                yield

        if phase >= 6:
            _pg = prep_gen()

            def emit_prep_slice(n):
                for _ in range(n):
                    next(_pg)
        else:
            def emit_prep_slice(n):
                pass

        if phase == 6:
            emit_prep_slice(64)
            V_SPILLS = 16 * 8
        if phase < 7:
            v_head_mult = None
        if phase >= 7:
            sidx = 0
            stripe_hist = []
            v_head_mult = [0] * H
            prev_srecip_mult = {0: None, 1024: None}
            prev_rs_read = {0: None, 1024: None}
            v_head_ln = [0] * H
            v_spill = [0] * H

            def half_stripes(jt, half):
                lo, hi = half * 1024, half * 1024 + 1024
                out = []
                p = max(jt * 128, lo)
                while p < hi:
                    e = min((p // 512 + 1) * 512, hi)
                    out.append((p, e))
                    p = e
                return out

            mask_gated = [False]
            pending_epi = [None]
            spill_idx = {}
            n_spilled = 0
            prev_half_guard = None

            for hp in range(8):
                hA, hB = 2 * hp, 2 * hp + 1
                for half in range(2):
                    pending_pv = []
                    LAG = 3
                    first_pv = True
                    jts = range(8) if half == 0 else range(16)
                    for jt in jts:
                        for (qs, qe) in half_stripes(jt, half):
                            w = qe - qs
                            pu = sidx // 2
                            slotA = (pu % 2) * 2
                            off = qs % 512
                            base = (pu % 4) * 1024
                            pbase = slotA * 512
                            psS_A = (psB, psC, psD, psE)[slotA]
                            psS_B = (psB, psC, psD, psE)[slotA + 1]
                            qkv = []
                            for head, po, psX in ((hA, 0, psS_A), (hB, 64, psS_B)):
                                waits = []
                                if head == hA and pu >= 2:
                                    waits.append(("sc", stripe_hist[pu - 2]["exp"]))
                                if sidx == 0:
                                    waits.append(("sc", V_K1))
                                    waits.append(("dv", V_V1))
                                    waits.append(("dv", V_ONES))
                                def qk(e, po=po, hp=hp, jt=jt, qs=qs, qe=qe, psX=psX, off=off, w=w):
                                    return e.matmul(psX[:, off:off + w],
                                                    krep[po:po + 64, jt * 128:(jt + 1) * 128],
                                                    qlat[po:po + 64, hp, qs:qe],
                                                    start=True, stop=True, skip_group_check=True)
                                qkv.append(P.add("pe", qk, waits=waits, inc=("pe", 1)))
                                sidx += 1
                            ew = [("pe", qkv[1])]
                            if pu >= 4:
                                assert stripe_hist[pu - 4]["pv"] is not None
                                ew.append(("pe", stripe_hist[pu - 4]["pv"]))
                            ew_w = 1024 - off
                            vexp = P.add("sc",
                                         lambda e, base=base, off=off, ew_w=ew_w, pbase=pbase: e.activation(
                                             ptS[:, base + off:base + off + ew_w],
                                             psS4[:, pbase + off:pbase + off + ew_w], AF.Exp),
                                         waits=ew, inc=("sc", 1))
                            if pending_epi[0] is not None:
                                pending_epi[0]()
                                pending_epi[0] = None
                            vmask = None
                            if qs == jt * 128:
                                mw0 = [("sc", vexp)]
                                if not mask_gated[0]:
                                    mw0.append(("dI", V_CONST))
                                    mask_gated[0] = True
                                P.add("gp",
                                      lambda e, base=base, off=off: e.tensor_tensor(
                                          ptS[:, base + off:base + off + 128],
                                          ptS[:, base + off:base + off + 128],
                                          mask01, op=ALU.mult),
                                      waits=mw0)
                                vmask = P.add("gp",
                                              lambda e, base=base, off=off: e.tensor_tensor(
                                                  ptS[:, base + 512 + off:base + 512 + off + 128],
                                                  ptS[:, base + 512 + off:base + 512 + off + 128],
                                                  mask01, op=ALU.mult),
                                              waits=[("sc", vexp)], inc=("gp", 1))
                            pw = [("sc", vexp)]
                            if vmask is not None:
                                pw.append(("gp", vmask))
                            if first_pv and prev_half_guard is not None:
                                pw.extend(prev_half_guard)
                            first_pv = False
                            def pvA(e, jt=jt, qs=qs, w=w, base=base, off=off, half=half):
                                return e.matmul(psA[:, qs - half * 1024:qs - half * 1024 + w],
                                                v1[:, jt, :], ptS[:, base + off:base + off + w],
                                                start=(jt == 0), stop=False, skip_group_check=True)
                            def pvB(e, jt=jt, qs=qs, w=w, base=base, off=off, half=half):
                                return e.matmul(psA[:, 1024 + qs - half * 1024:1024 + qs - half * 1024 + w],
                                                v1[:, jt, :], ptS[:, base + 512 + off:base + 512 + off + w],
                                                start=(jt == 0), stop=False, skip_group_check=True)
                            hist = dict(exp=vexp, pv=None)
                            stripe_hist.append(hist)
                            pending_pv.append((pvA, pw, None))
                            pending_pv.append((pvB, [], hist))
                            while len(pending_pv) > 2 * LAG:
                                assert pending_epi[0] is None
                                fn, fw, fh = pending_pv.pop(0)
                                vv = P.add("pe", fn, waits=fw, inc=("pe", 1))
                                if fh is not None:
                                    fh["pv"] = vv
                        if half == 0 and jt == 3:
                            emit_prep_slice(1)
                        if half == 1 and jt in (3, 9):
                            emit_prep_slice(1)
                    while pending_pv:
                        fn, fw, fh = pending_pv.pop(0)
                        vv = P.add("pe", fn, waits=fw, inc=("pe", 1))
                        if fh is not None:
                            fh["pv"] = vv
                    vpv_last = stripe_hist[-1]["pv"]
                    def run_epilogue(hA=hA, hB=hB, half=half, vpv_last=vpv_last):
                        nonlocal prev_half_guard, n_spilled
                        guard = []
                        vexp_h = {}
                        for head, cb in ((hA, 0), (hB, 1024)):
                            lnw = [("pe", vpv_last)] if cb == 0 else []
                            if prev_rs_read[cb] is not None:
                                lnw.append(("sc", prev_rs_read[cb]))
                            vexp_h[cb] = P.add("sc",
                                  lambda e, cb=cb: e.activation(rs[cb // 16:cb // 16 + 64, :],
                                                                psA[64:128, cb:cb + 1024], AF.Ln),
                                  waits=lnw, inc=("sc", 1))
                        for head, cb in ((hA, 0), (hB, 1024)):
                            ew2 = [("sc", vexp_h[cb])]
                            if prev_srecip_mult[cb] is not None:
                                ew2.append(("dv", prev_srecip_mult[cb]))
                            vexp_h[cb] = prev_rs_read[cb] = P.add("sc",
                                        lambda e, cb=cb: e.activation(rs2[cb // 16:cb // 16 + 64, :],
                                                                      rs[cb // 16:cb // 16 + 64, :],
                                                                      AF.Exp, scale=-1.0),
                                        waits=ew2, inc=("sc", 1))
                        for head, cb in ((hA, 0), (hB, 1024)):
                            vln = vexp_h[cb]
                            mw = [("sc", vln)]
                            if (head, half, "prevspill") in spill_idx:
                                csem = "dCW" if head % 2 == 0 else "dCW1"
                                mw.append((csem, spill_idx[(head, half, "prevspill")]))
                            def mult(e, cb=cb, head=head):
                                return e.tensor_tensor(
                                    ctxn[0:64, (head % 2) * 1024:(head % 2) * 1024 + 1024],
                                    psA[0:64, cb:cb + 1024],
                                    rs2[cb // 16:cb // 16 + 64, :], op=ALU.mult)
                            vm = P.add("dv", mult, waits=mw, inc=("dv", 1))
                            prev_srecip_mult[cb] = vm
                            guard.append(("dv", vm))
                            guard.append(("sc", vln))
                            n_spilled += 1
                            vsp = P.add("sy",
                                        lambda e, head=head, half=half: e.dma_start(
                                            d_ctxs[head * 64:(head + 1) * 64, half * 1024:half * 1024 + 1024],
                                            ctxn[0:64, (head % 2) * 1024:(head % 2) * 1024 + 1024]),
                                        waits=[("dv", vm)],
                                        inc=("dCW" if head % 2 == 0 else "dCW1", 16))
                            nxt = (head, half + 1) if half == 0 else (head + 2, 0)
                            spill_idx[(nxt[0], nxt[1], "prevspill")] = vsp
                        prev_half_guard = guard
                    pending_epi[0] = run_epilogue
                emit_prep_slice(1)

        if phase >= 7:
            if pending_epi[0] is not None:
                pending_epi[0]()
                pending_epi[0] = None
            emit_prep_slice(40)
            V_A1 = PR["V_A1"]
            V_A2 = PR["V_A2"]
            V_S1 = v_s1sign[7]
            V_SPILLS = 16 * 8

        if phase >= 8:
            v_outproj = {}
            v_res1 = {}
            v_epi = [0] * NB
            v_ffn2_grp = {}
            v_sums1 = [0] * NB
            v_sums2 = [0] * NB
            v_strips1 = [0] * NB
            v_strips2 = [0] * NB
            v_bcast1 = [0] * NB
            v_bcast2 = [0] * NB
            v_apply2 = [0] * NB
            v_apply1d3 = [0] * NB
            v_y2out = [0] * NB
            v_vr2b = [0] * NB
            v_vr2 = [0] * NB
            v_vr2_mc = {}
            v_ffn2_last = {}
            v_epi_mc = {}
            v_y1bc = {}
            v_r2sq = [0] * NB
            v_xsq = [0] * NB
            v_cin = [0] * (NB + 1)
            v_xin = [0] * (NB + 1)
            v_s2rd = {}
            v_gelu_hist = {}
            NG = 16

            def strip(bank, rg):
                o = bank * 1536
                return strips[:, o + rg[0]:o + rg[1]]

            def strip_i(bank, rg):
                return strip(bank, rg).bitcast(I32)

            def emit_strips(bank, vcopy_val):
                va = P.add("dv",
                           lambda e: e.tensor_tensor(strip(bank, S_T), strip(bank, S_MEAN),
                                                     strip(bank, S_MEAN), op=ALU.mult),
                           waits=[("sc", vcopy_val)], inc=("dv", 1))
                vb = P.add("dv",
                           lambda e: e.scalar_tensor_tensor(strip(bank, S_VAR), strip(bank, S_EX2),
                                                            EPS, strip(bank, S_T),
                                                            op0=ALU.add, op1=ALU.subtract),
                           waits=[("dv", va)], inc=("dv", 1))
                vc1 = P.add("sc",
                            lambda e: e.activation(strip(bank, S_T), strip(bank, S_VAR),
                                                   AF.Ln),
                            waits=[("dv", vb)], inc=("sc", 1))
                vc2 = P.add("sc",
                            lambda e: e.activation(strip(bank, S_Y), strip(bank, S_T),
                                                   AF.Exp, scale=-0.5),
                            waits=[("sc", vc1)], inc=("sc", 1))
                vd = P.add("dv",
                           lambda e: e.scalar_tensor_tensor(strip(bank, S_NMR), strip(bank, S_MEAN),
                                                            -1.0, strip(bank, S_Y),
                                                            op0=ALU.mult, op1=ALU.mult),
                           waits=[("sc", vc2)], inc=("dv", 1))
                return vd

            def emit_outproj_mc(bb, mc):
                waits = []
                if mc == 0:
                    waits.append(("dCR", v_cin[bb]))
                    waits.append(("sc", v_evac_vcomb[7]))
                    if bb >= 1:
                        waits.append(("dv", v_res1[(bb - 1, 7)]))
                        waits.append(("dv", v_apply2[bb - 2]) if bb >= 2 else ("dv", v_res1[(bb - 1, 7)]))
                else:
                    waits.append(("dv", v_res1[(bb, mc - 1)]))
                for kc in range(8):
                    def mm(e, mc=mc, kc=kc):
                        return e.matmul(psE[:, 0:256],
                                        vcomb[:, kc, mc * 128:(mc + 1) * 128],
                                        cin[:, kc, :],
                                        start=(kc == 0), stop=(kc == 7))
                    P.add("pe", mm, waits=waits if kc == 0 else (),
                          inc=("pe", 1) if kc == 7 else None)
                v_outproj[(bb, mc)] = P.val("pe")
                rw = [("pe", v_outproj[(bb, mc)]), ("dXR", v_xin[bb])]
                if mc == 0 and bb >= 2:
                    rw.append(("gp", v_r2sq[bb - 2]))
                    rw.append(("pe", v_sums2[bb - 2]))
                def r1(e, bb=bb, mc=mc):
                    return e.tensor_tensor(xresb[:, mc, :], psE[:, 0:256],
                                           xin[bb % 2][:, mc, :], op=ALU.add)
                v_res1[(bb, mc)] = P.add("dv", r1, waits=rw, inc=("dv", 1))

            def emit_outproj_tail(bb):
                xw = [("dv", v_res1[(bb, 7)])]
                if bb >= 2:
                    xw.append(("pe", v_sums2[bb - 2]))
                v_xsq[bb] = P.add("gp",
                                  lambda e: e.tensor_tensor(
                                      xsq.rearrange("p c t -> p (c t)"),
                                      xresb.rearrange("p c t -> p (c t)"),
                                      xresb.rearrange("p c t -> p (c t)"), op=ALU.mult),
                                  waits=xw, inc=("gp", 1))

            def emit_outproj(bb):
                for mc in range(8):
                    emit_outproj_mc(bb, mc)
                emit_outproj_tail(bb)

            def emit_sums1(bb):
                sw = [("dv", v_res1[(bb, 7)]), ("dI", V_CONST)]
                if bb >= 2:
                    sw.append(("dv", v_apply2[bb - 2]))
                for mc in range(8):
                    P.add("pe",
                          lambda e, mc=mc: e.matmul(psE[0:1, 0:256], onesb, xresb[:, mc, :],
                                                    start=(mc == 0), stop=(mc == 7)),
                          waits=sw if mc == 0 else ())
                for mc in range(8):
                    P.add("pe",
                          lambda e, mc=mc: e.matmul(psE[0:1, 256:512], onesb, xsq[:, mc, :],
                                                    start=(mc == 0), stop=(mc == 7)),
                          waits=[("gp", v_xsq[bb])] if mc == 0 else (),
                          inc=("pe", 1) if mc == 7 else None)
                v_sums1[bb] = P.val("pe")
                vcopy = P.add("sc",
                              lambda e: e.activation(strip(0, (0, 512)), psE[0:1, 0:512],
                                                     AF.Copy, scale=1.0 / E),
                              waits=[("pe", v_sums1[bb])] + ([("dv", PR["V_A1"])] if bb == 0 else []),
                              inc=("sc", 1))
                v_strips1[bb] = emit_strips(0, vcopy)

            def emit_bcast1(bb):
                P.add("pe", lambda e: e.matmul(psE[:, 0:256], onesf, strip(0, S_Y), start=True, stop=True),
                      waits=[("dv", v_strips1[bb])])
                P.add("pe", lambda e: e.matmul(psE[:, 256:512], onesf, strip(0, S_NMR), start=True, stop=True),
                      inc=("pe", 1))
                v_bcast1[bb] = P.val("pe")

            def emit_apply1(bb):
                yf = y1fb[bb % 2]
                vd1 = {}
                vd2 = {}
                def emit_d1(mc):
                    d1w = [("dv", v_res1[(bb, mc)])]
                    if mc == 0:
                        d1w.append(("pe", v_bcast1[bb]))
                    if bb >= 2:
                        d1w.append(("dv", v_vr2_mc[(bb - 2, mc)]))
                    vd1[mc] = P.add("dv",
                                    lambda e, mc=mc: e.tensor_tensor(yf[:, mc, :], xresb[:, mc, :],
                                                                     psE[:, 0:256], op=ALU.mult),
                                    waits=d1w, inc=("dv", 1))
                def emit_d2(mc):
                    vd2[mc] = P.add("dv",
                                    lambda e, mc=mc: e.tensor_tensor(yf[:, mc, :], yf[:, mc, :],
                                                                     psE[:, 256:512], op=ALU.add),
                                    waits=[("dv", vd1[mc])], inc=("dv", 1))
                    v_y1bc[(bb, mc)] = P.add("sc",
                                             lambda e, mc=mc, bb=bb: e.copy(y1b[bb % 2][:, mc, :],
                                                                            yf[:, mc, :]),
                                             waits=[("dv", vd2[mc])], inc=("sc", 1))
                emit_d1(0)
                for mc in range(1, 8):
                    emit_d1(mc)
                    emit_d2(mc - 1)
                emit_d2(7)
                v_apply1d3[bb] = v_y1bc[(bb, 7)]
                if debug:
                    P.add("sy", lambda e, bb=bb: e.dma_start(d_ty1[:, bb, :, :], y1fb[bb % 2]),
                          waits=[("sc", v_apply1d3[bb])], inc=("dO", 16))

            def emit_res2part1(bb):
                for mc in range(8):
                    def ep(e, mc=mc):
                        return e.activation(ffnout[:, mc, :], psA[:, mc * 256:mc * 256 + 256],
                                            AF.Identity, bias=b2c[:, mc:mc + 1], scale=a2c[:, mc:mc + 1])
                    v_epi_mc[(bb, mc)] = P.add(
                        "sc", ep,
                        waits=[("pe", v_ffn2_last[(bb, mc)])] + ([("dv", PR["V_A2"])] if mc == 0 else []),
                        inc=("sc", 1))
                v_epi[bb] = v_epi_mc[(bb, 7)]
                if debug:
                    P.add("sy", lambda e, bb=bb: e.dma_start(d_tffn[:, bb, :, :], ffnout),
                          waits=[("sc", v_epi[bb])], inc=("dO", 16))
                for mc in range(8):
                    v_vr2_mc[(bb, mc)] = P.add(
                        "dv",
                        lambda e, mc=mc, bb=bb: e.tensor_tensor(res2f[:, mc, :], ffnout[:, mc, :],
                                                                y1fb[bb % 2][:, mc, :], op=ALU.add),
                        waits=[("sc", v_epi_mc[(bb, mc)])], inc=("dv", 1))
                v_vr2[bb] = v_vr2_mc[(bb, 7)]

            def emit_res2part2(bb):
                nxt = bb + 1 if bb + 1 < NB else bb
                v_vr2b[bb] = P.add("dv",
                                   lambda e: e.tensor_copy(xresb.rearrange("p c t -> p (c t)"),
                                                           res2f.rearrange("p c t -> p (c t)")),
                                   waits=[("pe", v_sums1[nxt]), ("gp", v_xsq[nxt]), ("dv", v_vr2[bb])],
                                   inc=("dv", 1))
                v_r2sq[bb] = P.add("gp",
                                   lambda e: e.tensor_tensor(
                                       xsq.rearrange("p c t -> p (c t)"),
                                       xresb.rearrange("p c t -> p (c t)"),
                                       xresb.rearrange("p c t -> p (c t)"), op=ALU.mult),
                                   waits=[("dv", v_vr2b[bb]), ("pe", v_sums1[nxt])], inc=("gp", 1))

            def emit_sums2(bb):
                sw = [("dv", v_vr2b[bb]), ("sc", v_apply1d3[bb + 1] if bb + 1 < NB else v_apply1d3[bb])]
                for mc in range(8):
                    P.add("pe",
                          lambda e, mc=mc: e.matmul(psE[0:1, 0:256], onesb, xresb[:, mc, :],
                                                    start=(mc == 0), stop=(mc == 7)),
                          waits=sw if mc == 0 else ())
                for mc in range(8):
                    P.add("pe",
                          lambda e, mc=mc: e.matmul(psE[0:1, 256:512], onesb, xsq[:, mc, :],
                                                    start=(mc == 0), stop=(mc == 7)),
                          waits=[("gp", v_r2sq[bb])] if mc == 0 else (),
                          inc=("pe", 1) if mc == 7 else None)
                v_sums2[bb] = P.val("pe")
                vcopy = P.add("sc",
                              lambda e: e.activation(strip(1, (0, 512)), psE[0:1, 0:512],
                                                     AF.Copy, scale=1.0 / E),
                              waits=[("pe", v_sums2[bb])], inc=("sc", 1))
                v_strips2[bb] = emit_strips(1, vcopy)

            def emit_bcast2(bb):
                P.add("pe", lambda e: e.matmul(psE[:, 0:256], onesf, strip(1, S_Y), start=True, stop=True),
                      waits=[("dv", v_strips2[bb])])
                P.add("pe", lambda e: e.matmul(psE[:, 256:512], onesf, strip(1, S_NMR), start=True, stop=True),
                      inc=("pe", 1))
                v_bcast2[bb] = P.val("pe")

            def emit_apply2(bb):
                bca = psE[:, 0:256].unsqueeze(1).broadcast_to([128, 8, 256])
                bcb = psE[:, 256:512].unsqueeze(1).broadcast_to([128, 8, 256])
                a2w = [("pe", v_bcast2[bb]), ("dv", v_vr2[bb])]
                if bb >= 1:
                    a2w.append(("dO", v_y2out[bb - 1]))
                va1 = P.add("dv", lambda e, bca=bca: e.tensor_tensor(y2, res2f, bca, op=ALU.mult),
                            waits=a2w, inc=("dv", 1))
                v_apply2[bb] = P.add("dv", lambda e, bcb=bcb: e.tensor_tensor(y2, y2, bcb, op=ALU.add),
                                     waits=[("dv", va1)], inc=("dv", 1))
                v_y2out[bb] = P.add("sy",
                                    lambda e, bb=bb: e.dma_start(
                                        d_y.rearrange("(c p) t -> p c t", p=128)[:, :, bb * TB:bb * TB + TB], y2),
                                    waits=[("dv", v_apply2[bb])], inc=("dO", 16))

            def emit_ffn_steps(bb, j_lo, j_hi):
                for j in range(j_lo, j_hi):
                    if j < NJ:
                        g = j // 2
                        ps = (psB, psC, psD)[j % 3]
                        w1w = []
                        if j == 0:
                            w1w.append(("dv", V_S1))
                        if j <= 2 and bb >= 1:
                            w1w.append(("sc", v_gelu_hist[{0: 30, 1: 31, 2: 29}[j]]))
                        if j >= 3:
                            w1w.append(("sc", v_gelu_hist[j - 3]))
                        for kc in range(8):
                            kw = list(w1w) if kc == 0 else []
                            if j == 0:
                                kw.append(("sc", v_y1bc[(bb, kc)]))
                            def mm(e, j=j, kc=kc, ps=ps, bb=bb):
                                return e.matmul(ps[:, 0:256],
                                                s1T[:, kc, j * 128:(j + 1) * 128],
                                                y1b[bb % 2][:, kc, :],
                                                start=(kc == 0), stop=(kc == 7))
                            P.add("pe", mm, waits=kw,
                                  inc=("pe", 1) if kc == 7 else None)
                        vf1 = P.val("pe")
                        gw0 = [("pe", vf1), ("dv", PR["V_A1"])]
                        if bb == 0 and j == 0:
                            gw0.append(("dI", V_CONST))
                        vg = P.add("sc",
                                   lambda e, j=j, ps=ps: e.activation(
                                       hb[:, j % 4, :], ps[:, 0:256], AF.Gelu,
                                       bias=b1c[:, j:j + 1], scale=a1c[:, j:j + 1]),
                                   waits=gw0, inc=("sc", 1))
                        v_gelu_hist[j] = vg
                    jj = j - 2
                    if jj >= 0 and jj < NJ:
                        g2 = jj // 2
                        s2sem = "dS2" if g2 % 2 == 0 else "dS2b"
                        f2w = [("sc", v_gelu_hist[jj])]
                        if jj == 0:
                            f2w.append((s2sem, v_s2rd[(bb, g2)]))
                            if bb > 0:
                                f2w.append(("sc", v_epi[bb - 1]))
                            f2w.append(("dv", PR["V_A2"]))
                        elif jj % 2 == 0:
                            f2w.append((s2sem, v_s2rd[(bb, g2)]))
                        for mc in range(8):
                            def mm2(e, jj=jj, mc=mc, g2=g2, bb=bb):
                                return e.matmul(psA[:, mc * 256:mc * 256 + 256],
                                                s2b[:, g2 % 2, jj % 2, mc * 128:(mc + 1) * 128],
                                                hb[:, jj % 4, :],
                                                start=(jj == 0 and mc % 2 == 0),
                                                stop=(jj == NJ - 1), skip_group_check=True)
                            last = (jj == NJ - 1)
                            P.add("pe", mm2, waits=f2w if mc == 0 else (),
                                  inc=("pe", 1) if (mc == 7 or last) else None)
                            if last:
                                v_ffn2_last[(bb, mc)] = P.val("pe")
                        if jj % 2 == 1:
                            gd = jj // 2
                            v_ffn2_grp[(bb, gd)] = P.val("pe")
                            gg, b2 = gd + 2, bb
                            if gg >= NG:
                                gg -= NG
                                b2 += 1
                            if b2 < NB and (b2, gg) not in v_s2rd:
                                v_s2rd[(b2, gg)] = P.add(
                                    "sy",
                                    lambda e, b2=b2, gg=gg: e.dma_start(
                                        s2b[:, gg % 2, :, :],
                                        d_s2d[gg * 256:(gg + 1) * 256, :].rearrange("(j p) e -> p j e", p=128)),
                                    waits=[("pe", v_ffn2_grp[(bb, gd)])],
                                    inc=("dS2" if gg % 2 == 0 else "dS2b", 16))

            v_cin[0] = P.add("sy",
                             lambda e: e.dma_start(cin, d_ctxs.rearrange("(c p) t -> p c t", p=128)[:, :, 0:TB]),
                             waits=[("dCW", 16 * 15), ("dCW1", 16 * 15)], inc=("dCR", 16))
            v_xin[0] = P.add("sy",
                             lambda e: e.dma_start(xin[0], d_x.rearrange("(c p) t -> p c t", p=128)[:, :, 0:TB]),
                             waits=[("dCW", 16 * H), ("dCW1", 16 * H)], inc=("dXR", 16))
            for g in range(2):
                v_s2rd[(0, g)] = P.add("sy",
                                       lambda e, g=g: e.dma_start(
                                           s2b[:, g % 2, :, :],
                                           d_s2d[g * 256:(g + 1) * 256, :].rearrange("(j p) e -> p j e", p=128)),
                                       waits=[("dSP", V_SPILLS)],
                                       inc=("dS2" if g % 2 == 0 else "dS2b", 16))

            emit_outproj(0)
            v_cin[1] = P.add("sy",
                             lambda e: e.dma_start(cin, d_ctxs.rearrange("(c p) t -> p c t", p=128)[:, :, TB:2 * TB]),
                             waits=[("pe", v_outproj[(0, 7)])], inc=("dCR", 16))
            v_xin[1] = P.add("sy",
                             lambda e: e.dma_start(xin[1], d_x.rearrange("(c p) t -> p c t", p=128)[:, :, TB:2 * TB]),
                             waits=[("dv", v_res1[(0, 7)])], inc=("dXR", 16))
            emit_sums1(0)
            emit_bcast1(0)
            emit_apply1(0)

            for b in range(NB):
                emit_ffn_steps(b, 0, 8)
                if b >= 1:
                    emit_sums2(b - 1)
                emit_ffn_steps(b, 8, 12)
                if b >= 1:
                    emit_bcast2(b - 1)
                emit_ffn_steps(b, 12, 13)
                if b >= 1:
                    emit_apply2(b - 1)
                if b + 1 < NB:
                    for mc in range(8):
                        emit_outproj_mc(b + 1, mc)
                        emit_ffn_steps(b, 13 + mc, 14 + mc)
                    emit_outproj_tail(b + 1)
                    if b + 2 < NB:
                        v_cin[b + 2] = P.add("sy",
                                             lambda e, b=b: e.dma_start(
                                                 cin, d_ctxs.rearrange("(c p) t -> p c t", p=128)[:, :, (b + 2) * TB:(b + 3) * TB]),
                                             waits=[("pe", v_outproj[(b + 1, 7)])], inc=("dCR", 16))
                        v_xin[b + 2] = P.add("sy",
                                             lambda e, b=b: e.dma_start(
                                                 xin[(b + 2) % 2],
                                                 d_x.rearrange("(c p) t -> p c t", p=128)[:, :, (b + 2) * TB:(b + 3) * TB]),
                                             waits=[("dv", v_res1[(b + 1, 7)])], inc=("dXR", 16))
                else:
                    emit_ffn_steps(b, 13, 21)
                emit_ffn_steps(b, 21, 24)
                if b + 1 < NB:
                    emit_sums1(b + 1)
                emit_ffn_steps(b, 24, 27)
                if b + 1 < NB:
                    emit_bcast1(b + 1)
                emit_ffn_steps(b, 27, NJ + 2)
                if b + 1 < NB:
                    emit_apply1(b + 1)
                emit_res2part1(b)
                emit_res2part2(b)

            emit_sums2(NB - 1)
            emit_bcast2(NB - 1)
            emit_apply2(NB - 1)

        P.add("sy", lambda e: None, waits=[("dO", P.val("dO"))])

        def emit(eng_key):
            def run(eng):
                for waits, fn, inc in P.items[eng_key]:
                    for s, v in waits:
                        if v is None:
                            continue
                        eng.wait_ge(SEM[s], v)
                    ins = fn(eng)
                    if inc is not None:
                        assert ins is not None
                        ins.then_inc(SEM[inc[0]], inc[1])
            return run

        @block.sync
        def _(eng):
            emit("sy")(eng)

        @block.gpsimd
        def _(eng):
            emit("gp")(eng)

        @block.tensor
        def _(eng):
            emit("pe")(eng)

        @block.vector
        def _(eng):
            emit("dv")(eng)

        @block.scalar
        def _(eng):
            emit("sc")(eng)

    return nc


def _prep_shared(inp):
    bf = ml_dtypes.bfloat16
    wq = np.ascontiguousarray(inp["L0_attn_Wq"], dtype=np.float32)
    q2l = np.ascontiguousarray(
        inp["L0_attn_q_to_latent"].reshape(8, 2, 64, 64).transpose(1, 2, 0, 3).reshape(128, 512),
        dtype=np.float32)
    wkvT = np.ascontiguousarray(
        np.concatenate([inp["L0_attn_Wk_down"], inp["L0_attn_Wv_down"]], axis=0).T, dtype=np.float32)
    woutT = np.ascontiguousarray(inp["L0_attn_Wout"].T, dtype=np.float32)
    vupT = np.ascontiguousarray(
        inp["L0_attn_v_up"].reshape(8, 2, 64, 64).transpose(1, 3, 0, 2).reshape(128, 512),
        dtype=np.float32)
    w1 = np.ascontiguousarray(inp["L0_ffn_w1"], dtype=np.float32)
    w2 = np.ascontiguousarray(inp["L0_ffn_w2"], dtype=np.float32)
    s = np.arange(128)
    shared = dict(
        wq=wq,
        q2l=q2l,
        wkvT=wkvT,
        woutT=woutT,
        vupT=vupT,
        w1T=np.ascontiguousarray(w1.T),
        w2T=np.ascontiguousarray(w2.T),
        w1n=w1,
        w2n=w2,
        b1c=np.ascontiguousarray(inp["L0_ffn_b1"].reshape(32, 128).T, dtype=np.float32),
        b2c=np.ascontiguousarray(inp["L0_ffn_b2"].reshape(8, 128).T, dtype=np.float32),
        mask01=np.ascontiguousarray((s[None, :] >= s[:, None]).astype(bf)),
        ident=np.eye(128, dtype=bf),
        onesb=np.ones((128, 1), dtype=bf),
        onesf=np.ones((1, 128), dtype=np.float32),
    )
    return shared


def kernel(**inputs):
    x = np.asarray(inputs["L0_attn_hidden_states"], dtype=np.float32)
    shared = _prep_shared(inputs)
    nc = build(debug=False)
    in_maps = []
    for i in range(NCORES):
        m = dict(shared)
        m["x_fm"] = np.ascontiguousarray(x[i].T)
        in_maps.append(m)
    res = run_bass_kernel_spmd(nc, in_maps, core_ids=list(range(NCORES)))
    out = np.empty((NCORES, T, E), dtype=np.float32)
    for i in range(NCORES):
        out[i] = res.results[i]["y_fm"].T
    return out


if __name__ == "__main__":
    nc = build(debug=False)
    print("instructions:", nc.n_instructions())


# version 26 (final)
# speedup vs baseline: 1.0481x; 1.0215x over previous
import numpy as np
import ml_dtypes

import concourse.bass as bass
import concourse.mybir as mybir
from concourse.bass_utils import run_bass_kernel_spmd

F32 = mybir.dt.float32
BF16 = mybir.dt.bfloat16
I32 = mybir.dt.int32
AF = mybir.ActivationFunctionType
ALU = mybir.AluOpType
AX = mybir.AxisListType

NCORES = 8
T = 2048
E = 1024
H = 16
L = 64
HF = 4096
TB = 256
NB = T // TB
NJ = HF // 128
EPS = 1e-5
RSQRT_MAGIC = 0x5F3759DF

ENGS = ("sy", "gp", "pe", "dv", "sc")


class Plan:

    def __init__(self):
        self.items = {e: [] for e in ENGS}
        self.c = {}

    def val(self, sem):
        return self.c.get(sem, 0)

    def add(self, eng, fn, waits=(), inc=None):
        tgt = None
        if inc is not None:
            s, a = inc
            self.c[s] = self.c.get(s, 0) + a
            tgt = self.c[s]
        self.items[eng].append((tuple(waits), fn, inc))
        return tgt


def _stripes(jt):
    out = []
    p = jt * 128
    while p < T:
        e = min((p // 512 + 1) * 512, T)
        out.append((p, e))
        p = e
    return out


def build(debug=False, phase=8, bstage=7):
    nc = bass.Bass("TRN2", target_bir_lowering=False)

    d_x = nc.dram_tensor("x_fm", [E, T], F32, kind="ExternalInput").ap()
    d_wq = nc.dram_tensor("wq", [E, E], F32, kind="ExternalInput").ap()
    d_q2l = nc.dram_tensor("q2l", [128, 512], F32, kind="ExternalInput").ap()
    d_wkv = nc.dram_tensor("wkvT", [E, 128], F32, kind="ExternalInput").ap()
    d_wout = nc.dram_tensor("woutT", [E, E], F32, kind="ExternalInput").ap()
    d_vup = nc.dram_tensor("vupT", [128, 512], F32, kind="ExternalInput").ap()
    d_w1t = nc.dram_tensor("w1T", [E, HF], F32, kind="ExternalInput").ap()
    d_w2t = nc.dram_tensor("w2T", [HF, E], F32, kind="ExternalInput").ap()
    d_w1n = nc.dram_tensor("w1n", [HF, E], F32, kind="ExternalInput").ap()
    d_w2n = nc.dram_tensor("w2n", [E, HF], F32, kind="ExternalInput").ap()
    d_b1 = nc.dram_tensor("b1c", [128, 32], F32, kind="ExternalInput").ap()
    d_b2 = nc.dram_tensor("b2c", [128, 8], F32, kind="ExternalInput").ap()
    d_mask = nc.dram_tensor("mask01", [128, 128], BF16, kind="ExternalInput").ap()
    d_ident = nc.dram_tensor("ident", [128, 128], BF16, kind="ExternalInput").ap()
    d_onesb = nc.dram_tensor("onesb", [128, 1], BF16, kind="ExternalInput").ap()
    d_onesf = nc.dram_tensor("onesf", [1, 128], F32, kind="ExternalInput").ap()

    import os as _os
    spy = bool(_os.environ.get("KSPY"))
    d_y = nc.dram_tensor("y_fm", [E, T], F32, kind="ExternalOutput").ap()
    scratch_kind = "ExternalOutput" if (debug or spy) else "Internal"
    d_ctxs = nc.dram_tensor("ctxs", [E, T], BF16, kind=scratch_kind).ap()
    d_s2d = nc.dram_tensor("s2d", [HF, E], BF16, kind=scratch_kind).ap()
    d_rs_spy = nc.dram_tensor("rs_spy", [H, 2, 2, 64, 1024], F32, kind=scratch_kind).ap() if spy else None
    d_krep_spy = nc.dram_tensor("krep_spy", [128, T], BF16, kind=scratch_kind).ap() if spy else None
    d_qlat_spy = nc.dram_tensor("qlat_spy", [128, 8, T], BF16, kind=scratch_kind).ap() if spy else None
    d_v1_spy = nc.dram_tensor("v1_spy", [128, 16 * 128], BF16, kind=scratch_kind).ap() if spy else None
    if debug:
        d_tqlat = nc.dram_tensor("t_qlat", [128, 8, T], BF16, kind="ExternalOutput").ap()
        d_tkrep = nc.dram_tensor("t_krep", [128, T], BF16, kind="ExternalOutput").ap()
        d_tv1 = nc.dram_tensor("t_v1", [128, 16 * 128], BF16, kind="ExternalOutput").ap()
        d_ty1 = nc.dram_tensor("t_y1", [128, NB, 8, TB], F32, kind="ExternalOutput").ap()
        d_tffn = nc.dram_tensor("t_ffn", [128, NB, 8, TB], F32, kind="ExternalOutput").ap()
        d_ta1 = nc.dram_tensor("t_a1", [128, 32], F32, kind="ExternalOutput").ap()

    from contextlib import ExitStack
    with ExitStack() as ctx:
        s1T_h = ctx.enter_context(nc.sbuf_tensor("s1T", [128, 8, HF], BF16))
        s2b_h = ctx.enter_context(nc.sbuf_tensor("s2b", [128, 2, 2, E], BF16))
        vcomb_h = ctx.enter_context(nc.sbuf_tensor("vcomb", [128, 8, E], BF16))
        ar1_h = ctx.enter_context(nc.sbuf_tensor("arena1", [128, 32768], mybir.dt.uint8))
        ar2_h = ctx.enter_context(nc.sbuf_tensor("arena2", [128, 32768], mybir.dt.uint8))
        ar4_h = ctx.enter_context(nc.sbuf_tensor("arena4", [128, 16384], mybir.dt.uint8))
        v1_h = ctx.enter_context(nc.sbuf_tensor("v1", [128, 16, 128], BF16))
        krep_h = ctx.enter_context(nc.sbuf_tensor("krep", [128, T], BF16))
        q2lb_h = ctx.enter_context(nc.sbuf_tensor("q2lb", [128, 512], BF16))
        vupb_h = ctx.enter_context(nc.sbuf_tensor("vupb", [128, 512], BF16))
        wkvb_h = ctx.enter_context(nc.sbuf_tensor("wkvb", [128, 8, 128], BF16))

        rs_h = ctx.enter_context(nc.sbuf_tensor("rs", [128, 1024], F32))
        rs2_h = ctx.enter_context(nc.sbuf_tensor("rs2", [128, 1024], F32))
        ctxn_h = ctx.enter_context(nc.sbuf_tensor("ctxn", [64, T], BF16))
        a1c_h = ctx.enter_context(nc.sbuf_tensor("a1c", [128, 32], F32))
        a2c_h = ctx.enter_context(nc.sbuf_tensor("a2c", [128, 8], F32))
        b1c_h = ctx.enter_context(nc.sbuf_tensor("b1c_s", [128, 32], F32))
        b2c_h = ctx.enter_context(nc.sbuf_tensor("b2c_s", [128, 8], F32))
        mask_h = ctx.enter_context(nc.sbuf_tensor("mask01_s", [128, 128], BF16))
        ident_h = ctx.enter_context(nc.sbuf_tensor("ident_s", [128, 128], BF16))
        onesb_h = ctx.enter_context(nc.sbuf_tensor("onesb_s", [128, 1], BF16))
        onesf_h = ctx.enter_context(nc.sbuf_tensor("onesf_s", [1, 128], F32))
        hb_h = ctx.enter_context(nc.sbuf_tensor("hb", [128, 4, 256], BF16))
        psA_h = ctx.enter_context(nc.psum_tensor("psA", [128, 2048], F32))
        psS4_h = ctx.enter_context(nc.psum_tensor("psS4", [128, 2048], F32))
        S_dW = ctx.enter_context(nc.semaphore("dW"))
        S_qA = ctx.enter_context(nc.semaphore("qA"))
        S_qB = ctx.enter_context(nc.semaphore("qB"))
        S_qC = ctx.enter_context(nc.semaphore("qC"))
        S_qD = ctx.enter_context(nc.semaphore("qD"))
        S_qE0 = ctx.enter_context(nc.semaphore("qE0"))
        S_qE1 = ctx.enter_context(nc.semaphore("qE1"))
        S_qF0 = ctx.enter_context(nc.semaphore("qF0"))
        S_qF1 = ctx.enter_context(nc.semaphore("qF1"))
        S_qG0 = ctx.enter_context(nc.semaphore("qG0"))
        S_qG1 = ctx.enter_context(nc.semaphore("qG1"))
        S_qH0 = ctx.enter_context(nc.semaphore("qH0"))
        S_qH1 = ctx.enter_context(nc.semaphore("qH1"))
        S_dS2b = ctx.enter_context(nc.semaphore("dS2b"))
        S_dI = ctx.enter_context(nc.semaphore("dI"))
        S_dCW = ctx.enter_context(nc.semaphore("dCW"))
        S_dCW1 = ctx.enter_context(nc.semaphore("dCW1"))
        S_dCR = ctx.enter_context(nc.semaphore("dCR"))
        S_dXR = ctx.enter_context(nc.semaphore("dXR"))
        S_dS2 = ctx.enter_context(nc.semaphore("dS2"))
        S_dSP = ctx.enter_context(nc.semaphore("dSP"))
        S_dO = ctx.enter_context(nc.semaphore("dO"))
        S_pe = ctx.enter_context(nc.semaphore("pe"))
        S_sc = ctx.enter_context(nc.semaphore("sc"))
        S_dv = ctx.enter_context(nc.semaphore("dv"))
        S_gp = ctx.enter_context(nc.semaphore("gp"))
        block = ctx.enter_context(nc.Block())
        SEM = dict(dW=S_dW, dI=S_dI, dCW=S_dCW, dCW1=S_dCW1, dCR=S_dCR, dXR=S_dXR, dS2=S_dS2,
                   dS2b=S_dS2b, dSP=S_dSP, dO=S_dO, pe=S_pe, sc=S_sc, dv=S_dv, gp=S_gp,
                   qA=S_qA, qB=S_qB, qC=S_qC, qD=S_qD, qE0=S_qE0, qE1=S_qE1,
                   qF0=S_qF0, qF1=S_qF1, qG0=S_qG0, qG1=S_qG1, qH0=S_qH0, qH1=S_qH1)

        s1T = s1T_h.ap()
        s1flat = s1T.rearrange("p a b -> p (a b)")
        wqv = s1flat[:, 0:8192].rearrange("p (c e) -> p c e", c=8)
        woutv = s1flat[:, 8192:16384].rearrange("p (c e) -> p c e", c=8)
        s2b = s2b_h.ap()
        s2stage = s2b.rearrange("p a b e -> p (a b e)")
        vcomb = vcomb_h.ap()

        ar1 = ar1_h.ap()
        qlat = ar1[:, :].bitcast(BF16).rearrange("p (c t) -> p c t", c=8)
        xin = [ar1[:, 0:8192].bitcast(F32).rearrange("p (c t) -> p c t", c=8),
               ar1[:, 8192:16384].bitcast(F32).rearrange("p (c t) -> p c t", c=8)]
        y1f = ar1[:, 16384:24576].bitcast(F32).rearrange("p (c t) -> p c t", c=8)
        res2f = ar1[:, 24576:32768].bitcast(F32).rearrange("p (c t) -> p c t", c=8)

        ar2 = ar2_h.ap()
        xb = ar2[:, :].bitcast(BF16).rearrange("p (c t) -> p c t", c=8)
        wstage = [ar2[:, 0:8192].bitcast(BF16),
                  ar2[:, 8192:16384].bitcast(BF16)]
        wnstage = [ar2[:, 16384:24576].bitcast(BF16).rearrange("p (r e) -> p r e", r=4),
                   ar2[:, 24576:32768].bitcast(BF16).rearrange("p (r e) -> p r e", r=4)]
        vfm = ar2[0:64, 0:4096].bitcast(BF16)
        y2 = ar2[:, 0:8192].bitcast(F32).rearrange("p (c t) -> p c t", c=8)
        ffnout = ar2[:, 8192:16384].bitcast(F32).rearrange("p (c t) -> p c t", c=8)
        strips = ar2[0:1, 16384:28672].bitcast(F32)
        S_MEAN = (0, 256)
        S_EX2 = (256, 512)
        S_VAR = (512, 768)
        S_Y = (768, 1024)
        S_T = (1024, 1280)
        S_NMR = (1280, 1536)

        def strip(rg):
            return strips[:, rg[0]:rg[1]]

        def strip_i(rg):
            return strips[:, rg[0]:rg[1]].bitcast(I32)

        ar4 = ar4_h.ap()
        wq2l = ar4[:, :].bitcast(BF16).rearrange("p (c e) -> p c e", c=8)
        cin = ar4[:, 0:4096].bitcast(BF16).rearrange("p (c t) -> p c t", c=8)
        xresb = ar4[:, 4096:8192].bitcast(BF16).rearrange("p (c t) -> p c t", c=8)
        xsq = ar4[:, 8192:12288].bitcast(BF16).rearrange("p (c t) -> p c t", c=8)
        y1b = [ar4[:, 12288:16384].bitcast(BF16).rearrange("p (c t) -> p c t", c=8),
               ar2[:, 28672:32768].bitcast(BF16).rearrange("p (c t) -> p c t", c=8)]

        v1 = v1_h.ap()
        krep = krep_h.ap()
        q2lb = q2lb_h.ap()
        vupb = vupb_h.ap()
        wkvb = wkvb_h.ap()
        ptS_h = ctx.enter_context(nc.sbuf_tensor("ptS", [128, 4096], BF16))
        ptS = ptS_h.ap()
        y1fb = [y1f, ptS_h.ap().bitcast(F32).rearrange("p (c t) -> p c t", c=8)]
        rs = rs_h.ap()
        rs2 = rs2_h.ap()
        ctxn = ctxn_h.ap()
        a1c = a1c_h.ap()
        a2c = a2c_h.ap()
        b1c = b1c_h.ap()
        b2c = b2c_h.ap()
        mask01 = mask_h.ap()
        ident = ident_h.ap()
        onesb = onesb_h.ap()
        onesf = onesf_h.ap()
        hb = hb_h.ap()

        psA = psA_h.ap()
        psS4 = psS4_h.ap()
        psB = psS4[:, 0:512]
        psC = psS4[:, 512:1024]
        psD = psS4[:, 1024:1536]
        psE = psS4[:, 1536:2048]
        psB_bf = psB.bitcast(BF16)
        psC_bf = psC.bitcast(BF16)

        P = Plan()

        for dst, src in ((mask01, d_mask), (ident, d_ident), (onesb, d_onesb),
                         (onesf, d_onesf), (b1c, d_b1), (b2c, d_b2)):
            P.add("sy", lambda e, a=dst, b=src: e.dma_start(a, b), inc=("dI", 16))
        V_CONST = P.val("dI")

        def gload(dst, src, sem, waits=()):
            return P.add("gp", lambda e, a=dst, b=src: e.dma_start(a, b),
                         waits=waits, inc=(sem, 16))

        V_XB = gload(xb, d_x.rearrange("(c p) t -> p c t", p=128), "qA")
        V_WQ = gload(wqv, d_wq.rearrange("(c p) t -> p c t", p=128), "qB")
        V_Q2L = gload(q2lb, d_q2l, "qB")
        V_WKV = gload(wkvb, d_wkv.rearrange("(c p) t -> p c t", p=128), "qC")
        V_WOUT = gload(woutv, d_wout.rearrange("(c p) t -> p c t", p=128), "qD")
        V_VUP = gload(vupb, d_vup, "qD")

        if debug and phase >= 1:
            dbg_xb = nc.dram_tensor("t_xb", [128, 8, T], BF16, kind="ExternalOutput").ap()
            P.add("sy", lambda e: e.dma_start(dbg_xb, xb), waits=[("qA", 16)], inc=("dO", 16))
        v_evac_wq2l = [0] * 8
        for ec in (range(8) if phase >= 2 else ()):
            waits = [("qB", 32)]
            if ec >= 2:
                waits.append(("sc", v_evac_wq2l[ec - 2]))
            for h in range(H):
                po = (h % 2) * 64
                col = (ec % 2) * 1024 + (h % 2) * 512 + (h // 2) * 64
                def mm(e, ec=ec, h=h, po=po, col=col):
                    return e.matmul(
                        psA[:, col:col + 64],
                        wqv[po:po + 64, h // 2, ec * 128:(ec + 1) * 128],
                        q2lb[po:po + 64, (h // 2) * 64:(h // 2) * 64 + 64],
                        start=True, stop=True)
                P.add("pe", mm, waits=waits if h == 0 else (), inc=("pe", 1) if h == H - 1 else None)
            vp = P.val("pe")
            v_evac_wq2l[ec] = P.add(
                "sc",
                lambda e, ec=ec: e.activation(
                    wq2l[:, ec, :].rearrange("p (c a l) -> p c a l", a=2, c=8, l=64),
                    psA[:, (ec % 2) * 1024:(ec % 2) * 1024 + 1024].rearrange(
                        "p (a c l) -> p c a l", a=2, c=8, l=64),
                    AF.Copy, scale=0.125),
                waits=[("pe", vp)], inc=("sc", 1))

        if debug and phase >= 2:
            dbg_wq2l = nc.dram_tensor("t_wq2l", [128, 8, E], BF16, kind="ExternalOutput").ap()
            P.add("sy", lambda e: e.dma_start(dbg_wq2l, wq2l), waits=[("sc", v_evac_wq2l[7])], inc=("dO", 16))
        v_evac_vcomb = [0] * 8
        for c in (range(8) if phase >= 3 else ()):
            waits = [("qD", 32), ("sc", v_evac_wq2l[7])]
            if c >= 2:
                waits = [("sc", v_evac_vcomb[c - 2])]
            first = True
            for h, po in ((2 * c, 0), (2 * c + 1, 64)):
                for nb2 in range(2):
                    def mm(e, c=c, h=h, po=po, nb2=nb2):
                        return e.matmul(
                            psA[po:po + 64, (c % 2) * 1024 + nb2 * 512:(c % 2) * 1024 + nb2 * 512 + 512],
                            vupb[po:po + 64, (h // 2) * 64:(h // 2) * 64 + 64],
                            woutv[po:po + 64, h // 2, nb2 * 512:nb2 * 512 + 512],
                            start=True, stop=True,
                            tile_position=(po, po))
                    last = (h % 2 == 1 and nb2 == 1)
                    P.add("pe", mm, waits=waits if first else (), inc=("pe", 1) if last else None)
                    first = False
            vp = P.val("pe")
            v_evac_vcomb[c] = P.add(
                "sc",
                lambda e, c=c: e.copy(vcomb[:, c, :], psA[:, (c % 2) * 1024:(c % 2) * 1024 + 1024]),
                waits=[("pe", vp)], inc=("sc", 1))

        if debug and phase >= 3:
            dbg_vc = nc.dram_tensor("t_vcomb", [128, 8, E], BF16, kind="ExternalOutput").ap()
            P.add("sy", lambda e: e.dma_start(dbg_vc, vcomb), waits=[("sc", v_evac_vcomb[7])], inc=("dO", 16))
        v_qlat_evac = []
        unit = 0
        for mc in (range(8) if phase >= 4 else ()):
            for th in range(2):
                half = unit % 2
                waits = [("qA", 16), ("sc", v_evac_vcomb[7])]
                if len(v_qlat_evac) >= 2:
                    eng, val = v_qlat_evac[unit - 2]
                    waits.append((eng, val))
                for tc2 in range(2):
                    for kc in range(8):
                        def mm(e, mc=mc, th=th, half=half, tc2=tc2, kc=kc):
                            return e.matmul(
                                psA[:, half * 1024 + tc2 * 512: half * 1024 + tc2 * 512 + 512],
                                wq2l[:, kc, mc * 128:(mc + 1) * 128],
                                xb[:, kc, th * 1024 + tc2 * 512: th * 1024 + tc2 * 512 + 512],
                                start=(kc == 0), stop=(kc == 7))
                        P.add("pe", mm,
                              waits=waits if (tc2 == 0 and kc == 0) else (),
                              inc=("pe", 1) if (tc2 == 1 and kc == 7) else None)
                vp = P.val("pe")
                ev_eng = "dv"
                def ev(e, mc=mc, th=th, half=half, ev_eng=ev_eng):
                    if ev_eng == "dv":
                        return e.tensor_copy(qlat[:, mc, th * 1024: th * 1024 + 1024],
                                             psA[:, half * 1024: half * 1024 + 1024])
                    return e.copy(qlat[:, mc, th * 1024: th * 1024 + 1024],
                                  psA[:, half * 1024: half * 1024 + 1024])
                val = P.add(ev_eng, ev, waits=[("pe", vp)], inc=(ev_eng, 1))
                v_qlat_evac.append((ev_eng, val))
                unit += 1

        kv_waits = [("qC", 16)] + [(e, v) for e, v in v_qlat_evac[-2:]]
        for tc in (range(4) if phase >= 5 else ()):
            for kc in range(8):
                def mm(e, tc=tc, kc=kc):
                    return e.matmul(psA[:, tc * 512:tc * 512 + 512],
                                    wkvb[:, kc, :],
                                    xb[:, kc, tc * 512:tc * 512 + 512],
                                    start=(kc == 0), stop=(kc == 7))
                P.add("pe", mm, waits=kv_waits if (tc == 0 and kc == 0) else (),
                      inc=("pe", 1) if (tc == 3 and kc == 7) else None)
        V_KV = P.val("pe")

        if phase < 5:
            V_KV = None
        V_K0 = (P.add("sc", lambda e: e.copy(krep[0:64, :], psA[0:64, :]),
                     waits=[("pe", V_KV)], inc=("sc", 1)) if phase >= 5 else None)
        V_K1 = (P.add("sc", lambda e: e.copy(krep[64:128, :], psA[0:64, :]), inc=("sc", 1)) if phase >= 5 else None)
        V_VF = (P.add("sc", lambda e: e.copy(vfm, psA[64:128, :]), inc=("sc", 1)) if phase >= 5 else None)

        V_ONES = (P.add("dv", lambda e: e.memset(v1[:, :, 64:128], 1.0), inc=("dv", 1)) if phase >= 5 else None)

        v_tr = [0] * 16
        for jt in (range(16) if phase >= 5 else ()):
            waits = [("sc", V_VF)]
            if jt == 0:
                waits.append(("dI", V_CONST))
            if jt >= 2:
                waits.append(("dv", v_tr[jt - 2]))
            def tr(e, jt=jt):
                dst = psB_bf if jt % 2 == 0 else psC_bf
                return e.transpose(dst[:, 0:64],
                                   vfm[:, jt * 128:(jt + 1) * 128],
                                   ident[0:64, 0:64])
            P.add("pe", tr, waits=waits, inc=("pe", 1))
            vp = P.val("pe")
            def cp(e, jt=jt):
                srcb = psB_bf if jt % 2 == 0 else psC_bf
                return e.tensor_copy(v1[:, jt, 0:64], srcb[:, 0:64])
            v_tr[jt] = P.add("dv", cp, waits=[("pe", vp)], inc=("dv", 1))
        V_V1 = v_tr[15] if phase >= 5 else None
        v_tr_last = V_V1

        if debug and phase >= 4:
            P.add("sy", lambda e: e.dma_start(d_tqlat, qlat),
                  waits=[(e, v) for e, v in v_qlat_evac[-2:]], inc=("dO", 16))
        if debug and phase >= 5:
            P.add("sy", lambda e: e.dma_start(d_tkrep, krep), waits=[("sc", V_K1)], inc=("dO", 16))
            P.add("sy", lambda e: e.dma_start(d_tv1, v1.rearrange("p a b -> p (a b)")),
                  waits=[("dv", V_V1), ("dv", V_ONES)], inc=("dO", 16))

        PR = {}
        v_a2red = [0] * 8
        v_s2sign = [0] * 8
        v_a1red = [0] * 32
        v_s1sign = [0] * 8

        def prep_gen():
            for c in range(8):
                waits = [("pe", V_KV)]
                if c >= 2:
                    waits.append(("dv", v_a2red[c - 2]))
                vdma = P.add("gp",
                             lambda e, c=c: e.dma_start(
                                 wnstage[c % 2].rearrange("p r e -> p (r e)"),
                                 d_w2n[c * 128:(c + 1) * 128, :]),
                             waits=waits, inc=(f"qE{c % 2}", 16))
                yield
                v_a2red[c] = P.add("dv",
                                   lambda e, c=c: e.reduce_sum(
                                       a2c[:, c:c + 1],
                                       wnstage[c % 2].rearrange("p r e -> p (r e)"),
                                       axis=AX.X, apply_absolute_value=True),
                                   waits=[(f"qE{c % 2}", vdma)], inc=("dv", 1))
            PR["V_A2"] = P.add("dv",
                               lambda e: e.tensor_scalar(a2c, a2c, 1.0 / HF, None, op0=ALU.mult),
                               waits=[("dv", v_a2red[7])], inc=("dv", 1))
            for g in range(8):
                waits = []
                if g == 0:
                    waits.append(("dv", v_tr_last))
                if g >= 2:
                    waits.append(("dv", v_s2sign[g - 2]))
                vdma = P.add("gp",
                             lambda e, g=g: e.dma_start(
                                 wstage[g % 2].rearrange("p (r e) -> p r e", r=4),
                                 d_w2t[g * 512:(g + 1) * 512, :].rearrange("(r p) e -> p r e", p=128)),
                             waits=waits, inc=(f"qF{g % 2}", 16))
                yield
                w = [(f"qF{g % 2}", vdma)]
                if g >= 1:
                    w.append(("dSP", 16 * g))
                v_s2sign[g] = P.add("dv",
                      lambda e, g=g: e.tensor_scalar(
                          s2stage.bitcast(mybir.dt.int16), wstage[g % 2].bitcast(mybir.dt.int16),
                          0x8000 - 0x10000, 0x3F80, op0=ALU.bitwise_and, op1=ALU.bitwise_or),
                      waits=w, inc=("dv", 1))
                P.add("gp",
                      lambda e, g=g: e.dma_start(
                          d_s2d[g * 512:(g + 1) * 512, :].rearrange("(r p) e -> p r e", p=128),
                          s2stage.rearrange("p (r e) -> p r e", r=4)),
                      waits=[("dv", v_s2sign[g])], inc=("dSP", 16))
            for g in range(8):
                waits = []
                if g < 2:
                    waits.append(("dv", v_a2red[6 + g]))
                if g >= 2:
                    waits.append(("dv", v_a1red[4 * (g - 2) + 3]))
                vdma = P.add("gp",
                             lambda e, g=g: e.dma_start(
                                 wnstage[g % 2],
                                 d_w1n[g * 512:(g + 1) * 512, :].rearrange("(r p) e -> p r e", p=128)),
                             waits=waits, inc=(f"qG{g % 2}", 16))
                yield
                for r in range(4):
                    v_a1red[4 * g + r] = P.add(
                        "dv",
                        lambda e, g=g, r=r: e.reduce_sum(a1c[:, 4 * g + r:4 * g + r + 1],
                                                         wnstage[g % 2][:, r, :],
                                                         axis=AX.X, apply_absolute_value=True),
                        waits=[(f"qG{g % 2}", vdma)] if r == 0 else (), inc=("dv", 1))
            PR["V_A1"] = P.add("dv",
                               lambda e: e.tensor_scalar(a1c, a1c, 1.0 / E, None, op0=ALU.mult),
                               waits=[("dv", v_a1red[31])], inc=("dv", 1))
            for c in range(8):
                waits = []
                if c < 2:
                    waits.append(("dv", v_s2sign[6 + c]))
                if c >= 2:
                    waits.append(("dv", v_s1sign[c - 2]))
                vdma = P.add("gp",
                             lambda e, c=c: e.dma_start(wstage[c % 2], d_w1t[c * 128:(c + 1) * 128, :]),
                             waits=waits, inc=(f"qH{c % 2}", 16))
                yield
                v_s1sign[c] = P.add("dv",
                                    lambda e, c=c: e.tensor_scalar(
                                        s1T[:, c, :].bitcast(mybir.dt.int16), wstage[c % 2].bitcast(mybir.dt.int16),
                                        0x8000 - 0x10000, 0x3F80,
                                        op0=ALU.bitwise_and, op1=ALU.bitwise_or),
                                    waits=[(f"qH{c % 2}", vdma)], inc=("dv", 1))
            if debug:
                P.add("sy", lambda e: e.dma_start(d_ta1, a1c), waits=[("dv", PR["V_A1"])], inc=("dO", 16))
            while True:
                yield

        if phase >= 6:
            _pg = prep_gen()

            def emit_prep_slice(n):
                for _ in range(n):
                    next(_pg)
        else:
            def emit_prep_slice(n):
                pass

        if phase == 6:
            emit_prep_slice(64)
            V_SPILLS = 16 * 8
        if phase < 7:
            v_head_mult = None
        if phase >= 7:
            sidx = 0
            stripe_hist = []
            v_head_mult = [0] * H
            prev_srecip_mult = {0: None, 1024: None}
            prev_rs_read = {0: None, 1024: None}
            v_head_ln = [0] * H
            v_spill = [0] * H

            def half_stripes(jt, half):
                lo, hi = half * 1024, half * 1024 + 1024
                out = []
                p = max(jt * 128, lo)
                while p < hi:
                    e = min((p // 512 + 1) * 512, hi)
                    out.append((p, e))
                    p = e
                return out

            mask_gated = [False]
            pending_epi = [None]
            spill_idx = {}
            n_spilled = 0
            prev_half_guard = None

            for hp in range(8):
                hA, hB = 2 * hp, 2 * hp + 1
                for half in range(2):
                    pending_pv = []
                    LAG = 3
                    first_pv = True
                    jts = range(8) if half == 0 else range(16)
                    for jt in jts:
                        for (qs, qe) in half_stripes(jt, half):
                            w = qe - qs
                            pu = sidx // 2
                            slotA = (pu % 2) * 2
                            off = qs % 512
                            base = (pu % 4) * 1024
                            pbase = slotA * 512
                            psS_A = (psB, psC, psD, psE)[slotA]
                            psS_B = (psB, psC, psD, psE)[slotA + 1]
                            qkv = []
                            for head, po, psX in ((hA, 0, psS_A), (hB, 64, psS_B)):
                                waits = []
                                if head == hA and pu >= 2:
                                    waits.append(("sc", stripe_hist[pu - 2]["exp"]))
                                if sidx == 0:
                                    waits.append(("sc", V_K1))
                                    waits.append(("dv", V_V1))
                                    waits.append(("dv", V_ONES))
                                def qk(e, po=po, hp=hp, jt=jt, qs=qs, qe=qe, psX=psX, off=off, w=w):
                                    return e.matmul(psX[:, off:off + w],
                                                    krep[po:po + 64, jt * 128:(jt + 1) * 128],
                                                    qlat[po:po + 64, hp, qs:qe],
                                                    start=True, stop=True, skip_group_check=True)
                                qkv.append(P.add("pe", qk, waits=waits, inc=("pe", 1)))
                                sidx += 1
                            ew = [("pe", qkv[1])]
                            if pu >= 4:
                                assert stripe_hist[pu - 4]["pv"] is not None
                                ew.append(("pe", stripe_hist[pu - 4]["pv"]))
                            ew_w = 1024 - off
                            vexp = P.add("sc",
                                         lambda e, base=base, off=off, ew_w=ew_w, pbase=pbase: e.activation(
                                             ptS[:, base + off:base + off + ew_w],
                                             psS4[:, pbase + off:pbase + off + ew_w], AF.Exp),
                                         waits=ew, inc=("sc", 1))
                            if pending_epi[0] is not None:
                                pending_epi[0]()
                                pending_epi[0] = None
                            vmask = None
                            if qs == jt * 128:
                                mw0 = [("sc", vexp)]
                                if not mask_gated[0]:
                                    mw0.append(("dI", V_CONST))
                                    mask_gated[0] = True
                                P.add("gp",
                                      lambda e, base=base, off=off: e.tensor_tensor(
                                          ptS[:, base + off:base + off + 128],
                                          ptS[:, base + off:base + off + 128],
                                          mask01, op=ALU.mult),
                                      waits=mw0)
                                vmask = P.add("gp",
                                              lambda e, base=base, off=off: e.tensor_tensor(
                                                  ptS[:, base + 512 + off:base + 512 + off + 128],
                                                  ptS[:, base + 512 + off:base + 512 + off + 128],
                                                  mask01, op=ALU.mult),
                                              waits=[("sc", vexp)], inc=("gp", 1))
                            pw = [("sc", vexp)]
                            if vmask is not None:
                                pw.append(("gp", vmask))
                            if first_pv and prev_half_guard is not None:
                                pw.extend(prev_half_guard)
                            first_pv = False
                            def pvA(e, jt=jt, qs=qs, w=w, base=base, off=off, half=half):
                                return e.matmul(psA[:, qs - half * 1024:qs - half * 1024 + w],
                                                v1[:, jt, :], ptS[:, base + off:base + off + w],
                                                start=(jt == 0), stop=False, skip_group_check=True)
                            def pvB(e, jt=jt, qs=qs, w=w, base=base, off=off, half=half):
                                return e.matmul(psA[:, 1024 + qs - half * 1024:1024 + qs - half * 1024 + w],
                                                v1[:, jt, :], ptS[:, base + 512 + off:base + 512 + off + w],
                                                start=(jt == 0), stop=False, skip_group_check=True)
                            hist = dict(exp=vexp, pv=None)
                            stripe_hist.append(hist)
                            pending_pv.append((pvA, pw, None))
                            pending_pv.append((pvB, [], hist))
                            while len(pending_pv) > 2 * LAG:
                                assert pending_epi[0] is None
                                fn, fw, fh = pending_pv.pop(0)
                                vv = P.add("pe", fn, waits=fw, inc=("pe", 1))
                                if fh is not None:
                                    fh["pv"] = vv
                        if half == 0 and jt == 3:
                            emit_prep_slice(1)
                        if half == 1 and jt in (3, 9):
                            emit_prep_slice(1)
                    while pending_pv:
                        fn, fw, fh = pending_pv.pop(0)
                        vv = P.add("pe", fn, waits=fw, inc=("pe", 1))
                        if fh is not None:
                            fh["pv"] = vv
                    vpv_last = stripe_hist[-1]["pv"]
                    def run_epilogue(hA=hA, hB=hB, half=half, vpv_last=vpv_last):
                        nonlocal prev_half_guard, n_spilled
                        guard = []
                        vexp_h = {}
                        for head, cb in ((hA, 0), (hB, 1024)):
                            lnw = [("pe", vpv_last)] if cb == 0 else []
                            if prev_rs_read[cb] is not None:
                                lnw.append(("sc", prev_rs_read[cb]))
                            vexp_h[cb] = P.add("sc",
                                  lambda e, cb=cb: e.activation(rs[cb // 16:cb // 16 + 64, :],
                                                                psA[64:128, cb:cb + 1024], AF.Ln),
                                  waits=lnw, inc=("sc", 1))
                        for head, cb in ((hA, 0), (hB, 1024)):
                            ew2 = [("sc", vexp_h[cb])]
                            if prev_srecip_mult[cb] is not None:
                                ew2.append(("dv", prev_srecip_mult[cb]))
                            vexp_h[cb] = prev_rs_read[cb] = P.add("sc",
                                        lambda e, cb=cb: e.activation(rs2[cb // 16:cb // 16 + 64, :],
                                                                      rs[cb // 16:cb // 16 + 64, :],
                                                                      AF.Exp, scale=-1.0),
                                        waits=ew2, inc=("sc", 1))
                        for head, cb in ((hA, 0), (hB, 1024)):
                            vln = vexp_h[cb]
                            mw = [("sc", vln)]
                            if (head, half, "prevspill") in spill_idx:
                                csem = "dCW" if head % 2 == 0 else "dCW1"
                                mw.append((csem, spill_idx[(head, half, "prevspill")]))
                            def mult(e, cb=cb, head=head):
                                return e.tensor_tensor(
                                    ctxn[0:64, (head % 2) * 1024:(head % 2) * 1024 + 1024],
                                    psA[0:64, cb:cb + 1024],
                                    rs2[cb // 16:cb // 16 + 64, :], op=ALU.mult)
                            vm = P.add("dv", mult, waits=mw, inc=("dv", 1))
                            prev_srecip_mult[cb] = vm
                            guard.append(("dv", vm))
                            guard.append(("sc", vln))
                            n_spilled += 1
                            vsp = P.add("sy",
                                        lambda e, head=head, half=half: e.dma_start(
                                            d_ctxs[head * 64:(head + 1) * 64, half * 1024:half * 1024 + 1024],
                                            ctxn[0:64, (head % 2) * 1024:(head % 2) * 1024 + 1024]),
                                        waits=[("dv", vm)],
                                        inc=("dCW" if head % 2 == 0 else "dCW1", 16))
                            nxt = (head, half + 1) if half == 0 else (head + 2, 0)
                            spill_idx[(nxt[0], nxt[1], "prevspill")] = vsp
                        prev_half_guard = guard
                    pending_epi[0] = run_epilogue
                emit_prep_slice(1)

        if phase >= 7:
            if pending_epi[0] is not None:
                pending_epi[0]()
                pending_epi[0] = None
            emit_prep_slice(40)
            V_A1 = PR["V_A1"]
            V_A2 = PR["V_A2"]
            V_S1 = v_s1sign[7]
            V_SPILLS = 16 * 8

        if phase >= 8:
            v_outproj = {}
            v_res1 = {}
            v_epi = [0] * NB
            v_ffn2_grp = {}
            v_sums1 = [0] * NB
            v_sums2 = [0] * NB
            v_strips1 = [0] * NB
            v_strips2 = [0] * NB
            v_bcast1 = [0] * NB
            v_bcast2 = [0] * NB
            v_apply2 = [0] * NB
            v_apply1d3 = [0] * NB
            v_y2out = [0] * NB
            v_vr2b = [0] * NB
            v_vr2 = [0] * NB
            v_vr2_mc = {}
            v_ffn2_last = {}
            v_epi_mc = {}
            v_y1bc = {}
            v_r2sq = [0] * NB
            v_xsq = [0] * NB
            v_cin = [0] * (NB + 1)
            v_xin = [0] * (NB + 1)
            v_s2rd = {}
            v_gelu_hist = {}
            NG = 16

            def strip(bank, rg):
                o = bank * 1536
                return strips[:, o + rg[0]:o + rg[1]]

            def strip_i(bank, rg):
                return strip(bank, rg).bitcast(I32)

            def emit_strips(bank, vcopy_val):
                va = P.add("dv",
                           lambda e: e.tensor_tensor(strip(bank, S_T), strip(bank, S_MEAN),
                                                     strip(bank, S_MEAN), op=ALU.mult),
                           waits=[("sc", vcopy_val)], inc=("dv", 1))
                vb = P.add("dv",
                           lambda e: e.scalar_tensor_tensor(strip(bank, S_VAR), strip(bank, S_EX2),
                                                            EPS, strip(bank, S_T),
                                                            op0=ALU.add, op1=ALU.subtract),
                           waits=[("dv", va)], inc=("dv", 1))
                vc1 = P.add("sc",
                            lambda e: e.activation(strip(bank, S_T), strip(bank, S_VAR),
                                                   AF.Ln),
                            waits=[("dv", vb)], inc=("sc", 1))
                vc2 = P.add("sc",
                            lambda e: e.activation(strip(bank, S_Y), strip(bank, S_T),
                                                   AF.Exp, scale=-0.5),
                            waits=[("sc", vc1)], inc=("sc", 1))
                vd = P.add("dv",
                           lambda e: e.scalar_tensor_tensor(strip(bank, S_NMR), strip(bank, S_MEAN),
                                                            -1.0, strip(bank, S_Y),
                                                            op0=ALU.mult, op1=ALU.mult),
                           waits=[("sc", vc2)], inc=("dv", 1))
                return vd

            def emit_outproj_mc(bb, mc):
                waits = []
                if mc == 0:
                    waits.append(("dCR", v_cin[bb]))
                    waits.append(("sc", v_evac_vcomb[7]))
                    if bb >= 1:
                        waits.append(("dv", v_res1[(bb - 1, 7)]))
                        waits.append(("dv", v_apply2[bb - 2]) if bb >= 2 else ("dv", v_res1[(bb - 1, 7)]))
                else:
                    waits.append(("dv", v_res1[(bb, mc - 1)]))
                for kc in range(8):
                    def mm(e, mc=mc, kc=kc):
                        return e.matmul(psE[:, 0:256],
                                        vcomb[:, kc, mc * 128:(mc + 1) * 128],
                                        cin[:, kc, :],
                                        start=(kc == 0), stop=(kc == 7))
                    P.add("pe", mm, waits=waits if kc == 0 else (),
                          inc=("pe", 1) if kc == 7 else None)
                v_outproj[(bb, mc)] = P.val("pe")
                rw = [("pe", v_outproj[(bb, mc)]), ("dXR", v_xin[bb])]
                if mc == 0 and bb >= 2:
                    rw.append(("gp", v_r2sq[bb - 2]))
                    rw.append(("pe", v_sums2[bb - 2]))
                def r1(e, bb=bb, mc=mc):
                    return e.tensor_tensor(xresb[:, mc, :], psE[:, 0:256],
                                           xin[bb % 2][:, mc, :], op=ALU.add)
                v_res1[(bb, mc)] = P.add("dv", r1, waits=rw, inc=("dv", 1))

            def emit_outproj_tail(bb):
                xw = [("dv", v_res1[(bb, 7)])]
                if bb >= 2:
                    xw.append(("pe", v_sums2[bb - 2]))
                v_xsq[bb] = P.add("gp",
                                  lambda e: e.tensor_tensor(
                                      xsq.rearrange("p c t -> p (c t)"),
                                      xresb.rearrange("p c t -> p (c t)"),
                                      xresb.rearrange("p c t -> p (c t)"), op=ALU.mult),
                                  waits=xw, inc=("gp", 1))

            def emit_outproj(bb):
                for mc in range(8):
                    emit_outproj_mc(bb, mc)
                emit_outproj_tail(bb)

            def emit_sums1(bb):
                sw = [("dv", v_res1[(bb, 7)]), ("dI", V_CONST)]
                if bb >= 2:
                    sw.append(("dv", v_apply2[bb - 2]))
                for mc in range(8):
                    P.add("pe",
                          lambda e, mc=mc: e.matmul(psE[0:1, 0:256], onesb, xresb[:, mc, :],
                                                    start=(mc == 0), stop=(mc == 7)),
                          waits=sw if mc == 0 else ())
                for mc in range(8):
                    P.add("pe",
                          lambda e, mc=mc: e.matmul(psE[0:1, 256:512], onesb, xsq[:, mc, :],
                                                    start=(mc == 0), stop=(mc == 7)),
                          waits=[("gp", v_xsq[bb])] if mc == 0 else (),
                          inc=("pe", 1) if mc == 7 else None)
                v_sums1[bb] = P.val("pe")
                vcopy = P.add("sc",
                              lambda e: e.activation(strip(0, (0, 512)), psE[0:1, 0:512],
                                                     AF.Copy, scale=1.0 / E),
                              waits=[("pe", v_sums1[bb])] + ([("dv", PR["V_A1"])] if bb == 0 else []),
                              inc=("sc", 1))
                v_strips1[bb] = emit_strips(0, vcopy)

            def emit_bcast1(bb):
                P.add("pe", lambda e: e.matmul(psE[:, 0:256], onesf, strip(0, S_Y), start=True, stop=True),
                      waits=[("dv", v_strips1[bb])])
                P.add("pe", lambda e: e.matmul(psE[:, 256:512], onesf, strip(0, S_NMR), start=True, stop=True),
                      inc=("pe", 1))
                v_bcast1[bb] = P.val("pe")

            def emit_apply1(bb):
                yf = y1fb[bb % 2]
                vd1 = {}
                vd2 = {}
                def emit_d1(mc):
                    d1w = [("dv", v_res1[(bb, mc)])]
                    if mc == 0:
                        d1w.append(("pe", v_bcast1[bb]))
                    if bb >= 2:
                        d1w.append(("dv", v_vr2_mc[(bb - 2, mc)]))
                    vd1[mc] = P.add("dv",
                                    lambda e, mc=mc: e.tensor_tensor(yf[:, mc, :], xresb[:, mc, :],
                                                                     psE[:, 0:256], op=ALU.mult),
                                    waits=d1w, inc=("dv", 1))
                def emit_d2(mc):
                    vd2[mc] = P.add("dv",
                                    lambda e, mc=mc: e.tensor_tensor(yf[:, mc, :], yf[:, mc, :],
                                                                     psE[:, 256:512], op=ALU.add),
                                    waits=[("dv", vd1[mc])], inc=("dv", 1))
                    v_y1bc[(bb, mc)] = P.add("sc",
                                             lambda e, mc=mc, bb=bb: e.copy(y1b[bb % 2][:, mc, :],
                                                                            yf[:, mc, :]),
                                             waits=[("dv", vd2[mc])], inc=("sc", 1))
                emit_d1(0)
                for mc in range(1, 8):
                    emit_d1(mc)
                    emit_d2(mc - 1)
                emit_d2(7)
                v_apply1d3[bb] = v_y1bc[(bb, 7)]
                if debug:
                    P.add("sy", lambda e, bb=bb: e.dma_start(d_ty1[:, bb, :, :], y1fb[bb % 2]),
                          waits=[("sc", v_apply1d3[bb])], inc=("dO", 16))

            def emit_res2part1(bb):
                for mc in range(8):
                    def ep(e, mc=mc):
                        return e.activation(ffnout[:, mc, :], psA[:, mc * 256:mc * 256 + 256],
                                            AF.Identity, bias=b2c[:, mc:mc + 1], scale=a2c[:, mc:mc + 1])
                    v_epi_mc[(bb, mc)] = P.add(
                        "sc", ep,
                        waits=[("pe", v_ffn2_last[(bb, mc)])] + ([("dv", PR["V_A2"])] if mc == 0 else []),
                        inc=("sc", 1))
                v_epi[bb] = v_epi_mc[(bb, 7)]
                if debug:
                    P.add("sy", lambda e, bb=bb: e.dma_start(d_tffn[:, bb, :, :], ffnout),
                          waits=[("sc", v_epi[bb])], inc=("dO", 16))
                for mc in range(8):
                    v_vr2_mc[(bb, mc)] = P.add(
                        "dv",
                        lambda e, mc=mc, bb=bb: e.tensor_tensor(res2f[:, mc, :], ffnout[:, mc, :],
                                                                y1fb[bb % 2][:, mc, :], op=ALU.add),
                        waits=[("sc", v_epi_mc[(bb, mc)])], inc=("dv", 1))
                v_vr2[bb] = v_vr2_mc[(bb, 7)]

            def emit_res2part2(bb):
                nxt = bb + 1 if bb + 1 < NB else bb
                v_vr2b[bb] = P.add("dv",
                                   lambda e: e.tensor_copy(xresb.rearrange("p c t -> p (c t)"),
                                                           res2f.rearrange("p c t -> p (c t)")),
                                   waits=[("pe", v_sums1[nxt]), ("gp", v_xsq[nxt]), ("dv", v_vr2[bb])],
                                   inc=("dv", 1))
                v_r2sq[bb] = P.add("gp",
                                   lambda e: e.tensor_tensor(
                                       xsq.rearrange("p c t -> p (c t)"),
                                       xresb.rearrange("p c t -> p (c t)"),
                                       xresb.rearrange("p c t -> p (c t)"), op=ALU.mult),
                                   waits=[("dv", v_vr2b[bb]), ("pe", v_sums1[nxt])], inc=("gp", 1))

            def emit_sums2(bb):
                sw = [("dv", v_vr2b[bb]), ("sc", v_apply1d3[bb + 1] if bb + 1 < NB else v_apply1d3[bb])]
                for mc in range(8):
                    P.add("pe",
                          lambda e, mc=mc: e.matmul(psE[0:1, 0:256], onesb, xresb[:, mc, :],
                                                    start=(mc == 0), stop=(mc == 7)),
                          waits=sw if mc == 0 else ())
                for mc in range(8):
                    P.add("pe",
                          lambda e, mc=mc: e.matmul(psE[0:1, 256:512], onesb, xsq[:, mc, :],
                                                    start=(mc == 0), stop=(mc == 7)),
                          waits=[("gp", v_r2sq[bb])] if mc == 0 else (),
                          inc=("pe", 1) if mc == 7 else None)
                v_sums2[bb] = P.val("pe")
                vcopy = P.add("sc",
                              lambda e: e.activation(strip(1, (0, 512)), psE[0:1, 0:512],
                                                     AF.Copy, scale=1.0 / E),
                              waits=[("pe", v_sums2[bb])], inc=("sc", 1))
                v_strips2[bb] = emit_strips(1, vcopy)

            def emit_bcast2(bb):
                P.add("pe", lambda e: e.matmul(psE[:, 0:256], onesf, strip(1, S_Y), start=True, stop=True),
                      waits=[("dv", v_strips2[bb])])
                P.add("pe", lambda e: e.matmul(psE[:, 256:512], onesf, strip(1, S_NMR), start=True, stop=True),
                      inc=("pe", 1))
                v_bcast2[bb] = P.val("pe")

            def emit_apply2(bb):
                bca = psE[:, 0:256].unsqueeze(1).broadcast_to([128, 8, 256])
                bcb = psE[:, 256:512].unsqueeze(1).broadcast_to([128, 8, 256])
                a2w = [("pe", v_bcast2[bb]), ("dv", v_vr2[bb])]
                if bb >= 1:
                    a2w.append(("dO", v_y2out[bb - 1]))
                va1 = P.add("dv", lambda e, bca=bca: e.tensor_tensor(y2, res2f, bca, op=ALU.mult),
                            waits=a2w, inc=("dv", 1))
                v_apply2[bb] = P.add("dv", lambda e, bcb=bcb: e.tensor_tensor(y2, y2, bcb, op=ALU.add),
                                     waits=[("dv", va1)], inc=("dv", 1))
                v_y2out[bb] = P.add("sy",
                                    lambda e, bb=bb: e.dma_start(
                                        d_y.rearrange("(c p) t -> p c t", p=128)[:, :, bb * TB:bb * TB + TB], y2),
                                    waits=[("dv", v_apply2[bb])], inc=("dO", 16))

            def emit_ffn_steps(bb, j_lo, j_hi):
                for j in range(j_lo, j_hi):
                    if j < NJ:
                        g = j // 2
                        ps = (psB, psC, psD)[j % 3]
                        w1w = []
                        if j == 0:
                            w1w.append(("dv", V_S1))
                        if j <= 2 and bb >= 1:
                            w1w.append(("sc", v_gelu_hist[{0: 30, 1: 31, 2: 29}[j]]))
                        if j >= 3:
                            w1w.append(("sc", v_gelu_hist[j - 3]))
                        for kc in range(8):
                            kw = list(w1w) if kc == 0 else []
                            if j == 0:
                                kw.append(("sc", v_y1bc[(bb, kc)]))
                            def mm(e, j=j, kc=kc, ps=ps, bb=bb):
                                return e.matmul(ps[:, 0:256],
                                                s1T[:, kc, j * 128:(j + 1) * 128],
                                                y1b[bb % 2][:, kc, :],
                                                start=(kc == 0), stop=(kc == 7))
                            P.add("pe", mm, waits=kw,
                                  inc=("pe", 1) if kc == 7 else None)
                        vf1 = P.val("pe")
                        gw0 = [("pe", vf1), ("dv", PR["V_A1"])]
                        if bb == 0 and j == 0:
                            gw0.append(("dI", V_CONST))
                        vg = P.add("sc",
                                   lambda e, j=j, ps=ps: e.activation(
                                       hb[:, j % 4, :], ps[:, 0:256], AF.Gelu,
                                       bias=b1c[:, j:j + 1], scale=a1c[:, j:j + 1]),
                                   waits=gw0, inc=("sc", 1))
                        v_gelu_hist[j] = vg
                    jj = j - 2
                    if jj >= 0 and jj < NJ:
                        g2 = jj // 2
                        s2sem = "dS2" if g2 % 2 == 0 else "dS2b"
                        f2w = [("sc", v_gelu_hist[jj])]
                        if jj == 0:
                            f2w.append((s2sem, v_s2rd[(bb, g2)]))
                            if bb > 0:
                                f2w.append(("sc", v_epi[bb - 1]))
                            f2w.append(("dv", PR["V_A2"]))
                        elif jj % 2 == 0:
                            f2w.append((s2sem, v_s2rd[(bb, g2)]))
                        for mc in range(8):
                            def mm2(e, jj=jj, mc=mc, g2=g2, bb=bb):
                                return e.matmul(psA[:, mc * 256:mc * 256 + 256],
                                                s2b[:, g2 % 2, jj % 2, mc * 128:(mc + 1) * 128],
                                                hb[:, jj % 4, :],
                                                start=(jj == 0 and mc % 2 == 0),
                                                stop=(jj == NJ - 1), skip_group_check=True)
                            last = (jj == NJ - 1)
                            P.add("pe", mm2, waits=f2w if mc == 0 else (),
                                  inc=("pe", 1) if (mc == 7 or last) else None)
                            if last:
                                v_ffn2_last[(bb, mc)] = P.val("pe")
                        if jj % 2 == 1:
                            gd = jj // 2
                            v_ffn2_grp[(bb, gd)] = P.val("pe")
                            gg, b2 = gd + 2, bb
                            if gg >= NG:
                                gg -= NG
                                b2 += 1
                            if b2 < NB and (b2, gg) not in v_s2rd:
                                v_s2rd[(b2, gg)] = P.add(
                                    "sy",
                                    lambda e, b2=b2, gg=gg: e.dma_start(
                                        s2b[:, gg % 2, :, :],
                                        d_s2d[gg * 256:(gg + 1) * 256, :].rearrange("(j p) e -> p j e", p=128)),
                                    waits=[("pe", v_ffn2_grp[(bb, gd)])],
                                    inc=("dS2" if gg % 2 == 0 else "dS2b", 16))

            for kc in range(8):
                v_cin[0] = P.add("sy",
                                 lambda e, kc=kc: e.dma_start(
                                     cin[:, kc, :],
                                     d_ctxs.rearrange("(c p) t -> p c t", p=128)[:, kc, 0:TB]),
                                 waits=[("dCW", 16 * (2 * kc + 1)), ("dCW1", 16 * (2 * kc + 1))],
                                 inc=("dCR", 16))
            v_xin[0] = P.add("sy",
                             lambda e: e.dma_start(xin[0], d_x.rearrange("(c p) t -> p c t", p=128)[:, :, 0:TB]),
                             waits=[("dCW", 16 * H), ("dCW1", 16 * H)], inc=("dXR", 16))
            for g in range(2):
                v_s2rd[(0, g)] = P.add("sy",
                                       lambda e, g=g: e.dma_start(
                                           s2b[:, g % 2, :, :],
                                           d_s2d[g * 256:(g + 1) * 256, :].rearrange("(j p) e -> p j e", p=128)),
                                       waits=[("dSP", V_SPILLS)],
                                       inc=("dS2" if g % 2 == 0 else "dS2b", 16))

            emit_outproj(0)
            v_cin[1] = P.add("sy",
                             lambda e: e.dma_start(cin, d_ctxs.rearrange("(c p) t -> p c t", p=128)[:, :, TB:2 * TB]),
                             waits=[("pe", v_outproj[(0, 7)])], inc=("dCR", 16))
            v_xin[1] = P.add("sy",
                             lambda e: e.dma_start(xin[1], d_x.rearrange("(c p) t -> p c t", p=128)[:, :, TB:2 * TB]),
                             waits=[("dv", v_res1[(0, 7)])], inc=("dXR", 16))
            emit_sums1(0)
            emit_bcast1(0)
            emit_apply1(0)

            for b in range(NB):
                emit_ffn_steps(b, 0, 8)
                if b >= 1:
                    emit_sums2(b - 1)
                emit_ffn_steps(b, 8, 11)
                if b >= 1:
                    emit_bcast2(b - 1)
                emit_ffn_steps(b, 11, 12)
                if b >= 1:
                    emit_apply2(b - 1)
                emit_ffn_steps(b, 12, 13)
                if b + 1 < NB:
                    for mc in range(8):
                        emit_outproj_mc(b + 1, mc)
                        emit_ffn_steps(b, 13 + mc, 14 + mc)
                    emit_outproj_tail(b + 1)
                    if b + 2 < NB:
                        v_cin[b + 2] = P.add("sy",
                                             lambda e, b=b: e.dma_start(
                                                 cin, d_ctxs.rearrange("(c p) t -> p c t", p=128)[:, :, (b + 2) * TB:(b + 3) * TB]),
                                             waits=[("pe", v_outproj[(b + 1, 7)])], inc=("dCR", 16))
                        v_xin[b + 2] = P.add("sy",
                                             lambda e, b=b: e.dma_start(
                                                 xin[(b + 2) % 2],
                                                 d_x.rearrange("(c p) t -> p c t", p=128)[:, :, (b + 2) * TB:(b + 3) * TB]),
                                             waits=[("dv", v_res1[(b + 1, 7)])], inc=("dXR", 16))
                else:
                    emit_ffn_steps(b, 13, 21)
                emit_ffn_steps(b, 21, 24)
                if b + 1 < NB:
                    emit_sums1(b + 1)
                emit_ffn_steps(b, 24, 27)
                if b + 1 < NB:
                    emit_bcast1(b + 1)
                emit_ffn_steps(b, 27, NJ + 2)
                if b + 1 < NB:
                    emit_apply1(b + 1)
                emit_res2part1(b)
                emit_res2part2(b)

            emit_sums2(NB - 1)
            emit_bcast2(NB - 1)
            emit_apply2(NB - 1)

        P.add("sy", lambda e: None, waits=[("dO", P.val("dO"))])

        def emit(eng_key):
            def run(eng):
                for waits, fn, inc in P.items[eng_key]:
                    for s, v in waits:
                        if v is None:
                            continue
                        eng.wait_ge(SEM[s], v)
                    ins = fn(eng)
                    if inc is not None:
                        assert ins is not None
                        ins.then_inc(SEM[inc[0]], inc[1])
            return run

        @block.sync
        def _(eng):
            emit("sy")(eng)

        @block.gpsimd
        def _(eng):
            emit("gp")(eng)

        @block.tensor
        def _(eng):
            emit("pe")(eng)

        @block.vector
        def _(eng):
            emit("dv")(eng)

        @block.scalar
        def _(eng):
            emit("sc")(eng)

    return nc


def _prep_shared(inp):
    bf = ml_dtypes.bfloat16
    wq = np.ascontiguousarray(inp["L0_attn_Wq"], dtype=np.float32)
    q2l = np.ascontiguousarray(
        inp["L0_attn_q_to_latent"].reshape(8, 2, 64, 64).transpose(1, 2, 0, 3).reshape(128, 512),
        dtype=np.float32)
    wkvT = np.ascontiguousarray(
        np.concatenate([inp["L0_attn_Wk_down"], inp["L0_attn_Wv_down"]], axis=0).T, dtype=np.float32)
    woutT = np.ascontiguousarray(inp["L0_attn_Wout"].T, dtype=np.float32)
    vupT = np.ascontiguousarray(
        inp["L0_attn_v_up"].reshape(8, 2, 64, 64).transpose(1, 3, 0, 2).reshape(128, 512),
        dtype=np.float32)
    w1 = np.ascontiguousarray(inp["L0_ffn_w1"], dtype=np.float32)
    w2 = np.ascontiguousarray(inp["L0_ffn_w2"], dtype=np.float32)
    s = np.arange(128)
    shared = dict(
        wq=wq,
        q2l=q2l,
        wkvT=wkvT,
        woutT=woutT,
        vupT=vupT,
        w1T=np.ascontiguousarray(w1.T),
        w2T=np.ascontiguousarray(w2.T),
        w1n=w1,
        w2n=w2,
        b1c=np.ascontiguousarray(inp["L0_ffn_b1"].reshape(32, 128).T, dtype=np.float32),
        b2c=np.ascontiguousarray(inp["L0_ffn_b2"].reshape(8, 128).T, dtype=np.float32),
        mask01=np.ascontiguousarray((s[None, :] >= s[:, None]).astype(bf)),
        ident=np.eye(128, dtype=bf),
        onesb=np.ones((128, 1), dtype=bf),
        onesf=np.ones((1, 128), dtype=np.float32),
    )
    return shared


def kernel(**inputs):
    x = np.asarray(inputs["L0_attn_hidden_states"], dtype=np.float32)
    shared = _prep_shared(inputs)
    nc = build(debug=False)
    in_maps = []
    for i in range(NCORES):
        m = dict(shared)
        m["x_fm"] = np.ascontiguousarray(x[i].T)
        in_maps.append(m)
    res = run_bass_kernel_spmd(nc, in_maps, core_ids=list(range(NCORES)))
    out = np.empty((NCORES, T, E), dtype=np.float32)
    for i in range(NCORES):
        out[i] = res.results[i]["y_fm"].T
    return out


if __name__ == "__main__":
    nc = build(debug=False)
    print("instructions:", nc.n_instructions())


# version 27
# speedup vs baseline: 1.0611x; 1.0125x over previous
import numpy as np
import ml_dtypes

import concourse.bass as bass
import concourse.mybir as mybir
from concourse.bass_utils import run_bass_kernel_spmd

F32 = mybir.dt.float32
BF16 = mybir.dt.bfloat16
I32 = mybir.dt.int32
AF = mybir.ActivationFunctionType
ALU = mybir.AluOpType
AX = mybir.AxisListType

NCORES = 8
T = 2048
E = 1024
H = 16
L = 64
HF = 4096
TB = 256
NB = T // TB
NJ = HF // 128
EPS = 1e-5
RSQRT_MAGIC = 0x5F3759DF

ENGS = ("sy", "gp", "pe", "dv", "sc")


class Plan:

    def __init__(self):
        self.items = {e: [] for e in ENGS}
        self.c = {}

    def val(self, sem):
        return self.c.get(sem, 0)

    def add(self, eng, fn, waits=(), inc=None):
        tgt = None
        if inc is not None:
            s, a = inc
            self.c[s] = self.c.get(s, 0) + a
            tgt = self.c[s]
        self.items[eng].append((tuple(waits), fn, inc))
        return tgt


def _stripes(jt):
    out = []
    p = jt * 128
    while p < T:
        e = min((p // 512 + 1) * 512, T)
        out.append((p, e))
        p = e
    return out


def build(debug=False, phase=8, bstage=7):
    nc = bass.Bass("TRN2", target_bir_lowering=False)

    d_x = nc.dram_tensor("x_fm", [E, T], F32, kind="ExternalInput").ap()
    d_wq = nc.dram_tensor("wq", [E, E], F32, kind="ExternalInput").ap()
    d_q2l = nc.dram_tensor("q2l", [128, 512], F32, kind="ExternalInput").ap()
    d_wkv = nc.dram_tensor("wkvT", [E, 128], F32, kind="ExternalInput").ap()
    d_wout = nc.dram_tensor("woutT", [E, E], F32, kind="ExternalInput").ap()
    d_vup = nc.dram_tensor("vupT", [128, 512], F32, kind="ExternalInput").ap()
    d_w1t = nc.dram_tensor("w1T", [E, HF], F32, kind="ExternalInput").ap()
    d_w2t = nc.dram_tensor("w2T", [HF, E], F32, kind="ExternalInput").ap()
    d_w1n = nc.dram_tensor("w1n", [HF, E], F32, kind="ExternalInput").ap()
    d_w2n = nc.dram_tensor("w2n", [E, HF], F32, kind="ExternalInput").ap()
    d_b1 = nc.dram_tensor("b1c", [128, 32], F32, kind="ExternalInput").ap()
    d_b2 = nc.dram_tensor("b2c", [128, 8], F32, kind="ExternalInput").ap()
    d_mask = nc.dram_tensor("mask01", [128, 128], BF16, kind="ExternalInput").ap()
    d_ident = nc.dram_tensor("ident", [128, 128], BF16, kind="ExternalInput").ap()
    d_onesb = nc.dram_tensor("onesb", [128, 1], BF16, kind="ExternalInput").ap()
    d_onesf = nc.dram_tensor("onesf", [1, 128], F32, kind="ExternalInput").ap()

    import os as _os
    spy = bool(_os.environ.get("KSPY"))
    d_y = nc.dram_tensor("y_fm", [E, T], F32, kind="ExternalOutput").ap()
    scratch_kind = "ExternalOutput" if (debug or spy) else "Internal"
    d_ctxs = nc.dram_tensor("ctxs", [E, T], BF16, kind=scratch_kind).ap()
    d_s2d = nc.dram_tensor("s2d", [HF, E], BF16, kind=scratch_kind).ap()
    d_rs_spy = nc.dram_tensor("rs_spy", [H, 2, 2, 64, 1024], F32, kind=scratch_kind).ap() if spy else None
    d_krep_spy = nc.dram_tensor("krep_spy", [128, T], BF16, kind=scratch_kind).ap() if spy else None
    d_qlat_spy = nc.dram_tensor("qlat_spy", [128, 8, T], BF16, kind=scratch_kind).ap() if spy else None
    d_v1_spy = nc.dram_tensor("v1_spy", [128, 16 * 128], BF16, kind=scratch_kind).ap() if spy else None
    if debug:
        d_tqlat = nc.dram_tensor("t_qlat", [128, 8, T], BF16, kind="ExternalOutput").ap()
        d_tkrep = nc.dram_tensor("t_krep", [128, T], BF16, kind="ExternalOutput").ap()
        d_tv1 = nc.dram_tensor("t_v1", [128, 16 * 128], BF16, kind="ExternalOutput").ap()
        d_ty1 = nc.dram_tensor("t_y1", [128, NB, 8, TB], F32, kind="ExternalOutput").ap()
        d_tffn = nc.dram_tensor("t_ffn", [128, NB, 8, TB], F32, kind="ExternalOutput").ap()
        d_ta1 = nc.dram_tensor("t_a1", [128, 32], F32, kind="ExternalOutput").ap()

    from contextlib import ExitStack
    with ExitStack() as ctx:
        s1T_h = ctx.enter_context(nc.sbuf_tensor("s1T", [128, 8, HF], BF16))
        s2b_h = ctx.enter_context(nc.sbuf_tensor("s2b", [128, 2, 2, E], BF16))
        vcomb_h = ctx.enter_context(nc.sbuf_tensor("vcomb", [128, 8, E], BF16))
        ar1_h = ctx.enter_context(nc.sbuf_tensor("arena1", [128, 32768], mybir.dt.uint8))
        ar2_h = ctx.enter_context(nc.sbuf_tensor("arena2", [128, 32768], mybir.dt.uint8))
        ar4_h = ctx.enter_context(nc.sbuf_tensor("arena4", [128, 16384], mybir.dt.uint8))
        v1_h = ctx.enter_context(nc.sbuf_tensor("v1", [128, 16, 128], BF16))
        krep_h = ctx.enter_context(nc.sbuf_tensor("krep", [128, T], BF16))
        q2lb_h = ctx.enter_context(nc.sbuf_tensor("q2lb", [128, 512], BF16))
        vupb_h = ctx.enter_context(nc.sbuf_tensor("vupb", [128, 512], BF16))
        wkvb_h = ctx.enter_context(nc.sbuf_tensor("wkvb", [128, 8, 128], BF16))

        rs_h = ctx.enter_context(nc.sbuf_tensor("rs", [128, 1024], F32))
        rs2_h = ctx.enter_context(nc.sbuf_tensor("rs2", [128, 1024], F32))
        ctxn_h = ctx.enter_context(nc.sbuf_tensor("ctxn", [64, T], BF16))
        a1c_h = ctx.enter_context(nc.sbuf_tensor("a1c", [128, 32], F32))
        a2c_h = ctx.enter_context(nc.sbuf_tensor("a2c", [128, 8], F32))
        b1c_h = ctx.enter_context(nc.sbuf_tensor("b1c_s", [128, 32], F32))
        b2c_h = ctx.enter_context(nc.sbuf_tensor("b2c_s", [128, 8], F32))
        mask_h = ctx.enter_context(nc.sbuf_tensor("mask01_s", [128, 128], BF16))
        ident_h = ctx.enter_context(nc.sbuf_tensor("ident_s", [128, 128], BF16))
        onesb_h = ctx.enter_context(nc.sbuf_tensor("onesb_s", [128, 1], BF16))
        onesf_h = ctx.enter_context(nc.sbuf_tensor("onesf_s", [1, 128], F32))
        hb_h = ctx.enter_context(nc.sbuf_tensor("hb", [128, 4, 256], BF16))
        psA_h = ctx.enter_context(nc.psum_tensor("psA", [128, 2048], F32))
        psS4_h = ctx.enter_context(nc.psum_tensor("psS4", [128, 2048], F32))
        S_dW = ctx.enter_context(nc.semaphore("dW"))
        S_qA = ctx.enter_context(nc.semaphore("qA"))
        S_qB = ctx.enter_context(nc.semaphore("qB"))
        S_qC = ctx.enter_context(nc.semaphore("qC"))
        S_qD = ctx.enter_context(nc.semaphore("qD"))
        S_qE0 = ctx.enter_context(nc.semaphore("qE0"))
        S_qE1 = ctx.enter_context(nc.semaphore("qE1"))
        S_qF0 = ctx.enter_context(nc.semaphore("qF0"))
        S_qF1 = ctx.enter_context(nc.semaphore("qF1"))
        S_qG0 = ctx.enter_context(nc.semaphore("qG0"))
        S_qG1 = ctx.enter_context(nc.semaphore("qG1"))
        S_qH0 = ctx.enter_context(nc.semaphore("qH0"))
        S_qH1 = ctx.enter_context(nc.semaphore("qH1"))
        S_dS2b = ctx.enter_context(nc.semaphore("dS2b"))
        S_dI = ctx.enter_context(nc.semaphore("dI"))
        S_dCW = ctx.enter_context(nc.semaphore("dCW"))
        S_dCW1 = ctx.enter_context(nc.semaphore("dCW1"))
        S_dCR = ctx.enter_context(nc.semaphore("dCR"))
        S_dXR = ctx.enter_context(nc.semaphore("dXR"))
        S_dS2 = ctx.enter_context(nc.semaphore("dS2"))
        S_dSP = ctx.enter_context(nc.semaphore("dSP"))
        S_dO = ctx.enter_context(nc.semaphore("dO"))
        S_pe = ctx.enter_context(nc.semaphore("pe"))
        S_sc = ctx.enter_context(nc.semaphore("sc"))
        S_dv = ctx.enter_context(nc.semaphore("dv"))
        S_gp = ctx.enter_context(nc.semaphore("gp"))
        block = ctx.enter_context(nc.Block())
        SEM = dict(dW=S_dW, dI=S_dI, dCW=S_dCW, dCW1=S_dCW1, dCR=S_dCR, dXR=S_dXR, dS2=S_dS2,
                   dS2b=S_dS2b, dSP=S_dSP, dO=S_dO, pe=S_pe, sc=S_sc, dv=S_dv, gp=S_gp,
                   qA=S_qA, qB=S_qB, qC=S_qC, qD=S_qD, qE0=S_qE0, qE1=S_qE1,
                   qF0=S_qF0, qF1=S_qF1, qG0=S_qG0, qG1=S_qG1, qH0=S_qH0, qH1=S_qH1)

        s1T = s1T_h.ap()
        s1flat = s1T.rearrange("p a b -> p (a b)")
        wqv = s1flat[:, 0:8192].rearrange("p (c e) -> p c e", c=8)
        woutv = s1flat[:, 8192:16384].rearrange("p (c e) -> p c e", c=8)
        s2b = s2b_h.ap()
        s2stage = s2b.rearrange("p a b e -> p (a b e)")
        vcomb = vcomb_h.ap()

        ar1 = ar1_h.ap()
        qlat = ar1[:, :].bitcast(BF16).rearrange("p (c t) -> p c t", c=8)
        xin = [ar1[:, 0:8192].bitcast(F32).rearrange("p (c t) -> p c t", c=8),
               ar1[:, 8192:16384].bitcast(F32).rearrange("p (c t) -> p c t", c=8)]
        y1f = ar1[:, 16384:24576].bitcast(F32).rearrange("p (c t) -> p c t", c=8)
        res2f = ar1[:, 24576:32768].bitcast(F32).rearrange("p (c t) -> p c t", c=8)

        ar2 = ar2_h.ap()
        xb = ar2[:, :].bitcast(BF16).rearrange("p (c t) -> p c t", c=8)
        wstage = [ar2[:, 0:8192].bitcast(BF16),
                  ar2[:, 8192:16384].bitcast(BF16)]
        wnstage = [ar2[:, 16384:24576].bitcast(BF16).rearrange("p (r e) -> p r e", r=4),
                   ar2[:, 24576:32768].bitcast(BF16).rearrange("p (r e) -> p r e", r=4)]
        vfm = ar2[0:64, 0:4096].bitcast(BF16)
        y2 = ar2[:, 0:8192].bitcast(F32).rearrange("p (c t) -> p c t", c=8)
        ffnout = ar2[:, 8192:16384].bitcast(F32).rearrange("p (c t) -> p c t", c=8)
        strips = ar2[0:1, 16384:28672].bitcast(F32)
        S_MEAN = (0, 256)
        S_EX2 = (256, 512)
        S_VAR = (512, 768)
        S_Y = (768, 1024)
        S_T = (1024, 1280)
        S_NMR = (1280, 1536)

        def strip(rg):
            return strips[:, rg[0]:rg[1]]

        def strip_i(rg):
            return strips[:, rg[0]:rg[1]].bitcast(I32)

        ar4 = ar4_h.ap()
        wq2l = ar4[:, :].bitcast(BF16).rearrange("p (c e) -> p c e", c=8)
        cin = ar4[:, 0:4096].bitcast(BF16).rearrange("p (c t) -> p c t", c=8)
        xresb = ar4[:, 4096:8192].bitcast(BF16).rearrange("p (c t) -> p c t", c=8)
        xsq = ar4[:, 8192:12288].bitcast(BF16).rearrange("p (c t) -> p c t", c=8)
        y1b = [ar4[:, 12288:16384].bitcast(BF16).rearrange("p (c t) -> p c t", c=8),
               ar2[:, 28672:32768].bitcast(BF16).rearrange("p (c t) -> p c t", c=8)]

        v1 = v1_h.ap()
        krep = krep_h.ap()
        q2lb = q2lb_h.ap()
        vupb = vupb_h.ap()
        wkvb = wkvb_h.ap()
        ptS_h = ctx.enter_context(nc.sbuf_tensor("ptS", [128, 4096], BF16))
        ptS = ptS_h.ap()
        y1fb = [y1f, ptS_h.ap().bitcast(F32).rearrange("p (c t) -> p c t", c=8)]
        rs = rs_h.ap()
        rs2 = rs2_h.ap()
        ctxn = ctxn_h.ap()
        a1c = a1c_h.ap()
        a2c = a2c_h.ap()
        b1c = b1c_h.ap()
        b2c = b2c_h.ap()
        mask01 = mask_h.ap()
        ident = ident_h.ap()
        onesb = onesb_h.ap()
        onesf = onesf_h.ap()
        hb = hb_h.ap()

        psA = psA_h.ap()
        psS4 = psS4_h.ap()
        psB = psS4[:, 0:512]
        psC = psS4[:, 512:1024]
        psD = psS4[:, 1024:1536]
        psE = psS4[:, 1536:2048]
        psB_bf = psB.bitcast(BF16)
        psC_bf = psC.bitcast(BF16)

        P = Plan()

        for dst, src in ((mask01, d_mask), (ident, d_ident), (onesb, d_onesb),
                         (onesf, d_onesf), (b1c, d_b1), (b2c, d_b2)):
            P.add("sy", lambda e, a=dst, b=src: e.dma_start(a, b), inc=("dI", 16))
        V_CONST = P.val("dI")

        def gload(dst, src, sem, waits=()):
            return P.add("gp", lambda e, a=dst, b=src: e.dma_start(a, b),
                         waits=waits, inc=(sem, 16))

        V_XB = gload(xb, d_x.rearrange("(c p) t -> p c t", p=128), "qA")
        V_WQ = gload(wqv, d_wq.rearrange("(c p) t -> p c t", p=128), "qB")
        V_Q2L = gload(q2lb, d_q2l, "qB")
        V_WKV = gload(wkvb, d_wkv.rearrange("(c p) t -> p c t", p=128), "qC")
        V_WOUT = gload(woutv, d_wout.rearrange("(c p) t -> p c t", p=128), "qD")
        V_VUP = gload(vupb, d_vup, "qD")

        if debug and phase >= 1:
            dbg_xb = nc.dram_tensor("t_xb", [128, 8, T], BF16, kind="ExternalOutput").ap()
            P.add("sy", lambda e: e.dma_start(dbg_xb, xb), waits=[("qA", 16)], inc=("dO", 16))
        v_evac_wq2l = [0] * 8
        for ec in (range(8) if phase >= 2 else ()):
            waits = [("qB", 32)]
            if ec >= 2:
                waits.append(("sc", v_evac_wq2l[ec - 2]))
            for h in range(H):
                po = (h % 2) * 64
                col = (ec % 2) * 1024 + (h % 2) * 512 + (h // 2) * 64
                def mm(e, ec=ec, h=h, po=po, col=col):
                    return e.matmul(
                        psA[:, col:col + 64],
                        wqv[po:po + 64, h // 2, ec * 128:(ec + 1) * 128],
                        q2lb[po:po + 64, (h // 2) * 64:(h // 2) * 64 + 64],
                        start=True, stop=True)
                P.add("pe", mm, waits=waits if h == 0 else (), inc=("pe", 1) if h == H - 1 else None)
            vp = P.val("pe")
            v_evac_wq2l[ec] = P.add(
                "sc",
                lambda e, ec=ec: e.activation(
                    wq2l[:, ec, :].rearrange("p (c a l) -> p c a l", a=2, c=8, l=64),
                    psA[:, (ec % 2) * 1024:(ec % 2) * 1024 + 1024].rearrange(
                        "p (a c l) -> p c a l", a=2, c=8, l=64),
                    AF.Copy, scale=0.125),
                waits=[("pe", vp)], inc=("sc", 1))

        if debug and phase >= 2:
            dbg_wq2l = nc.dram_tensor("t_wq2l", [128, 8, E], BF16, kind="ExternalOutput").ap()
            P.add("sy", lambda e: e.dma_start(dbg_wq2l, wq2l), waits=[("sc", v_evac_wq2l[7])], inc=("dO", 16))
        v_evac_vcomb = [0] * 8
        for c in (range(8) if phase >= 3 else ()):
            waits = [("qD", 32), ("sc", v_evac_wq2l[7])]
            if c >= 2:
                waits = [("sc", v_evac_vcomb[c - 2])]
            first = True
            for h, po in ((2 * c, 0), (2 * c + 1, 64)):
                for nb2 in range(2):
                    def mm(e, c=c, h=h, po=po, nb2=nb2):
                        return e.matmul(
                            psA[po:po + 64, (c % 2) * 1024 + nb2 * 512:(c % 2) * 1024 + nb2 * 512 + 512],
                            vupb[po:po + 64, (h // 2) * 64:(h // 2) * 64 + 64],
                            woutv[po:po + 64, h // 2, nb2 * 512:nb2 * 512 + 512],
                            start=True, stop=True,
                            tile_position=(po, po))
                    last = (h % 2 == 1 and nb2 == 1)
                    P.add("pe", mm, waits=waits if first else (), inc=("pe", 1) if last else None)
                    first = False
            vp = P.val("pe")
            v_evac_vcomb[c] = P.add(
                "sc",
                lambda e, c=c: e.copy(vcomb[:, c, :], psA[:, (c % 2) * 1024:(c % 2) * 1024 + 1024]),
                waits=[("pe", vp)], inc=("sc", 1))

        if debug and phase >= 3:
            dbg_vc = nc.dram_tensor("t_vcomb", [128, 8, E], BF16, kind="ExternalOutput").ap()
            P.add("sy", lambda e: e.dma_start(dbg_vc, vcomb), waits=[("sc", v_evac_vcomb[7])], inc=("dO", 16))
        v_qlat_evac = []
        unit = 0
        for mc in (range(8) if phase >= 4 else ()):
            for th in range(2):
                half = unit % 2
                waits = [("qA", 16), ("sc", v_evac_vcomb[7])]
                if len(v_qlat_evac) >= 2:
                    eng, val = v_qlat_evac[unit - 2]
                    waits.append((eng, val))
                for tc2 in range(2):
                    for kc in range(8):
                        def mm(e, mc=mc, th=th, half=half, tc2=tc2, kc=kc):
                            return e.matmul(
                                psA[:, half * 1024 + tc2 * 512: half * 1024 + tc2 * 512 + 512],
                                wq2l[:, kc, mc * 128:(mc + 1) * 128],
                                xb[:, kc, th * 1024 + tc2 * 512: th * 1024 + tc2 * 512 + 512],
                                start=(kc == 0), stop=(kc == 7))
                        P.add("pe", mm,
                              waits=waits if (tc2 == 0 and kc == 0) else (),
                              inc=("pe", 1) if (tc2 == 1 and kc == 7) else None)
                vp = P.val("pe")
                ev_eng = "dv"
                def ev(e, mc=mc, th=th, half=half, ev_eng=ev_eng):
                    if ev_eng == "dv":
                        return e.tensor_copy(qlat[:, mc, th * 1024: th * 1024 + 1024],
                                             psA[:, half * 1024: half * 1024 + 1024])
                    return e.copy(qlat[:, mc, th * 1024: th * 1024 + 1024],
                                  psA[:, half * 1024: half * 1024 + 1024])
                val = P.add(ev_eng, ev, waits=[("pe", vp)], inc=(ev_eng, 1))
                v_qlat_evac.append((ev_eng, val))
                unit += 1

        kv_waits = [("qC", 16)] + [(e, v) for e, v in v_qlat_evac[-2:]]
        for tc in (range(4) if phase >= 5 else ()):
            for kc in range(8):
                def mm(e, tc=tc, kc=kc):
                    return e.matmul(psA[:, tc * 512:tc * 512 + 512],
                                    wkvb[:, kc, :],
                                    xb[:, kc, tc * 512:tc * 512 + 512],
                                    start=(kc == 0), stop=(kc == 7))
                P.add("pe", mm, waits=kv_waits if (tc == 0 and kc == 0) else (),
                      inc=("pe", 1) if (tc == 3 and kc == 7) else None)
        V_KV = P.val("pe")

        if phase < 5:
            V_KV = None
        V_K0 = (P.add("sc", lambda e: e.copy(krep[0:64, :], psA[0:64, :]),
                     waits=[("pe", V_KV)], inc=("sc", 1)) if phase >= 5 else None)
        V_K1 = (P.add("sc", lambda e: e.copy(krep[64:128, :], psA[0:64, :]), inc=("sc", 1)) if phase >= 5 else None)
        V_VF = (P.add("sc", lambda e: e.copy(vfm, psA[64:128, :]), inc=("sc", 1)) if phase >= 5 else None)

        V_ONES = (P.add("dv", lambda e: e.memset(v1[:, :, 64:128], 1.0), inc=("dv", 1)) if phase >= 5 else None)

        v_tr = [0] * 16
        for jt in (range(16) if phase >= 5 else ()):
            waits = [("sc", V_VF)]
            if jt == 0:
                waits.append(("dI", V_CONST))
            if jt >= 2:
                waits.append(("dv", v_tr[jt - 2]))
            def tr(e, jt=jt):
                dst = psB_bf if jt % 2 == 0 else psC_bf
                return e.transpose(dst[:, 0:64],
                                   vfm[:, jt * 128:(jt + 1) * 128],
                                   ident[0:64, 0:64])
            P.add("pe", tr, waits=waits, inc=("pe", 1))
            vp = P.val("pe")
            def cp(e, jt=jt):
                srcb = psB_bf if jt % 2 == 0 else psC_bf
                return e.tensor_copy(v1[:, jt, 0:64], srcb[:, 0:64])
            v_tr[jt] = P.add("dv", cp, waits=[("pe", vp)], inc=("dv", 1))
        V_V1 = v_tr[15] if phase >= 5 else None
        v_tr_last = V_V1

        if debug and phase >= 4:
            P.add("sy", lambda e: e.dma_start(d_tqlat, qlat),
                  waits=[(e, v) for e, v in v_qlat_evac[-2:]], inc=("dO", 16))
        if debug and phase >= 5:
            P.add("sy", lambda e: e.dma_start(d_tkrep, krep), waits=[("sc", V_K1)], inc=("dO", 16))
            P.add("sy", lambda e: e.dma_start(d_tv1, v1.rearrange("p a b -> p (a b)")),
                  waits=[("dv", V_V1), ("dv", V_ONES)], inc=("dO", 16))

        PR = {}
        v_a2red = [0] * 8
        v_s2sign = [0] * 8
        v_a1red = [0] * 32
        v_s1sign = [0] * 8

        def prep_gen():
            for c in range(8):
                waits = [("pe", V_KV)]
                if c >= 2:
                    waits.append(("dv", v_a2red[c - 2]))
                vdma = P.add("gp",
                             lambda e, c=c: e.dma_start(
                                 wnstage[c % 2].rearrange("p r e -> p (r e)"),
                                 d_w2n[c * 128:(c + 1) * 128, :]),
                             waits=waits, inc=(f"qE{c % 2}", 16))
                yield
                v_a2red[c] = P.add("dv",
                                   lambda e, c=c: e.reduce_sum(
                                       a2c[:, c:c + 1],
                                       wnstage[c % 2].rearrange("p r e -> p (r e)"),
                                       axis=AX.X, apply_absolute_value=True),
                                   waits=[(f"qE{c % 2}", vdma)], inc=("dv", 1))
            PR["V_A2"] = P.add("dv",
                               lambda e: e.tensor_scalar(a2c, a2c, 1.0 / HF, None, op0=ALU.mult),
                               waits=[("dv", v_a2red[7])], inc=("dv", 1))
            for g in range(8):
                waits = []
                if g == 0:
                    waits.append(("dv", v_tr_last))
                if g >= 2:
                    waits.append(("dv", v_s2sign[g - 2]))
                vdma = P.add("gp",
                             lambda e, g=g: e.dma_start(
                                 wstage[g % 2].rearrange("p (r e) -> p r e", r=4),
                                 d_w2t[g * 512:(g + 1) * 512, :].rearrange("(r p) e -> p r e", p=128)),
                             waits=waits, inc=(f"qF{g % 2}", 16))
                yield
                w = [(f"qF{g % 2}", vdma)]
                if g >= 1:
                    w.append(("dSP", 16 * g))
                v_s2sign[g] = P.add("dv",
                      lambda e, g=g: e.tensor_scalar(
                          s2stage.bitcast(mybir.dt.int16), wstage[g % 2].bitcast(mybir.dt.int16),
                          0x8000 - 0x10000, 0x3F80, op0=ALU.bitwise_and, op1=ALU.bitwise_or),
                      waits=w, inc=("dv", 1))
                P.add("gp",
                      lambda e, g=g: e.dma_start(
                          d_s2d[g * 512:(g + 1) * 512, :].rearrange("(r p) e -> p r e", p=128),
                          s2stage.rearrange("p (r e) -> p r e", r=4)),
                      waits=[("dv", v_s2sign[g])], inc=("dSP", 16))
            for g in range(8):
                waits = []
                if g < 2:
                    waits.append(("dv", v_a2red[6 + g]))
                if g >= 2:
                    waits.append(("dv", v_a1red[4 * (g - 2) + 3]))
                vdma = P.add("gp",
                             lambda e, g=g: e.dma_start(
                                 wnstage[g % 2],
                                 d_w1n[g * 512:(g + 1) * 512, :].rearrange("(r p) e -> p r e", p=128)),
                             waits=waits, inc=(f"qG{g % 2}", 16))
                yield
                for r in range(4):
                    v_a1red[4 * g + r] = P.add(
                        "dv",
                        lambda e, g=g, r=r: e.reduce_sum(a1c[:, 4 * g + r:4 * g + r + 1],
                                                         wnstage[g % 2][:, r, :],
                                                         axis=AX.X, apply_absolute_value=True),
                        waits=[(f"qG{g % 2}", vdma)] if r == 0 else (), inc=("dv", 1))
            PR["V_A1"] = P.add("dv",
                               lambda e: e.tensor_scalar(a1c, a1c, 1.0 / E, None, op0=ALU.mult),
                               waits=[("dv", v_a1red[31])], inc=("dv", 1))
            for c in range(8):
                waits = []
                if c < 2:
                    waits.append(("dv", v_s2sign[6 + c]))
                if c >= 2:
                    waits.append(("dv", v_s1sign[c - 2]))
                vdma = P.add("gp",
                             lambda e, c=c: e.dma_start(wstage[c % 2], d_w1t[c * 128:(c + 1) * 128, :]),
                             waits=waits, inc=(f"qH{c % 2}", 16))
                yield
                v_s1sign[c] = P.add("dv",
                                    lambda e, c=c: e.tensor_scalar(
                                        s1T[:, c, :].bitcast(mybir.dt.int16), wstage[c % 2].bitcast(mybir.dt.int16),
                                        0x8000 - 0x10000, 0x3F80,
                                        op0=ALU.bitwise_and, op1=ALU.bitwise_or),
                                    waits=[(f"qH{c % 2}", vdma)], inc=("dv", 1))
            if debug:
                P.add("sy", lambda e: e.dma_start(d_ta1, a1c), waits=[("dv", PR["V_A1"])], inc=("dO", 16))
            while True:
                yield

        if phase >= 6:
            _pg = prep_gen()

            def emit_prep_slice(n):
                for _ in range(n):
                    next(_pg)
        else:
            def emit_prep_slice(n):
                pass

        if phase == 6:
            emit_prep_slice(64)
            V_SPILLS = 16 * 8
        if phase < 7:
            v_head_mult = None
        if phase >= 7:
            sidx = 0
            stripe_hist = []
            v_head_mult = [0] * H
            prev_srecip_mult = {0: None, 1024: None}
            prev_rs_read = {0: None, 1024: None}
            v_head_ln = [0] * H
            v_spill = [0] * H

            def half_stripes(jt, half):
                lo, hi = half * 1024, half * 1024 + 1024
                out = []
                p = max(jt * 128, lo)
                while p < hi:
                    e = min((p // 512 + 1) * 512, hi)
                    out.append((p, e))
                    p = e
                return out

            mask_gated = [False]
            pending_epi = [None]
            spill_idx = {}
            n_spilled = 0
            prev_half_guard = None

            for hp in range(8):
                hA, hB = 2 * hp, 2 * hp + 1
                for half in range(2):
                    pending_pv = []
                    LAG = 3
                    first_pv = True
                    jts = range(8) if half == 0 else range(16)
                    for jt in jts:
                        for (qs, qe) in half_stripes(jt, half):
                            w = qe - qs
                            pu = sidx // 2
                            slotA = (pu % 2) * 2
                            off = qs % 512
                            base = (pu % 4) * 1024
                            pbase = slotA * 512
                            psS_A = (psB, psC, psD, psE)[slotA]
                            psS_B = (psB, psC, psD, psE)[slotA + 1]
                            qkv = []
                            for head, po, psX in ((hA, 0, psS_A), (hB, 64, psS_B)):
                                waits = []
                                if head == hA and pu >= 2:
                                    waits.append(("sc", stripe_hist[pu - 2]["exp"]))
                                if sidx == 0:
                                    waits.append(("sc", V_K1))
                                    waits.append(("dv", V_V1))
                                    waits.append(("dv", V_ONES))
                                def qk(e, po=po, hp=hp, jt=jt, qs=qs, qe=qe, psX=psX, off=off, w=w):
                                    return e.matmul(psX[:, off:off + w],
                                                    krep[po:po + 64, jt * 128:(jt + 1) * 128],
                                                    qlat[po:po + 64, hp, qs:qe],
                                                    start=True, stop=True, skip_group_check=True)
                                qkv.append(P.add("pe", qk, waits=waits, inc=("pe", 1)))
                                sidx += 1
                            ew = [("pe", qkv[1])]
                            if pu >= 4:
                                assert stripe_hist[pu - 4]["pv"] is not None
                                ew.append(("pe", stripe_hist[pu - 4]["pv"]))
                            ew_w = 1024 - off
                            vexp = P.add("sc",
                                         lambda e, base=base, off=off, ew_w=ew_w, pbase=pbase: e.activation(
                                             ptS[:, base + off:base + off + ew_w],
                                             psS4[:, pbase + off:pbase + off + ew_w], AF.Exp),
                                         waits=ew, inc=("sc", 1))
                            if pending_epi[0] is not None:
                                pending_epi[0]()
                                pending_epi[0] = None
                            vmask = None
                            if qs == jt * 128:
                                mw0 = [("sc", vexp)]
                                if not mask_gated[0]:
                                    mw0.append(("dI", V_CONST))
                                    mask_gated[0] = True
                                P.add("gp",
                                      lambda e, base=base, off=off: e.tensor_tensor(
                                          ptS[:, base + off:base + off + 128],
                                          ptS[:, base + off:base + off + 128],
                                          mask01, op=ALU.mult),
                                      waits=mw0)
                                vmask = P.add("gp",
                                              lambda e, base=base, off=off: e.tensor_tensor(
                                                  ptS[:, base + 512 + off:base + 512 + off + 128],
                                                  ptS[:, base + 512 + off:base + 512 + off + 128],
                                                  mask01, op=ALU.mult),
                                              waits=[("sc", vexp)], inc=("gp", 1))
                            pw = [("sc", vexp)]
                            if vmask is not None:
                                pw.append(("gp", vmask))
                            if first_pv and prev_half_guard is not None:
                                pw.extend(prev_half_guard)
                            first_pv = False
                            def pvA(e, jt=jt, qs=qs, w=w, base=base, off=off, half=half):
                                return e.matmul(psA[:, qs - half * 1024:qs - half * 1024 + w],
                                                v1[:, jt, :], ptS[:, base + off:base + off + w],
                                                start=(jt == 0), stop=False, skip_group_check=True)
                            def pvB(e, jt=jt, qs=qs, w=w, base=base, off=off, half=half):
                                return e.matmul(psA[:, 1024 + qs - half * 1024:1024 + qs - half * 1024 + w],
                                                v1[:, jt, :], ptS[:, base + 512 + off:base + 512 + off + w],
                                                start=(jt == 0), stop=False, skip_group_check=True)
                            hist = dict(exp=vexp, pv=None)
                            stripe_hist.append(hist)
                            pending_pv.append((pvA, pw, None))
                            pending_pv.append((pvB, [], hist))
                            while len(pending_pv) > 2 * LAG:
                                assert pending_epi[0] is None
                                fn, fw, fh = pending_pv.pop(0)
                                vv = P.add("pe", fn, waits=fw, inc=("pe", 1))
                                if fh is not None:
                                    fh["pv"] = vv
                        if half == 0 and jt == 3:
                            emit_prep_slice(1)
                        if half == 1 and jt in (3, 9):
                            emit_prep_slice(1)
                    while pending_pv:
                        fn, fw, fh = pending_pv.pop(0)
                        vv = P.add("pe", fn, waits=fw, inc=("pe", 1))
                        if fh is not None:
                            fh["pv"] = vv
                    vpv_last = stripe_hist[-1]["pv"]
                    def run_epilogue(hA=hA, hB=hB, half=half, vpv_last=vpv_last):
                        nonlocal prev_half_guard, n_spilled
                        guard = []
                        vexp_h = {}
                        for head, cb in ((hA, 0), (hB, 1024)):
                            lnw = [("pe", vpv_last)] if cb == 0 else []
                            if prev_rs_read[cb] is not None:
                                lnw.append(("sc", prev_rs_read[cb]))
                            vexp_h[cb] = P.add("sc",
                                  lambda e, cb=cb: e.activation(rs[cb // 16:cb // 16 + 64, :],
                                                                psA[64:128, cb:cb + 1024], AF.Ln),
                                  waits=lnw, inc=("sc", 1))
                        for head, cb in ((hA, 0), (hB, 1024)):
                            ew2 = [("sc", vexp_h[cb])]
                            if prev_srecip_mult[cb] is not None:
                                ew2.append(("dv", prev_srecip_mult[cb]))
                            vexp_h[cb] = prev_rs_read[cb] = P.add("sc",
                                        lambda e, cb=cb: e.activation(rs2[cb // 16:cb // 16 + 64, :],
                                                                      rs[cb // 16:cb // 16 + 64, :],
                                                                      AF.Exp, scale=-1.0),
                                        waits=ew2, inc=("sc", 1))
                        for head, cb in ((hA, 0), (hB, 1024)):
                            vln = vexp_h[cb]
                            mw = [("sc", vln)]
                            if (head, half, "prevspill") in spill_idx:
                                csem = "dCW" if head % 2 == 0 else "dCW1"
                                mw.append((csem, spill_idx[(head, half, "prevspill")]))
                            def mult(e, cb=cb, head=head):
                                return e.tensor_tensor(
                                    ctxn[0:64, (head % 2) * 1024:(head % 2) * 1024 + 1024],
                                    psA[0:64, cb:cb + 1024],
                                    rs2[cb // 16:cb // 16 + 64, :], op=ALU.mult)
                            vm = P.add("dv", mult, waits=mw, inc=("dv", 1))
                            prev_srecip_mult[cb] = vm
                            guard.append(("dv", vm))
                            guard.append(("sc", vln))
                            n_spilled += 1
                            vsp = P.add("sy",
                                        lambda e, head=head, half=half: e.dma_start(
                                            d_ctxs[head * 64:(head + 1) * 64, half * 1024:half * 1024 + 1024],
                                            ctxn[0:64, (head % 2) * 1024:(head % 2) * 1024 + 1024]),
                                        waits=[("dv", vm)],
                                        inc=("dCW" if head % 2 == 0 else "dCW1", 16))
                            nxt = (head, half + 1) if half == 0 else (head + 2, 0)
                            spill_idx[(nxt[0], nxt[1], "prevspill")] = vsp
                        prev_half_guard = guard
                    pending_epi[0] = run_epilogue
                if hp == 1:
                    V_XIN0G = stripe_hist[-1]["exp"]
                emit_prep_slice(1)

        if phase >= 7:
            if pending_epi[0] is not None:
                pending_epi[0]()
                pending_epi[0] = None
            emit_prep_slice(40)
            V_A1 = PR["V_A1"]
            V_A2 = PR["V_A2"]
            V_S1 = v_s1sign[7]
            V_SPILLS = 16 * 8

        if phase >= 8:
            v_outproj = {}
            v_res1 = {}
            v_epi = [0] * NB
            v_ffn2_grp = {}
            v_sums1 = [0] * NB
            v_sums2 = [0] * NB
            v_strips1 = [0] * NB
            v_strips2 = [0] * NB
            v_bcast1 = [0] * NB
            v_bcast2 = [0] * NB
            v_apply2 = [0] * NB
            v_apply1d3 = [0] * NB
            v_y2out = [0] * NB
            v_vr2b = [0] * NB
            v_vr2 = [0] * NB
            v_vr2_mc = {}
            v_ffn2_last = {}
            v_epi_mc = {}
            v_y1bc = {}
            v_r2sq = [0] * NB
            v_xsq = [0] * NB
            v_cin = [0] * (NB + 1)
            v_xin = [0] * (NB + 1)
            v_s2rd = {}
            v_gelu_hist = {}
            NG = 16

            def strip(bank, rg):
                o = bank * 1536
                return strips[:, o + rg[0]:o + rg[1]]

            def strip_i(bank, rg):
                return strip(bank, rg).bitcast(I32)

            def emit_strips(bank, vcopy_val):
                va = P.add("dv",
                           lambda e: e.tensor_tensor(strip(bank, S_T), strip(bank, S_MEAN),
                                                     strip(bank, S_MEAN), op=ALU.mult),
                           waits=[("sc", vcopy_val)], inc=("dv", 1))
                vb = P.add("dv",
                           lambda e: e.scalar_tensor_tensor(strip(bank, S_VAR), strip(bank, S_EX2),
                                                            EPS, strip(bank, S_T),
                                                            op0=ALU.add, op1=ALU.subtract),
                           waits=[("dv", va)], inc=("dv", 1))
                vc1 = P.add("sc",
                            lambda e: e.activation(strip(bank, S_T), strip(bank, S_VAR),
                                                   AF.Ln),
                            waits=[("dv", vb)], inc=("sc", 1))
                vc2 = P.add("sc",
                            lambda e: e.activation(strip(bank, S_Y), strip(bank, S_T),
                                                   AF.Exp, scale=-0.5),
                            waits=[("sc", vc1)], inc=("sc", 1))
                vd = P.add("dv",
                           lambda e: e.scalar_tensor_tensor(strip(bank, S_NMR), strip(bank, S_MEAN),
                                                            -1.0, strip(bank, S_Y),
                                                            op0=ALU.mult, op1=ALU.mult),
                           waits=[("sc", vc2)], inc=("dv", 1))
                return vd

            def emit_outproj_mc(bb, mc):
                waits = []
                if mc == 0:
                    waits.append(("dCR", v_cin[bb]))
                    waits.append(("sc", v_evac_vcomb[7]))
                    if bb >= 1:
                        waits.append(("dv", v_res1[(bb - 1, 7)]))
                        waits.append(("dv", v_apply2[bb - 2]) if bb >= 2 else ("dv", v_res1[(bb - 1, 7)]))
                else:
                    waits.append(("dv", v_res1[(bb, mc - 1)]))
                for kc in range(8):
                    def mm(e, mc=mc, kc=kc):
                        return e.matmul(psE[:, 0:256],
                                        vcomb[:, kc, mc * 128:(mc + 1) * 128],
                                        cin[:, kc, :],
                                        start=(kc == 0), stop=(kc == 7))
                    P.add("pe", mm, waits=waits if kc == 0 else (),
                          inc=("pe", 1) if kc == 7 else None)
                v_outproj[(bb, mc)] = P.val("pe")
                rw = [("pe", v_outproj[(bb, mc)]), ("dXR", v_xin[bb])]
                if mc == 0 and bb >= 2:
                    rw.append(("gp", v_r2sq[bb - 2]))
                    rw.append(("pe", v_sums2[bb - 2]))
                def r1(e, bb=bb, mc=mc):
                    return e.tensor_tensor(xresb[:, mc, :], psE[:, 0:256],
                                           xin[bb % 2][:, mc, :], op=ALU.add)
                v_res1[(bb, mc)] = P.add("dv", r1, waits=rw, inc=("dv", 1))

            def emit_outproj_tail(bb):
                xw = [("dv", v_res1[(bb, 7)])]
                if bb >= 2:
                    xw.append(("pe", v_sums2[bb - 2]))
                v_xsq[bb] = P.add("gp",
                                  lambda e: e.tensor_tensor(
                                      xsq.rearrange("p c t -> p (c t)"),
                                      xresb.rearrange("p c t -> p (c t)"),
                                      xresb.rearrange("p c t -> p (c t)"), op=ALU.mult),
                                  waits=xw, inc=("gp", 1))

            def emit_outproj(bb):
                for mc in range(8):
                    emit_outproj_mc(bb, mc)
                emit_outproj_tail(bb)

            def emit_sums1(bb):
                sw = [("dv", v_res1[(bb, 7)]), ("dI", V_CONST)]
                if bb >= 2:
                    sw.append(("dv", v_apply2[bb - 2]))
                for mc in range(8):
                    P.add("pe",
                          lambda e, mc=mc: e.matmul(psE[0:1, 0:256], onesb, xresb[:, mc, :],
                                                    start=(mc == 0), stop=(mc == 7)),
                          waits=sw if mc == 0 else ())
                for mc in range(8):
                    P.add("pe",
                          lambda e, mc=mc: e.matmul(psE[0:1, 256:512], onesb, xsq[:, mc, :],
                                                    start=(mc == 0), stop=(mc == 7)),
                          waits=[("gp", v_xsq[bb])] if mc == 0 else (),
                          inc=("pe", 1) if mc == 7 else None)
                v_sums1[bb] = P.val("pe")
                vcopy = P.add("sc",
                              lambda e: e.activation(strip(0, (0, 512)), psE[0:1, 0:512],
                                                     AF.Copy, scale=1.0 / E),
                              waits=[("pe", v_sums1[bb])] + ([("dv", PR["V_A1"])] if bb == 0 else []),
                              inc=("sc", 1))
                v_strips1[bb] = emit_strips(0, vcopy)

            def emit_bcast1(bb):
                P.add("pe", lambda e: e.matmul(psE[:, 0:256], onesf, strip(0, S_Y), start=True, stop=True),
                      waits=[("dv", v_strips1[bb])])
                P.add("pe", lambda e: e.matmul(psE[:, 256:512], onesf, strip(0, S_NMR), start=True, stop=True),
                      inc=("pe", 1))
                v_bcast1[bb] = P.val("pe")

            def emit_apply1(bb):
                yf = y1fb[bb % 2]
                vd1 = {}
                vd2 = {}
                def emit_d1(mc):
                    d1w = [("dv", v_res1[(bb, mc)])]
                    if mc == 0:
                        d1w.append(("pe", v_bcast1[bb]))
                    if bb >= 2:
                        d1w.append(("dv", v_vr2_mc[(bb - 2, mc)]))
                    vd1[mc] = P.add("dv",
                                    lambda e, mc=mc: e.tensor_tensor(yf[:, mc, :], xresb[:, mc, :],
                                                                     psE[:, 0:256], op=ALU.mult),
                                    waits=d1w, inc=("dv", 1))
                def emit_d2(mc):
                    vd2[mc] = P.add("dv",
                                    lambda e, mc=mc: e.tensor_tensor(yf[:, mc, :], yf[:, mc, :],
                                                                     psE[:, 256:512], op=ALU.add),
                                    waits=[("dv", vd1[mc])], inc=("dv", 1))
                    v_y1bc[(bb, mc)] = P.add("sc",
                                             lambda e, mc=mc, bb=bb: e.copy(y1b[bb % 2][:, mc, :],
                                                                            yf[:, mc, :]),
                                             waits=[("dv", vd2[mc])], inc=("sc", 1))
                emit_d1(0)
                for mc in range(1, 8):
                    emit_d1(mc)
                    emit_d2(mc - 1)
                emit_d2(7)
                v_apply1d3[bb] = v_y1bc[(bb, 7)]
                if debug:
                    P.add("sy", lambda e, bb=bb: e.dma_start(d_ty1[:, bb, :, :], y1fb[bb % 2]),
                          waits=[("sc", v_apply1d3[bb])], inc=("dO", 16))

            def emit_res2part1(bb):
                for mc in range(8):
                    def ep(e, mc=mc):
                        return e.activation(ffnout[:, mc, :], psA[:, mc * 256:mc * 256 + 256],
                                            AF.Identity, bias=b2c[:, mc:mc + 1], scale=a2c[:, mc:mc + 1])
                    v_epi_mc[(bb, mc)] = P.add(
                        "sc", ep,
                        waits=[("pe", v_ffn2_last[(bb, mc)])] + ([("dv", PR["V_A2"])] if mc == 0 else []),
                        inc=("sc", 1))
                v_epi[bb] = v_epi_mc[(bb, 7)]
                if debug:
                    P.add("sy", lambda e, bb=bb: e.dma_start(d_tffn[:, bb, :, :], ffnout),
                          waits=[("sc", v_epi[bb])], inc=("dO", 16))
                for mc in range(8):
                    v_vr2_mc[(bb, mc)] = P.add(
                        "dv",
                        lambda e, mc=mc, bb=bb: e.tensor_tensor(res2f[:, mc, :], ffnout[:, mc, :],
                                                                y1fb[bb % 2][:, mc, :], op=ALU.add),
                        waits=[("sc", v_epi_mc[(bb, mc)])], inc=("dv", 1))
                v_vr2[bb] = v_vr2_mc[(bb, 7)]

            def emit_res2part2(bb):
                nxt = bb + 1 if bb + 1 < NB else bb
                v_vr2b[bb] = P.add("dv",
                                   lambda e: e.tensor_copy(xresb.rearrange("p c t -> p (c t)"),
                                                           res2f.rearrange("p c t -> p (c t)")),
                                   waits=[("pe", v_sums1[nxt]), ("gp", v_xsq[nxt]), ("dv", v_vr2[bb])],
                                   inc=("dv", 1))
                v_r2sq[bb] = P.add("gp",
                                   lambda e: e.tensor_tensor(
                                       xsq.rearrange("p c t -> p (c t)"),
                                       xresb.rearrange("p c t -> p (c t)"),
                                       xresb.rearrange("p c t -> p (c t)"), op=ALU.mult),
                                   waits=[("dv", v_vr2b[bb]), ("pe", v_sums1[nxt])], inc=("gp", 1))

            def emit_sums2(bb):
                sw = [("dv", v_vr2b[bb]), ("sc", v_apply1d3[bb + 1] if bb + 1 < NB else v_apply1d3[bb])]
                for mc in range(8):
                    P.add("pe",
                          lambda e, mc=mc: e.matmul(psE[0:1, 0:256], onesb, xresb[:, mc, :],
                                                    start=(mc == 0), stop=(mc == 7)),
                          waits=sw if mc == 0 else ())
                for mc in range(8):
                    P.add("pe",
                          lambda e, mc=mc: e.matmul(psE[0:1, 256:512], onesb, xsq[:, mc, :],
                                                    start=(mc == 0), stop=(mc == 7)),
                          waits=[("gp", v_r2sq[bb])] if mc == 0 else (),
                          inc=("pe", 1) if mc == 7 else None)
                v_sums2[bb] = P.val("pe")
                vcopy = P.add("sc",
                              lambda e: e.activation(strip(1, (0, 512)), psE[0:1, 0:512],
                                                     AF.Copy, scale=1.0 / E),
                              waits=[("pe", v_sums2[bb])], inc=("sc", 1))
                v_strips2[bb] = emit_strips(1, vcopy)

            def emit_bcast2(bb):
                P.add("pe", lambda e: e.matmul(psE[:, 0:256], onesf, strip(1, S_Y), start=True, stop=True),
                      waits=[("dv", v_strips2[bb])])
                P.add("pe", lambda e: e.matmul(psE[:, 256:512], onesf, strip(1, S_NMR), start=True, stop=True),
                      inc=("pe", 1))
                v_bcast2[bb] = P.val("pe")

            def emit_apply2(bb):
                bca = psE[:, 0:256].unsqueeze(1).broadcast_to([128, 8, 256])
                bcb = psE[:, 256:512].unsqueeze(1).broadcast_to([128, 8, 256])
                a2w = [("pe", v_bcast2[bb]), ("dv", v_vr2[bb])]
                if bb >= 1:
                    a2w.append(("dO", v_y2out[bb - 1]))
                va1 = P.add("dv", lambda e, bca=bca: e.tensor_tensor(y2, res2f, bca, op=ALU.mult),
                            waits=a2w, inc=("dv", 1))
                v_apply2[bb] = P.add("dv", lambda e, bcb=bcb: e.tensor_tensor(y2, y2, bcb, op=ALU.add),
                                     waits=[("dv", va1)], inc=("dv", 1))
                v_y2out[bb] = P.add("sy",
                                    lambda e, bb=bb: e.dma_start(
                                        d_y.rearrange("(c p) t -> p c t", p=128)[:, :, bb * TB:bb * TB + TB], y2),
                                    waits=[("dv", v_apply2[bb])], inc=("dO", 16))

            def emit_ffn_steps(bb, j_lo, j_hi):
                for j in range(j_lo, j_hi):
                    if j < NJ:
                        g = j // 2
                        ps = (psB, psC, psD)[j % 3]
                        w1w = []
                        if j == 0:
                            w1w.append(("dv", V_S1))
                        if j <= 2 and bb >= 1:
                            w1w.append(("sc", v_gelu_hist[{0: 30, 1: 31, 2: 29}[j]]))
                        if j >= 3:
                            w1w.append(("sc", v_gelu_hist[j - 3]))
                        for kc in range(8):
                            kw = list(w1w) if kc == 0 else []
                            if j == 0:
                                kw.append(("sc", v_y1bc[(bb, kc)]))
                            def mm(e, j=j, kc=kc, ps=ps, bb=bb):
                                return e.matmul(ps[:, 0:256],
                                                s1T[:, kc, j * 128:(j + 1) * 128],
                                                y1b[bb % 2][:, kc, :],
                                                start=(kc == 0), stop=(kc == 7))
                            P.add("pe", mm, waits=kw,
                                  inc=("pe", 1) if kc == 7 else None)
                        vf1 = P.val("pe")
                        gw0 = [("pe", vf1), ("dv", PR["V_A1"])]
                        if bb == 0 and j == 0:
                            gw0.append(("dI", V_CONST))
                        vg = P.add("sc",
                                   lambda e, j=j, ps=ps: e.activation(
                                       hb[:, j % 4, :], ps[:, 0:256], AF.Gelu,
                                       bias=b1c[:, j:j + 1], scale=a1c[:, j:j + 1]),
                                   waits=gw0, inc=("sc", 1))
                        v_gelu_hist[j] = vg
                    jj = j - 2
                    if jj >= 0 and jj < NJ:
                        g2 = jj // 2
                        s2sem = "dS2" if g2 % 2 == 0 else "dS2b"
                        f2w = [("sc", v_gelu_hist[jj])]
                        if jj == 0:
                            f2w.append((s2sem, v_s2rd[(bb, g2)]))
                            if bb > 0:
                                f2w.append(("sc", v_epi[bb - 1]))
                            f2w.append(("dv", PR["V_A2"]))
                        elif jj % 2 == 0:
                            f2w.append((s2sem, v_s2rd[(bb, g2)]))
                        for mc in range(8):
                            def mm2(e, jj=jj, mc=mc, g2=g2, bb=bb):
                                return e.matmul(psA[:, mc * 256:mc * 256 + 256],
                                                s2b[:, g2 % 2, jj % 2, mc * 128:(mc + 1) * 128],
                                                hb[:, jj % 4, :],
                                                start=(jj == 0 and mc % 2 == 0),
                                                stop=(jj == NJ - 1), skip_group_check=True)
                            last = (jj == NJ - 1)
                            P.add("pe", mm2, waits=f2w if mc == 0 else (),
                                  inc=("pe", 1) if (mc == 7 or last) else None)
                            if last:
                                v_ffn2_last[(bb, mc)] = P.val("pe")
                        if jj % 2 == 1:
                            gd = jj // 2
                            v_ffn2_grp[(bb, gd)] = P.val("pe")
                            gg, b2 = gd + 2, bb
                            if gg >= NG:
                                gg -= NG
                                b2 += 1
                            if b2 < NB and (b2, gg) not in v_s2rd:
                                v_s2rd[(b2, gg)] = P.add(
                                    "sy",
                                    lambda e, b2=b2, gg=gg: e.dma_start(
                                        s2b[:, gg % 2, :, :],
                                        d_s2d[gg * 256:(gg + 1) * 256, :].rearrange("(j p) e -> p j e", p=128)),
                                    waits=[("pe", v_ffn2_grp[(bb, gd)])],
                                    inc=("dS2" if gg % 2 == 0 else "dS2b", 16))

            for kc in range(8):
                v_cin[0] = P.add("sy",
                                 lambda e, kc=kc: e.dma_start(
                                     cin[:, kc, :],
                                     d_ctxs.rearrange("(c p) t -> p c t", p=128)[:, kc, 0:TB]),
                                 waits=[("dCW", 16 * (2 * kc + 1)), ("dCW1", 16 * (2 * kc + 1))],
                                 inc=("dCR", 16))
            v_xin[0] = P.add("sy",
                             lambda e: e.dma_start(xin[0], d_x.rearrange("(c p) t -> p c t", p=128)[:, :, 0:TB]),
                             waits=[("sc", V_XIN0G)], inc=("dXR", 16))
            for g in range(2):
                v_s2rd[(0, g)] = P.add("sy",
                                       lambda e, g=g: e.dma_start(
                                           s2b[:, g % 2, :, :],
                                           d_s2d[g * 256:(g + 1) * 256, :].rearrange("(j p) e -> p j e", p=128)),
                                       waits=[("dSP", V_SPILLS)],
                                       inc=("dS2" if g % 2 == 0 else "dS2b", 16))

            emit_outproj(0)
            v_cin[1] = P.add("sy",
                             lambda e: e.dma_start(cin, d_ctxs.rearrange("(c p) t -> p c t", p=128)[:, :, TB:2 * TB]),
                             waits=[("pe", v_outproj[(0, 7)])], inc=("dCR", 16))
            v_xin[1] = P.add("sy",
                             lambda e: e.dma_start(xin[1], d_x.rearrange("(c p) t -> p c t", p=128)[:, :, TB:2 * TB]),
                             waits=[("dv", v_res1[(0, 7)])], inc=("dXR", 16))
            emit_sums1(0)
            emit_bcast1(0)
            emit_apply1(0)

            for b in range(NB):
                emit_ffn_steps(b, 0, 8)
                if b >= 1:
                    emit_sums2(b - 1)
                emit_ffn_steps(b, 8, 11)
                if b >= 1:
                    emit_bcast2(b - 1)
                emit_ffn_steps(b, 11, 12)
                if b >= 1:
                    emit_apply2(b - 1)
                emit_ffn_steps(b, 12, 13)
                if b + 1 < NB:
                    for mc in range(8):
                        emit_outproj_mc(b + 1, mc)
                        emit_ffn_steps(b, 13 + mc, 14 + mc)
                    emit_outproj_tail(b + 1)
                    if b + 2 < NB:
                        v_cin[b + 2] = P.add("sy",
                                             lambda e, b=b: e.dma_start(
                                                 cin, d_ctxs.rearrange("(c p) t -> p c t", p=128)[:, :, (b + 2) * TB:(b + 3) * TB]),
                                             waits=[("pe", v_outproj[(b + 1, 7)])], inc=("dCR", 16))
                        v_xin[b + 2] = P.add("sy",
                                             lambda e, b=b: e.dma_start(
                                                 xin[(b + 2) % 2],
                                                 d_x.rearrange("(c p) t -> p c t", p=128)[:, :, (b + 2) * TB:(b + 3) * TB]),
                                             waits=[("dv", v_res1[(b + 1, 7)])], inc=("dXR", 16))
                else:
                    emit_ffn_steps(b, 13, 21)
                emit_ffn_steps(b, 21, 24)
                if b + 1 < NB:
                    emit_sums1(b + 1)
                emit_ffn_steps(b, 24, 27)
                if b + 1 < NB:
                    emit_bcast1(b + 1)
                emit_ffn_steps(b, 27, NJ + 2)
                if b + 1 < NB:
                    emit_apply1(b + 1)
                emit_res2part1(b)
                emit_res2part2(b)

            emit_sums2(NB - 1)
            emit_bcast2(NB - 1)
            emit_apply2(NB - 1)

        P.add("sy", lambda e: None, waits=[("dO", P.val("dO"))])

        def emit(eng_key):
            def run(eng):
                for waits, fn, inc in P.items[eng_key]:
                    for s, v in waits:
                        if v is None:
                            continue
                        eng.wait_ge(SEM[s], v)
                    ins = fn(eng)
                    if inc is not None:
                        assert ins is not None
                        ins.then_inc(SEM[inc[0]], inc[1])
            return run

        @block.sync
        def _(eng):
            emit("sy")(eng)

        @block.gpsimd
        def _(eng):
            emit("gp")(eng)

        @block.tensor
        def _(eng):
            emit("pe")(eng)

        @block.vector
        def _(eng):
            emit("dv")(eng)

        @block.scalar
        def _(eng):
            emit("sc")(eng)

    return nc


def _prep_shared(inp):
    bf = ml_dtypes.bfloat16
    wq = np.ascontiguousarray(inp["L0_attn_Wq"], dtype=np.float32)
    q2l = np.ascontiguousarray(
        inp["L0_attn_q_to_latent"].reshape(8, 2, 64, 64).transpose(1, 2, 0, 3).reshape(128, 512),
        dtype=np.float32)
    wkvT = np.ascontiguousarray(
        np.concatenate([inp["L0_attn_Wk_down"], inp["L0_attn_Wv_down"]], axis=0).T, dtype=np.float32)
    woutT = np.ascontiguousarray(inp["L0_attn_Wout"].T, dtype=np.float32)
    vupT = np.ascontiguousarray(
        inp["L0_attn_v_up"].reshape(8, 2, 64, 64).transpose(1, 3, 0, 2).reshape(128, 512),
        dtype=np.float32)
    w1 = np.ascontiguousarray(inp["L0_ffn_w1"], dtype=np.float32)
    w2 = np.ascontiguousarray(inp["L0_ffn_w2"], dtype=np.float32)
    s = np.arange(128)
    shared = dict(
        wq=wq,
        q2l=q2l,
        wkvT=wkvT,
        woutT=woutT,
        vupT=vupT,
        w1T=np.ascontiguousarray(w1.T),
        w2T=np.ascontiguousarray(w2.T),
        w1n=w1,
        w2n=w2,
        b1c=np.ascontiguousarray(inp["L0_ffn_b1"].reshape(32, 128).T, dtype=np.float32),
        b2c=np.ascontiguousarray(inp["L0_ffn_b2"].reshape(8, 128).T, dtype=np.float32),
        mask01=np.ascontiguousarray((s[None, :] >= s[:, None]).astype(bf)),
        ident=np.eye(128, dtype=bf),
        onesb=np.ones((128, 1), dtype=bf),
        onesf=np.ones((1, 128), dtype=np.float32),
    )
    return shared


def kernel(**inputs):
    x = np.asarray(inputs["L0_attn_hidden_states"], dtype=np.float32)
    shared = _prep_shared(inputs)
    nc = build(debug=False)
    in_maps = []
    for i in range(NCORES):
        m = dict(shared)
        m["x_fm"] = np.ascontiguousarray(x[i].T)
        in_maps.append(m)
    res = run_bass_kernel_spmd(nc, in_maps, core_ids=list(range(NCORES)))
    out = np.empty((NCORES, T, E), dtype=np.float32)
    for i in range(NCORES):
        out[i] = res.results[i]["y_fm"].T
    return out


if __name__ == "__main__":
    nc = build(debug=False)
    print("instructions:", nc.n_instructions())
